# version 35
# baseline (speedup 1.0000x reference)
import numpy as np
from contextlib import ExitStack

import concourse.bass as bass
import concourse.mybir as mybir
from concourse.bass_utils import run_bass_kernel_spmd

F32 = mybir.dt.float32
F32R = mybir.dt.float32r
BF16 = mybir.dt.bfloat16
I32 = mybir.dt.int32
AF = mybir.ActivationFunctionType
ALU = mybir.AluOpType
AX = mybir.AxisListType

D = 1024
NTOK = 4096
NH = 16
DH = 64
ST = 512
NST = NTOK // ST
TPS = ST // 128
NB = D // 32

ENGS = ("sync", "tensor", "vector", "scalar", "gpsimd")


class Ch:
    __slots__ = ("name", "n", "sem")

    def __init__(self, name):
        self.name = name
        self.n = 0
        self.sem = None


class Buf:
    __slots__ = ("w", "readers")

    def __init__(self):
        self.w = None
        self.readers = []

    def rd_waits(self):
        return [self.w] if self.w else []

    def did_rd(self, ch):
        self.readers.append((ch, ch.n))

    def wr_waits(self):
        ws = list(self.readers)
        if self.w:
            ws.append(self.w)
        return ws

    def did_wr(self, ch):
        self.w = (ch, ch.n)
        self.readers = []


class Sched:
    def __init__(self):
        self.ops = {e: [] for e in ENGS}
        self.chans = []

    def ch(self, name):
        c = Ch(name)
        self.chans.append(c)
        return c

    def add(self, eng, emit, waits=(), inc=None, amt=1, drain_inc=False):
        m = {}
        for c, v in waits:
            if v > 0 and (c not in m or v > m[c]):
                m[c] = v
        self.ops[eng].append((sorted(m.items(), key=lambda kv: kv[0].name), emit, inc,
                              amt, drain_inc))
        if inc is not None:
            inc.n += amt
        return (inc, inc.n if inc else 0)


def build_nc(dbg=False, phases=(1, 2)):
    nc = bass.Bass("TRN2", target_bir_lowering=False)

    x_d = nc.dram_tensor("x", [NTOK, D], F32, kind="ExternalInput").ap()
    wqkv_d = nc.dram_tensor("wqkv", [3 * D, D], F32, kind="ExternalInput").ap()
    wp_d = nc.dram_tensor("wproj", [D, D], F32, kind="ExternalInput").ap()
    bp_d = nc.dram_tensor("bproj", [D], F32, kind="ExternalInput").ap()
    wf_d = nc.dram_tensor("wffn", [D, D], F32, kind="ExternalInput").ap()
    bfn_d = nc.dram_tensor("bffn", [D], F32, kind="ExternalInput").ap()
    out_d = nc.dram_tensor("out", [NTOK, D], F32, kind="ExternalOutput").ap()
    lnt_d = nc.dram_tensor("lnt_scratch", [NST, 128, TPS * D], BF16, kind="Internal").ap()

    dbg_o = {}
    if dbg:
        dbg_o["kvx_o"] = nc.dram_tensor("kvx_o", [128, 512], BF16, kind="ExternalOutput").ap()
        dbg_o["ksr_o"] = nc.dram_tensor("ksr_o", [1, 1024], BF16, kind="ExternalOutput").ap()
        dbg_o["kall_o"] = nc.dram_tensor("kall_o", [128, 128], BF16, kind="ExternalOutput").ap()
        dbg_o["lnt0_o"] = nc.dram_tensor("lnt0_o", [128, TPS * D], BF16, kind="ExternalOutput").ap()
        dbg_o["qfm0_o"] = nc.dram_tensor("qfm0_o", [128, 8 * 512], BF16, kind="ExternalOutput").ap()
        dbg_o["z0_o"] = nc.dram_tensor("z0_o", [16, 512], BF16, kind="ExternalOutput").ap()
        dbg_o["att0_o"] = nc.dram_tensor("att0_o", [128, 8 * 512], BF16, kind="ExternalOutput").ap()
        dbg_o["xmid0_o"] = nc.dram_tensor("xmid0_o", [128, D], F32, kind="ExternalOutput").ap()
        dbg_o["xsb0_o"] = nc.dram_tensor("xsb0_o", [128, TPS * D], F32, kind="ExternalOutput").ap()
        dbg_o["lntm0_o"] = nc.dram_tensor("lntm0_o", [128, D], BF16, kind="ExternalOutput").ap()
        dbg_o["st0_o"] = nc.dram_tensor("st0_o", [128, 8], F32, kind="ExternalOutput").ap()
        dbg_o["wdt_o"] = nc.dram_tensor("wdt_o", [128, 8 * 1024], BF16, kind="ExternalOutput").ap()
        dbg_o["xd0_o"] = nc.dram_tensor("xd0_o", [128, D], BF16, kind="ExternalOutput").ap()
        dbg_o["wst_o"] = nc.dram_tensor("wst_o", [128, D], F32, kind="ExternalOutput").ap()
        dbg_o["wyq_o"] = nc.dram_tensor("wyq_o", [128, D], F32, kind="ExternalOutput").ap()
        dbg_o["wcq_o"] = nc.dram_tensor("wcq_o", [128, D], F32, kind="ExternalOutput").ap()
        dbg_o["wscl_o"] = nc.dram_tensor("wscl_o", [128, 32], F32, kind="ExternalOutput").ap()
        dbg_o["wbf_o"] = nc.dram_tensor("wbf_o", [128, D], BF16, kind="ExternalOutput").ap()
        dbg_o["wst2_o"] = nc.dram_tensor("wst2_o", [128, D], F32, kind="ExternalOutput").ap()
        dbg_o["wbf33_o"] = nc.dram_tensor("wbf33_o", [128, D], BF16, kind="ExternalOutput").ap()
        dbg_o["stg33_o"] = nc.dram_tensor("stg33_o", [128, D], F32, kind="ExternalOutput").ap()
        dbg_o["amx_o"] = nc.dram_tensor("amx_o", [128, 32], F32, kind="ExternalOutput").ap()
        dbg_o["yq0_o"] = nc.dram_tensor("yq0_o", [128, D], F32, kind="ExternalOutput").ap()
        dbg_o["scl2_o"] = nc.dram_tensor("scl2_o", [128, 32], F32, kind="ExternalOutput").ap()
        dbg_o["inv2_o"] = nc.dram_tensor("inv2_o", [128, 32], F32, kind="ExternalOutput").ap()
        dbg_o["cq2_o"] = nc.dram_tensor("cq2_o", [128, D], F32, kind="ExternalOutput").ap()
        dbg_o["wq2_o"] = nc.dram_tensor("wq2_o", [128, D], F32, kind="ExternalOutput").ap()
        dbg_o["qq2_o"] = nc.dram_tensor("qq2_o", [128, D], F32, kind="ExternalOutput").ap()
        dbg_o["dq2_o"] = nc.dram_tensor("dq2_o", [128, D], BF16, kind="ExternalOutput").ap()

    ctx = ExitStack()
    sbt = lambda name, shape, dt: ctx.enter_context(nc.sbuf_tensor(name, shape, dt))
    pst = lambda name, shape, dt: ctx.enter_context(nc.psum_tensor(name, shape, dt))

    wkvT = sbt("wkvT", [128, 8 * 2048], BF16)
    wqT = sbt("wqT", [128, 8 * 1024], BF16)
    wpT = sbt("wpT", [128, 8 * 1024], BF16)
    wdT = sbt("wdT", [128, 8 * 1024], BF16)
    x_st = sbt("x_st", [128, 2 * TPS * D], F32)
    lnT = sbt("lnT", [128, 2 * TPS * D], BF16)
    ln_tm = sbt("ln_tm", [128, 2 * D], BF16)
    ktm = sbt("ktm", [128, 2 * D], BF16)
    vtm = sbt("vtm", [128, 2 * D], BF16)
    sqs = sbt("sqs", [128, 2 * 512], F32)
    qfm = sbt("qfm", [128, 2 * 8 * 512], BF16)
    zsb = sbt("zsb", [16, 512], BF16)
    zbc = sbt("zbc", [128, 2 * 512], F32)
    attsb = sbt("attsb", [128, 8 * 512], BF16)
    y_q = sbt("y_q", [128, 2 * 512], F32)
    c_q = sbt("c_q", [128, 2 * 512], F32)
    w_q = sbt("w_q", [128, 2 * 512], F32)
    xd = sbt("xd", [128, D], BF16)
    xdT = sbt("xdT", [128, TPS * D], BF16)
    wstage = sbt("wstage", [128, 2 * 2 * D], BF16)
    amx = sbt("amx", [128, 2 * 16], F32)
    scl = sbt("scl", [128, 4 * 16], F32)
    inv = sbt("inv", [128, 2 * 16], F32)
    K_all = sbt("K_all", [128, 128], BF16)
    sel_all = sbt("sel_all", [16, 8 * 128], BF16)
    ident = sbt("ident", [128, 128], BF16)
    onesr = sbt("onesr", [1, 128], BF16)
    onesc = sbt("onesc", [128, 1], BF16)
    ones64r = sbt("ones64r", [1, 64], BF16)
    eps6 = sbt("eps6", [128, 1], F32)
    eps5 = sbt("eps5", [128, 1], F32)
    mv_s = sbt("mv_s", [128, 8 * 2], F32)
    ve_s = sbt("ve_s", [128, 8], F32)
    sqv_s = sbt("sqv_s", [128, 8], F32)
    rstd_s = sbt("rstd_s", [128, 8], F32)
    bn6 = sbt("bn6", [128, 12], F32)

    stg_ps = pst("stg_ps", [128, 1024], BF16)
    stg_ps2 = pst("stg_ps2", [128, 1024], BF16)
    mmA = pst("mmA", [128, 512], F32)
    mmB = pst("mmB", [128, 512], F32)
    attA = pst("attA", [128, 512], F32)
    attB = pst("attB", [128, 512], F32)
    zq = pst("zq", [128, 512], F32)

    nc.const_aps.register(F32, 1e-6, eps6[:])
    nc.const_aps.register(F32, 1e-5, eps5[:])

    S = Sched()
    cst = S.ch("cst")
    iot = S.ch("iot")
    seldma = S.ch("seldma")
    wdma = S.ch("wdma")
    wdmb = S.ch("wdmb")
    bdma = S.ch("bdma")
    wfdma = S.ch("wfdma")
    wq_c = S.ch("wq")
    wtr = S.ch("wtr")
    wev = S.ch("wev")
    x_cp = [S.ch("x0"), S.ch("x1")]
    bnv = S.ch("bnv")
    sqc = S.ch("sqc")
    ln_c = S.ch("ln")
    lntr = S.ch("lntr")
    lnev = S.ch("lnev")
    lnstp = [S.ch("lnsta"), S.ch("lnstb")]
    kvmm = S.ch("kvmm")
    ksq = S.ch("ksq")
    kexp = S.ch("kexp")
    vcp = S.ch("vcp")
    kvein = S.ch("kvein")
    bnd = S.ch("bnd")
    kall = S.ch("kall")
    lnt2p = [S.ch("lnt2a"), S.ch("lnt2b")]
    qmm = S.ch("qmm")
    qsq = S.ch("qsq")
    qexp = S.ch("qexp")
    qdot = S.ch("qdot")
    zact = S.ch("zact")
    zbcmm = S.ch("zbcmm")
    zbcev = S.ch("zbcev")
    qz_c = S.ch("qz")
    attmm = S.ch("attmm")
    attev = S.ch("attev")
    pjmm = S.ch("pjmm")
    rs1 = S.ch("rs1")
    bnv2 = S.ch("bnv2")
    sq2 = S.ch("sq2")
    qna = S.ch("qna")
    qnw = S.ch("qnw")
    qnr = S.ch("qnr")
    qnq = S.ch("qnq")
    xdtr = S.ch("xdtr")
    xdev = S.ch("xdev")
    ffmm = S.ch("ffmm")
    rs2 = S.ch("rs2")
    outp = [S.ch("outsta"), S.ch("outstb")]
    dbgc = S.ch("dbgc")
    qdbg = S.ch("qdbg")

    B = {}
    for k in ["xst0", "xst1", "lnT0", "lnT1", "lntm0", "lntm1", "ktm0", "ktm1",
              "vtm0", "vtm1", "qfm0", "qfm1", "zsb", "zbc0", "zbc1", "attsb", "bkzb2", "stg2",
              "y0", "y1", "c0", "c1", "w0", "w1", "ln2h0", "ln2h1", "xd0", "xd1",
              "xdT0", "xdT1", "wst0", "wst1", "stg", "bkA", "bkB", "bkattA",
              "bkattB", "bkzq", "kvx", "ksr", "K_all", "sel", "wfst",
              "mv0", "mv1", "mv2", "mv3", "sqs0", "sqs1"]:
        B[k] = Buf()

    def dve_memset(ap, val):
        return lambda v: v.memset(ap, val)

    S.add("vector", dve_memset(eps6[:], 1e-6), inc=cst)
    S.add("vector", dve_memset(eps5[:], 1e-5), inc=cst)
    S.add("vector", dve_memset(K_all[:], 0.0), inc=cst)
    S.add("vector", dve_memset(sel_all[:].bitcast(mybir.dt.uint16), 0), inc=cst)
    S.add("vector", dve_memset(onesr[:].bitcast(mybir.dt.uint16), 0x3F80), inc=cst)
    S.add("vector", dve_memset(onesc[:].bitcast(mybir.dt.uint16), 0x3F80), inc=cst)
    S.add("vector", dve_memset(ones64r[:].bitcast(mybir.dt.uint16), 0x3F80), inc=cst)
    S.add("gpsimd", lambda g: g.iota(ident[:].bitcast(mybir.dt.int16), [[-1, 128]],
                                     base=0, channel_multiplier=1,
                                     allow_small_or_imprecise_dtypes=True), inc=iot)
    S.add("vector", lambda v: v.tensor_scalar(ident[:].bitcast(mybir.dt.int16),
                                              ident[:].bitcast(mybir.dt.int16),
                                              0, 0x3F80, ALU.is_equal, ALU.mult),
          waits=[(iot, 1)], inc=cst)
    N_CST = cst.n
    for c in range(8):
        for j in range(2):
            h = 2 * c + j

            def mk_sel(c=c, j=j, h=h):
                return lambda g: g.dma_start(
                    out=sel_all[h:h + 1, 128 * c + 64 * j:128 * c + 64 * j + 64],
                    in_=ones64r[0:1, :])
            S.add("gpsimd", mk_sel(), waits=[(cst, N_CST)], inc=seldma, amt=16)
    biasp = ln_tm[0:1, 0:D]
    biasf = ln_tm[0:1, D:2 * D]
    kvb = wstage

    def w_src(wi):
        if wi < 16:
            return wqkv_d[D + 128 * wi:D + 128 * (wi + 1), :]
        if wi < 24:
            return wqkv_d[128 * (wi - 16):128 * (wi - 15), :]
        if wi < 32:
            return wp_d[128 * (wi - 24):128 * (wi - 23), :]
        return wf_d[128 * (wi - 32):128 * (wi - 31), :]

    def w_dst(wi):
        if wi < 16:
            return wkvT, 2048, 128 * wi
        if wi < 24:
            return wqT, 1024, 128 * (wi - 16)
        if wi < 32:
            return wpT, 1024, 128 * (wi - 24)
        return wdT, 1024, 128 * (wi - 32)

    def emit_quant(eng_sched, src_ap, dst_ap, n_blk, am, sc, iv, yb, cb, wb,
                   waits0, inc_final, dbg_steps=False):
        F = 32 * n_blk
        ws = list(waits0)
        S.add(eng_sched, lambda v: v.tensor_reduce(
            am, src_ap.rearrange("p (b k) -> p b k", k=32), AX.X, ALU.max,
            apply_absolute_value=True), waits=ws,
            inc=qdbg if dbg_steps else None)
        S.add(eng_sched, lambda v: v.drain())
        if dbg_steps:
            S.add("sync", lambda s: s.dma_start(out=dbg_o["amx_o"], in_=am),
                  waits=[(qdbg, qdbg.n)], inc=dbgc, amt=16)
        S.add(eng_sched, lambda v: v.tensor_scalar(sc, am, 1.0 / 7.5, 1e-30, ALU.mult, ALU.max),
              waits=[(dbgc, dbgc.n)] if dbg_steps else [],
              inc=qdbg if dbg_steps else None)
        if dbg_steps:
            S.add("sync", lambda s: s.dma_start(out=dbg_o["scl2_o"], in_=sc),
                  waits=[(qdbg, qdbg.n)], inc=dbgc, amt=16)
        S.add(eng_sched, lambda v: v.drain())
        S.add(eng_sched, lambda v: v.reciprocal(iv, sc),
              waits=[(dbgc, dbgc.n)] if dbg_steps else [],
              inc=qdbg if dbg_steps else None)
        S.add(eng_sched, lambda v: v.drain())
        if dbg_steps:
            S.add("sync", lambda s: s.dma_start(out=dbg_o["inv2_o"], in_=iv),
                  waits=[(qdbg, qdbg.n)], inc=dbgc, amt=16)
            S.add(eng_sched, lambda v: v.drain(), waits=[(dbgc, dbgc.n)])
        S.add(eng_sched, lambda v: v.tensor_tensor(
            yb, src_ap.rearrange("p (b k) -> p b k", k=32),
            iv.broadcast_to([128, n_blk, 32]), ALU.mult),
            inc=qdbg if dbg_steps else None)
        if dbg_steps:
            S.add("sync", lambda s: s.dma_start(out=dbg_o["yq0_o"], in_=yb),
                  waits=[(qdbg, qdbg.n)], inc=dbgc, amt=16)
        S.add(eng_sched, lambda v: v.drain())
        S.add(eng_sched, lambda v: v.tensor_scalar(
            cb.bitcast(I32), yb.bitcast(I32), 0x7F800000, None, ALU.bitwise_and))
        S.add(eng_sched, lambda v: v.drain())
        S.add(eng_sched, lambda v: v.tensor_scalar(cb, cb, 1.0, 24.0, ALU.max, ALU.mult),
              inc=qdbg if dbg_steps else None)
        if dbg_steps:
            S.add("sync", lambda s: s.dma_start(out=dbg_o["cq2_o"], in_=cb),
                  waits=[(qdbg, qdbg.n)], inc=dbgc, amt=16)
        S.add(eng_sched, lambda v: v.drain())
        S.add(eng_sched, lambda v: v.tensor_tensor(wb, yb, cb, ALU.add),
              waits=[(dbgc, dbgc.n)] if dbg_steps else [])
        S.add(eng_sched, lambda v: v.drain())
        S.add(eng_sched, lambda v: v.scalar_tensor_tensor(wb, cb, 1.0 / 384.0, wb, ALU.mult, ALU.add))
        S.add(eng_sched, lambda v: v.drain())
        S.add(eng_sched, lambda v: v.tensor_scalar(
            wb.bitcast(I32), wb.bitcast(I32), -65536, None, ALU.bitwise_and),
            inc=qdbg if dbg_steps else None)
        if dbg_steps:
            S.add("sync", lambda s: s.dma_start(out=dbg_o["wq2_o"], in_=wb),
                  waits=[(qdbg, qdbg.n)], inc=dbgc, amt=16)
        S.add(eng_sched, lambda v: v.drain())
        S.add(eng_sched, lambda v: v.tensor_tensor(yb, wb, cb, ALU.subtract),
              waits=[(dbgc, dbgc.n)] if dbg_steps else [],
              inc=qdbg if dbg_steps else None)
        S.add(eng_sched, lambda v: v.drain())
        if dbg_steps:
            S.add("sync", lambda s: s.dma_start(out=dbg_o["qq2_o"], in_=yb),
                  waits=[(qdbg, qdbg.n)], inc=dbgc, amt=16)
        S.add(eng_sched, lambda v: v.tensor_tensor(
            dst_ap.rearrange("p (b k) -> p b k", k=32),
            yb.rearrange("p (b k) -> p b k", k=32),
            sc.broadcast_to([128, n_blk, 32]), ALU.mult),
            waits=[(dbgc, dbgc.n)] if dbg_steps else [], inc=inc_final)
        if dbg_steps:
            S.add("sync", lambda s: s.dma_start(out=dbg_o["dq2_o"], in_=dst_ap),
                  waits=[(inc_final, inc_final.n)], inc=dbgc, amt=16)

    def w_src_pair(wi):
        if wi < 16:
            return wqkv_d[D + 128 * wi:D + 128 * (wi + 2), :]
        if wi < 24:
            return wqkv_d[128 * (wi - 16):128 * (wi - 14), :]
        return wp_d[128 * (wi - 24):128 * (wi - 22), :]

    def emit_wtr_evac(wi, stage_ap, wst_key, ready):
        sbank = stg_ps if wi % 2 == 0 else stg_ps2
        skey = "stg" if wi % 2 == 0 else "stg2"
        tws = ready + [(cst, N_CST)] + B[skey].wr_waits()

        def mk_tr(stage_ap=stage_ap, sbank=sbank):
            def f(t):
                r = None
                for c in range(8):
                    r = t.transpose(sbank[:, 128 * c:128 * (c + 1)],
                                    stage_ap[:, 128 * c:128 * (c + 1)], ident[:])
                return r
            return f
        S.add("tensor", mk_tr(), waits=tws, inc=wtr, drain_inc=True)
        B[wst_key].did_rd(wtr)
        B[skey].did_wr(wtr)
        dstt, ofw, coff = w_dst(wi)
        dst_ap = dstt[:].rearrange("p (c o) -> p c o", c=8)[:, :, coff:coff + 128]
        src_ap2 = sbank[:].rearrange("p (c i) -> p c i", c=8)

        def mk_ev(dst_ap=dst_ap, src_ap2=src_ap2):
            return lambda s: s.activation(dst_ap, src_ap2, AF.Copy, bias=0.0)
        S.add("scalar", mk_ev(), waits=[(wtr, wtr.n)], inc=wev)
        B[skey].did_rd(wev)

    def emit_wpair(wi):
        pr = (wi // 2) % 2
        wst_key = "wst%d" % pr
        pair_ap = wstage[:, pr * 2 * D:(pr + 1) * 2 * D]
        ws = B[wst_key].wr_waits()
        if wi < 4:
            ws = ws + [(cst, N_CST)]

        def mk_wdma(wi=wi, pair_ap=pair_ap):
            return lambda g: g.dma_start(
                out=pair_ap.rearrange("p (c i) -> p c i", c=2),
                in_=w_src_pair(wi).rearrange("(c p) i -> p c i", p=128))
        wch = wdma if pr == 0 else wdmb
        S.add("gpsimd", mk_wdma(), waits=ws, inc=wch, amt=16)
        B[wst_key].did_wr(wch)
        ready = [(wch, wch.n)]
        for sub in range(2):
            emit_wtr_evac(wi + sub, pair_ap[:, sub * D:(sub + 1) * D], wst_key, ready)

    def emit_ffn_chunk(wi):
        pr = (wi // 2) % 2
        wst_key = "wst%d" % pr
        stage_ap = wstage[:, pr * 2 * D + (wi % 2) * D:pr * 2 * D + (wi % 2 + 1) * D]
        ws = (B["wfst"].wr_waits() + B["sqs0"].wr_waits() + B["sqs1"].wr_waits())

        def mk_wfdma(wi=wi):
            return lambda s: s.dma_start(out=sqs[:], in_=w_src(wi))
        S.add("sync", mk_wfdma(), waits=ws, inc=wfdma, amt=16)
        B["wfst"].did_wr(wfdma)
        B["sqs0"].did_wr(wfdma)
        B["sqs1"].did_wr(wfdma)
        qws = [(wfdma, wfdma.n), (cst, N_CST)] + B[wst_key].wr_waits()
        emit_quant("vector", sqs[:], stage_ap, 32, amx[:], scl[:, 0:32],
                   inv[:], y_q[:], c_q[:], w_q[:], qws, wq_c)
        B["wfst"].did_rd(wq_c)
        B["sqs0"].did_rd(wq_c)
        B["sqs1"].did_rd(wq_c)
        B[wst_key].did_wr(wq_c)
        emit_wtr_evac(wi, stage_ap, wst_key, [(wq_c, wq_c.n)])

    S.add("sync", lambda s: s.dma_start(
        out=x_st[:, 0:TPS * D].rearrange("p (t d) -> p t d", t=TPS),
        in_=x_d[0:ST, :].rearrange("(t p) d -> p t d", p=128)),
        waits=[], inc=x_cp[0], amt=16)
    B["xst0"].did_wr(x_cp[0])
    XC0 = (x_cp[0], x_cp[0].n)

    for k in range(16):
        emit_wpair(2 * k)
        if k % 2 == 1:
            emit_ffn_chunk(32 + k // 2)

    S.add("gpsimd", lambda g: g.dma_start(out=biasp, in_=bp_d.unsqueeze(0)),
          waits=[(lntr, NST * TPS)], inc=bdma, amt=16)
    S.add("gpsimd", lambda g: g.dma_start(out=biasf, in_=bfn_d.unsqueeze(0)),
          waits=[(lntr, NST * TPS)], inc=bdma, amt=16)
    BIAS_WDMA = bdma.n
    WEV_KV = 16
    WEV_Q = 24
    WEV_P = 32
    WEV_ALL = 40

    if dbg:
        S.add("sync", lambda s: s.dma_start(out=dbg_o["wdt_o"], in_=wdT[:]),
              waits=[(wev, WEV_ALL)], inc=dbgc, amt=16)
    for st in range(NST):
        p = st % 2
        xk = "xst%d" % p
        lk = "lnT%d" % p
        x_sl = x_st[:, p * TPS * D:(p + 1) * TPS * D]
        lnT_sl = lnT[:, p * TPS * D:(p + 1) * TPS * D]
        if st == 0:
            XC = XC0
        else:
            xw = B[xk].wr_waits()

            def mk_xl(st=st, x_sl=x_sl):
                return lambda s: s.dma_start(
                    out=x_sl.rearrange("p (t d) -> p t d", t=TPS),
                    in_=x_d[st * ST:(st + 1) * ST, :].rearrange("(t p) d -> p t d", p=128))
            xch = x_cp[st % 2]
            S.add("sync", mk_xl(), waits=xw, inc=xch, amt=16)
            B[xk].did_wr(xch)
            XC = (xch, xch.n)
        for t in range(TPS):
            xt = x_sl[:, t * D:(t + 1) * D]

            def mk_bn(xt=xt):
                return lambda v: v.bn_stats(bn6[:, 0:6], xt[:, 0:512])
            S.add("vector", mk_bn(), waits=[XC])

            def mk_bnb(xt=xt):
                return lambda v: v.bn_stats(bn6[:, 6:12], xt[:, 512:1024])
            S.add("vector", mk_bnb())
            S.add("vector", lambda v: v.drain())

            def mk_ag(t=t):
                return lambda v: v.bn_aggr(mv_s[:, 2 * t:2 * t + 2], bn6[:])
            S.add("vector", mk_ag(), inc=bnv)

            def mk_ln(t=t):
                return lambda s: s.activation(sqv_s[:, t:t + 1],
                                              mv_s[:, 2 * t + 1:2 * t + 2],
                                              AF.Ln, bias=1e-5)
            S.add("scalar", mk_ln(), waits=[(bnv, bnv.n)])
            S.add("scalar", lambda s: s.drain())

            def mk_sq(t=t):
                return lambda s: s.activation(rstd_s[:, t:t + 1], sqv_s[:, t:t + 1],
                                              AF.Exp, scale=-0.5)
            S.add("scalar", mk_sq(), inc=sqc)
        for t in range(TPS):
            tp = t % 2
            lntmk = "lntm%d" % tp
            xt = x_sl[:, t * D:(t + 1) * D]
            ln_ap = ln_tm[:, tp * D:(tp + 1) * D]

            nws = [(sqc, 4 * st + t + 1)] + B[lntmk].wr_waits()

            def mk_nm(t=t, xt=xt, ln_ap=ln_ap):
                return lambda v: v.tensor_scalar(
                    ln_ap, xt, mv_s[:, 2 * t:2 * t + 1], rstd_s[:, t:t + 1],
                    ALU.subtract, ALU.mult)
            S.add("vector", mk_nm(), waits=nws, inc=ln_c)
            B[lntmk].did_wr(ln_c)
            if t == TPS - 1:
                B[xk].did_rd(ln_c)
            if dbg and st == 0 and t == 0:
                S.add("sync", (lambda x_sl=x_sl: lambda s: s.dma_start(
                    out=dbg_o["xsb0_o"], in_=x_sl))(),
                    waits=[XC], inc=dbgc, amt=16)
                B[xk].did_rd(dbgc)
                S.add("sync", (lambda ln_ap=ln_ap: lambda s: s.dma_start(
                    out=dbg_o["lntm0_o"], in_=ln_tm[:, 0:D]))(),
                    waits=[(ln_c, ln_c.n)], inc=dbgc, amt=16)
                B[lntmk].did_rd(dbgc)
                S.add("sync", lambda s: s.dma_start(out=dbg_o["st0_o"], in_=mv_s[:, 0:8]),
                      waits=[(ln_c, ln_c.n)], inc=dbgc, amt=16)
            sbank = stg_ps if t % 2 == 0 else stg_ps2
            skey = "stg" if t % 2 == 0 else "stg2"
            tws = [(ln_c, ln_c.n), (wev, WEV_ALL)] + B[skey].wr_waits()

            def mk_lt(ln_ap=ln_ap, sbank=sbank):
                def f(te):
                    r = None
                    for c in range(8):
                        r = te.transpose(sbank[:, 128 * c:128 * (c + 1)],
                                         ln_ap[:, 128 * c:128 * (c + 1)], ident[:])
                    return r
                return f
            S.add("tensor", mk_lt(), waits=tws, inc=lntr, drain_inc=True)
            B[lntmk].did_rd(lntr)
            B[skey].did_wr(lntr)
            ews = [(lntr, lntr.n)] + B[lk].wr_waits() if t == 0 else [(lntr, lntr.n)]
            dst = lnT_sl.rearrange("p (c i) -> p c i", c=8)[:, :, t * 128:(t + 1) * 128]
            src = sbank[:].rearrange("p (c i) -> p c i", c=8)

            def mk_le(dst=dst, src=src):
                return lambda s: s.activation(dst, src, AF.Copy, bias=0.0)
            S.add("scalar", mk_le(), waits=ews, inc=lnev)
            B[skey].did_rd(lnev)
            if t == 0:
                B[lk].did_wr(lnev)
        sws = [(lnev, lnev.n)]

        def mk_ls(st=st, lnT_sl=lnT_sl):
            return lambda s: s.dma_start(out=lnt_d[st], in_=lnT_sl)
        sch = lnstp[st % 2]
        S.add("sync", mk_ls(), waits=sws, inc=sch, amt=16)
        B[lk].did_rd(sch)
        if dbg and st == 0:
            S.add("sync", lambda s: s.dma_start(out=dbg_o["lnt0_o"], in_=lnt_d[0]),
                  waits=[(lnstp[0], lnstp[0].n)], inc=dbgc, amt=16)
        for t in range(TPS):
            tp = t % 2
            ktm_ap = ktm[:, tp * D:(tp + 1) * D]
            vtm_ap = vtm[:, tp * D:(tp + 1) * D]
            for ob in (0, 2, 1, 3):
                bank = mmA if ob < 2 else mmB
                bkk = "bkA" if ob < 2 else "bkB"
                gws = [(lnev, 4 * st + t + 1), (wev, WEV_KV)] + B[bkk].wr_waits()

                def mk_g(t=t, ob=ob, bank=bank, lnT_sl=lnT_sl):
                    def f(te):
                        r = None
                        for c in range(8):
                            r = te.matmul(
                                bank[:],
                                lnT_sl[:, c * 512 + t * 128:c * 512 + (t + 1) * 128],
                                wkvT[:, c * 2048 + ob * 512:c * 2048 + (ob + 1) * 512],
                                start=(c == 0), stop=(c == 7))
                        return r
                    return f
                S.add("tensor", mk_g(), waits=gws, inc=kvmm)
                B[bkk].did_wr(kvmm)
                if t == TPS - 1 and ob == 3:
                    B[lk].did_rd(kvmm)
                if ob < 2:
                    sqp = (2 * t + ob) % 2
                    sws2 = [(kvmm, kvmm.n)] + B["sqs%d" % sqp].wr_waits()

                    def mk_s1(bank=bank, sqp=sqp):
                        return lambda s: s.activation(
                            sqs[:, sqp * 512:(sqp + 1) * 512], bank[:], AF.Square)
                    S.add("scalar", mk_s1(), waits=sws2, inc=ksq)
                    S.add("scalar", lambda s: s.drain())
                    B[bkk].did_rd(ksq)
                    B["sqs%d" % sqp].did_wr(ksq)
                    ews2 = B["ktm%d" % tp].wr_waits() if ob == 0 else []

                    def mk_s2(ktm_ap=ktm_ap, ob=ob, sqp=sqp):
                        return lambda s: s.activation(
                            ktm_ap[:, ob * 512:(ob + 1) * 512],
                            sqs[:, sqp * 512:(sqp + 1) * 512], AF.Exp, scale=-0.5)
                    S.add("scalar", mk_s2(), waits=ews2, inc=kexp)
                    B["sqs%d" % sqp].did_rd(kexp)
                    if ob == 0:
                        B["ktm%d" % tp].did_wr(kexp)
                else:
                    vws = [(kvmm, kvmm.n)] + (B["vtm%d" % tp].wr_waits() if ob == 2 else [])

                    def mk_vc(vtm_ap=vtm_ap, ob=ob, bank=bank):
                        return lambda v: v.tensor_copy(
                            vtm_ap[:, (ob - 2) * 512:(ob - 1) * 512], bank[:])
                    S.add("vector", mk_vc(), waits=vws, inc=vcp)
                    B[bkk].did_rd(vcp)
                    if ob == 2:
                        B["vtm%d" % tp].did_wr(vcp)
            kws = [(kexp, kexp.n), (vcp, vcp.n), (wev, WEV_KV)]
            first = (st == 0 and t == 0)
            last = (st == NST - 1 and t == TPS - 1)

            def mk_kv(ktm_ap=ktm_ap, vtm_ap=vtm_ap, first=first, last=last):
                def f(te):
                    for h in range(NH):
                        te.matmul(
                            attA[64 * (h % 2):64 * (h % 2) + 64,
                                 64 * (h // 2):64 * (h // 2) + 64],
                            ktm_ap[:, 64 * h:64 * (h + 1)],
                            vtm_ap[:, 64 * h:64 * (h + 1)],
                            start=(first and h < 2), stop=(last and h >= NH - 2),
                            skip_group_check=True)
                    r = te.matmul(zq[0:1, :], onesc[:], ktm_ap[:, 0:512],
                                  start=first, stop=last, skip_group_check=True)
                    r = te.matmul(attB[0:1, :], onesc[:], ktm_ap[:, 512:1024],
                                  start=first, stop=last, skip_group_check=True)
                    return r
                return f
            S.add("tensor", mk_kv(), waits=kws, inc=kvein)
            B["ktm%d" % tp].did_rd(kvein)
            B["vtm%d" % tp].did_rd(kvein)

    bws = [(kvein, kvein.n), (wtr, 40)]
    S.add("vector", dve_memset(kvb[:, 0:1024], 0.0), waits=bws)
    S.add("vector", lambda v: v.drain())
    S.add("vector", lambda v: v.tensor_copy(
        kvb[0:64, 0:1024].rearrange("p (c s i) -> p c s i", s=2, i=64)[:, :, 0, :],
        attA[0:64, :].rearrange("p (c i) -> p c i", i=64)))
    S.add("vector", lambda v: v.drain())
    S.add("vector", lambda v: v.tensor_copy(
        kvb[64:128, 0:1024].rearrange("p (c s i) -> p c s i", s=2, i=64)[:, :, 1, :],
        attA[64:128, :].rearrange("p (c i) -> p c i", i=64)))
    S.add("vector", lambda v: v.drain())
    S.add("vector", lambda v: v.tensor_copy(attsb[0:1, 0:512], zq[0:1, :]))
    S.add("vector", lambda v: v.tensor_copy(attsb[0:1, 512:1024], attB[0:1, :]), inc=bnd)
    BND = bnd.n
    for c in range(8):
        for j in range(2):
            h = 2 * c + j

            def mk_ka(c=c, j=j, h=h):
                return lambda g: g.dma_start(
                    out=K_all[64 * j:64 * j + 64, 16 * c + h:16 * c + h + 1],
                    in_=attsb[0:1, h * 64:(h + 1) * 64])
            S.add("gpsimd", mk_ka(), waits=[(bnd, BND)], inc=kall, amt=16)
    KALL = kall.n
    if dbg:
        S.add("sync", lambda s: s.dma_start(out=dbg_o["kvx_o"], in_=kvb[:, 0:512]),
              waits=[(bnd, BND)], inc=dbgc, amt=16)
        S.add("sync", lambda s: s.dma_start(out=dbg_o["ksr_o"], in_=attsb[0:1, 0:1024]),
              waits=[(bnd, BND)], inc=dbgc, amt=16)
        S.add("sync", lambda s: s.dma_start(out=dbg_o["kall_o"], in_=K_all[:]),
              waits=[(kall, KALL)], inc=dbgc, amt=16)

    zb2 = pst("zb2", [128, 512], F32)

    def xd_ap(tb):
        return [xd[:, 0:D], ktm[:, 0:D], ktm[:, D:2 * D], vtm[:, 0:D]][tb]
    XDK = ["xd0", "ktm0", "ktm1", "vtm0"]

    p2ctx = {}

    def p2_front(st):
        p = st % 2
        xk = "xst%d" % p
        lk = "lnT%d" % p
        x_sl = x_st[:, p * TPS * D:(p + 1) * TPS * D]
        lnT_sl = lnT[:, p * TPS * D:(p + 1) * TPS * D]
        qf_sl = qfm[:, p * 8 * 512:(p + 1) * 8 * 512]
        lws = B[lk].wr_waits()

        def mk_ll(st=st, lnT_sl=lnT_sl):
            return lambda s: s.dma_start(out=lnT_sl, in_=lnt_d[st])
        lch = lnt2p[st % 2]
        S.add("sync", mk_ll(),
              waits=lws + [(lnstp[0], lnstp[0].n), (lnstp[1], lnstp[1].n)],
              inc=lch, amt=16)
        B[lk].did_wr(lch)
        LNT2 = (lch, lch.n)
        xw = B[xk].wr_waits()

        def mk_xl2(st=st, x_sl=x_sl):
            return lambda s: s.dma_start(
                out=x_sl.rearrange("p (t d) -> p t d", t=TPS),
                in_=x_d[st * ST:(st + 1) * ST, :].rearrange("(t p) d -> p t d", p=128))
        xch = x_cp[st % 2]
        S.add("sync", mk_xl2(), waits=xw, inc=xch, amt=16)
        B[xk].did_wr(xch)
        XC = (xch, xch.n)
        p2ctx[st] = {"p": p, "xk": xk, "x_sl": x_sl, "qf_sl": qf_sl, "XC": XC,
                     "qnq_at": {}, "sq2_at": {}}
        for oc in range(8):
            bank = mmA if oc % 2 == 0 else mmB
            bkk = "bkA" if oc % 2 == 0 else "bkB"
            gws = [LNT2, (wev, WEV_Q)] + B[bkk].wr_waits()
            qfw = B["qfm%d" % p].wr_waits() if oc == 0 else []

            def mk_q(oc=oc, bank=bank, lnT_sl=lnT_sl):
                def f(te):
                    r = None
                    for c in range(8):
                        r = te.matmul(
                            bank[:],
                            wqT[:, c * 1024 + oc * 128:c * 1024 + (oc + 1) * 128],
                            lnT_sl[:, c * 512:(c + 1) * 512],
                            start=(c == 0), stop=(c == 7))
                    return r
                return f
            S.add("tensor", mk_q(), waits=gws, inc=qmm)
            B[bkk].did_wr(qmm)
            if oc == 7:
                B[lk].did_rd(qmm)
            sqp = oc % 2
            sws2 = [(qmm, qmm.n)] + B["sqs%d" % sqp].wr_waits()

            def mk_qs(bank=bank, sqp=sqp):
                return lambda s: s.activation(
                    sqs[:, sqp * 512:(sqp + 1) * 512], bank[:], AF.Square)
            S.add("scalar", mk_qs(), waits=sws2, inc=qsq)
            S.add("scalar", lambda s: s.drain())
            B[bkk].did_rd(qsq)
            B["sqs%d" % sqp].did_wr(qsq)

            def mk_qe(oc=oc, sqp=sqp, qf_sl=qf_sl):
                return lambda s: s.activation(
                    qf_sl[:, oc * 512:(oc + 1) * 512],
                    sqs[:, sqp * 512:(sqp + 1) * 512], AF.Exp, scale=-0.5)
            S.add("scalar", mk_qe(), waits=qfw, inc=qexp)
            B["sqs%d" % sqp].did_rd(qexp)
            if oc == 0:
                B["qfm%d" % p].did_wr(qexp)
        qdw = [(qexp, qexp.n), (kall, KALL)] + B["bkzq"].wr_waits()

        def mk_qd(qf_sl=qf_sl):
            def f(te):
                r = None
                for c in range(8):
                    r = te.matmul(zq[0:16, :], K_all[:, 16 * c:16 * (c + 1)],
                                  qf_sl[:, c * 512:(c + 1) * 512],
                                  start=(c == 0), stop=(c == 7))
                return r
            return f
        S.add("tensor", mk_qd(), waits=qdw, inc=qdot)
        B["bkzq"].did_wr(qdot)
        B["qfm%d" % p].did_rd(qdot)
        zws = [(qdot, qdot.n)] + B["zsb"].wr_waits() + B["sqs0"].wr_waits()
        S.add("scalar", lambda s: s.activation(sqs[0:16, 0:512], zq[0:16, :],
                                               AF.Ln, bias=1e-6), waits=zws)
        S.add("scalar", lambda s: s.drain())
        S.add("scalar", lambda s: s.activation(zsb[:], sqs[0:16, 0:512],
                                               AF.Exp, scale=-1.0), inc=zact)
        B["bkzq"].did_rd(zact)
        B["sqs0"].did_rd(zact)
        B["zsb"].did_wr(zact)

    def p2_attproj(st):
        ctx = p2ctx[st]
        p = ctx["p"]; x_sl = ctx["x_sl"]; qf_sl = ctx["qf_sl"]; XC = ctx["XC"]
        def emit_att(c):
            bank = attA if c % 2 == 0 else attB
            bkk = "bkattA" if c % 2 == 0 else "bkattB"
            aws = [(qz_c, qz_c.n), (bnd, BND)] + B[bkk].wr_waits()

            def mk_at(c=c, bank=bank, qf_sl=qf_sl):
                return lambda te: te.matmul(
                    bank[:],
                    kvb[:, 128 * c:128 * (c + 1)],
                    qf_sl[:, c * 512:(c + 1) * 512],
                    start=True, stop=True)
            S.add("tensor", mk_at(), waits=aws, inc=attmm)
            B[bkk].did_wr(attmm)
            if c == 7:
                B["qfm%d" % p].did_rd(attmm)
                B["zsb"].did_rd(attmm)
            ew = [(attmm, attmm.n)] + (B["attsb"].wr_waits() if c == 0 else [])
            if st == 0 and c <= 1:
                ew = ew + [(kall, KALL)]

            def mk_ae(c=c, bank=bank):
                return lambda s: s.activation(
                    attsb[:, c * 512:(c + 1) * 512],
                    bank[:], AF.Copy, bias=0.0)
            S.add("scalar", mk_ae(), waits=ew, inc=attev)
            B[bkk].did_rd(attev)
            if c == 0:
                B["attsb"].did_wr(attev)

        for c in range(8):
            zbank = zb2 if c % 2 == 0 else zq
            zbkk = "bkzb2" if c % 2 == 0 else "bkzq"
            zw = [(zact, zact.n)] + B[zbkk].wr_waits()
            if st == 0 and c <= 1:
                zw = zw + [(seldma, seldma.n)]

            def mk_zb(c=c, zbank=zbank):
                return lambda te: te.matmul(zbank[:], sel_all[:, 128 * c:128 * (c + 1)],
                                            zsb[:], start=True, stop=True)
            S.add("tensor", mk_zb(), waits=zw, inc=zbcmm)
            B[zbkk].did_wr(zbcmm)
            qzw = [(zbcmm, zbcmm.n), (qexp, qexp.n)]

            def mk_qz(c=c, zbank=zbank, qf_sl=qf_sl):
                return lambda v: v.tensor_tensor(
                    qf_sl[:, c * 512:(c + 1) * 512], qf_sl[:, c * 512:(c + 1) * 512],
                    zbank[:], ALU.mult)
            S.add("vector", mk_qz(), waits=qzw, inc=qz_c)
            B[zbkk].did_rd(qz_c)
            if c >= 1:
                emit_att(c - 1)
        emit_att(7)
        for tb in range(TPS):
            for ocb in range(2):
                g = 2 * tb + ocb
                bank = mmA if g % 2 == 0 else mmB
                bkk = "bkA" if g % 2 == 0 else "bkB"
                gws = [(attev, attev.n), (wev, WEV_P), (bdma, BIAS_WDMA)] + B[bkk].wr_waits()

                def mk_pj(tb=tb, ocb=ocb, bank=bank):
                    def f(te):
                        for c in range(8):
                            te.matmul(
                                bank[:],
                                attsb[:, c * 512 + tb * 128:c * 512 + (tb + 1) * 128],
                                wpT[:, c * 1024 + ocb * 512:c * 1024 + (ocb + 1) * 512],
                                start=(c == 0), stop=False)
                        return te.matmul(bank[:], onesr[:],
                                         biasp[:, ocb * 512:(ocb + 1) * 512],
                                         start=False, stop=True)
                    return f
                S.add("tensor", mk_pj(), waits=gws, inc=pjmm)
                B[bkk].did_wr(pjmm)
                if tb == TPS - 1 and ocb == 1:
                    B["attsb"].did_rd(pjmm)
                rws = [(pjmm, pjmm.n), XC]

                def mk_r1(tb=tb, ocb=ocb, bank=bank, x_sl=x_sl):
                    sl = x_sl[:, tb * D + ocb * 512:tb * D + (ocb + 1) * 512]
                    return lambda v: v.tensor_tensor(sl, bank[:], sl, ALU.add)
                S.add("vector", mk_r1(), waits=rws, inc=rs1)
                B[bkk].did_rd(rs1)
            tp = tb % 2
            xt = x_sl[:, tb * D:(tb + 1) * D]

            def mk_bn2(xt=xt):
                return lambda v: v.bn_stats(bn6[:, 0:6], xt[:, 0:512])
            S.add("vector", mk_bn2())

            def mk_bn2b(xt=xt):
                return lambda v: v.bn_stats(bn6[:, 6:12], xt[:, 512:1024])
            S.add("vector", mk_bn2b())
            S.add("vector", lambda v: v.drain())

            tq_ = 4 * (st % 2) + tb

            def mk_ag2(tq_=tq_):
                return lambda v: v.bn_aggr(mv_s[:, 2 * tq_:2 * tq_ + 2], bn6[:])
            S.add("vector", mk_ag2(), inc=bnv2)

            def mk_ln2(tq_=tq_):
                return lambda s: s.activation(sqv_s[:, tq_:tq_ + 1],
                                              mv_s[:, 2 * tq_ + 1:2 * tq_ + 2],
                                              AF.Ln, bias=1e-5)
            S.add("scalar", mk_ln2(), waits=[(bnv2, bnv2.n)])
            S.add("scalar", lambda s: s.drain())

            def mk_sq2(tq_=tq_):
                return lambda s: s.activation(rstd_s[:, tq_:tq_ + 1], sqv_s[:, tq_:tq_ + 1],
                                              AF.Exp, scale=-0.5)
            S.add("scalar", mk_sq2(), inc=sq2)
            ctx["sq2_at"][tb] = sq2.n

    def p2_attproj_B(st):
        ctx = p2ctx[st]
        p = ctx["p"]; x_sl = ctx["x_sl"]; XC = ctx["XC"]
        for tb in range(TPS):
            xt = x_sl[:, tb * D:(tb + 1) * D]
            xdk = XDK[tb]
            xda_t = xd_ap(tb)
            l2 = zbc[:, 0:D]
            yb = y_q[:, 0:D]
            cb = c_q[:, 0:D]
            wb = w_q[:, 0:D]
            am = amx[:, 0:32]
            sc = scl[:, 32 * (tb % 2):32 * (tb % 2) + 32]
            iv = inv[:, 0:32]
            fws = B["ln2h0"].wr_waits()

            tq_ = 4 * (st % 2) + tb

            def mk_n2(xt=xt, l2=l2, tq_=tq_):
                return lambda v: v.tensor_scalar(
                    l2, xt, mv_s[:, 2 * tq_:2 * tq_ + 1], rstd_s[:, tq_:tq_ + 1],
                    ALU.subtract, ALU.mult)
            S.add("vector", mk_n2(), waits=fws + [(sq2, ctx["sq2_at"][tb])])
            S.add("vector", lambda v: v.drain())

            def mk_am(l2=l2, am=am):
                return lambda v: v.tensor_reduce(
                    am, l2.rearrange("p (b k) -> p b k", k=32), AX.X, ALU.max,
                    apply_absolute_value=True)
            S.add("vector", mk_am())
            S.add("vector", lambda v: v.drain())

            def mk_sc(am=am, sc=sc):
                return lambda v: v.tensor_scalar(sc, am, 1.0 / 7.5, 1e-30,
                                                 ALU.mult, ALU.max)
            S.add("vector", mk_sc())
            S.add("vector", lambda v: v.drain())

            def mk_iv(sc=sc, iv=iv):
                return lambda v: v.reciprocal(iv, sc)
            S.add("vector", mk_iv())
            S.add("vector", lambda v: v.drain())
            yws = B["y0"].wr_waits() + B["y1"].wr_waits()

            def mk_y(l2=l2, iv=iv, yb=yb):
                return lambda v: v.tensor_tensor(
                    yb, l2.rearrange("p (b k) -> p b k", k=32),
                    iv.broadcast_to([128, 32, 32]), ALU.mult)
            S.add("vector", mk_y(), waits=yws)
            S.add("vector", lambda v: v.drain())
            cws = B["c0"].wr_waits() + B["c1"].wr_waits()

            def mk_c1(yb=yb, cb=cb):
                return lambda v: v.tensor_scalar(
                    cb.bitcast(I32), yb.bitcast(I32), 0x7F800000, None,
                    ALU.bitwise_and)
            S.add("vector", mk_c1(), waits=cws)
            S.add("vector", lambda v: v.drain())

            def mk_c2(cb=cb):
                return lambda v: v.tensor_scalar(cb, cb, 1.0, 24.0, ALU.max, ALU.mult)
            S.add("vector", mk_c2())
            S.add("vector", lambda v: v.drain())

            def mk_yofs(yb=yb, cb=cb):
                return lambda v: v.scalar_tensor_tensor(yb, cb, 1.0 / 384.0, yb,
                                                        ALU.mult, ALU.add)
            S.add("vector", mk_yofs(), inc=qna)
            B["ln2h0"].did_wr(qna)
            B["y0"].did_wr(qna)
            B["y1"].did_wr(qna)
            B["c0"].did_wr(qna)
            B["c1"].did_wr(qna)
            pws = [(qna, qna.n)] + B["w0"].wr_waits() + B["w1"].wr_waits()

            def mk_w1(yb=yb, cb=cb, wb=wb):
                return lambda g: g.tensor_tensor(wb, yb, cb, ALU.add)
            S.add("gpsimd", lambda g: g.drain())
            S.add("gpsimd", mk_w1(), waits=pws, inc=qnw)
            B["w0"].did_wr(qnw)
            B["w1"].did_wr(qnw)
            S.add("vector", (lambda wb=wb: (lambda v: v.tensor_scalar(
                wb.bitcast(I32), wb.bitcast(I32), -65536, None,
                ALU.bitwise_and)))(), waits=[(qnw, qnw.n)], inc=qnr)
            S.add("vector", lambda v: v.drain())
            dws = [(qnr, qnr.n)] + B[xdk].wr_waits()

            def mk_q1(yb=yb, cb=cb, wb=wb):
                return lambda g: g.tensor_tensor(yb, wb, cb, ALU.subtract)
            S.add("gpsimd", mk_q1(), waits=dws)
            S.add("gpsimd", lambda g: g.drain())

            def mk_dq(yb=yb, sc=sc, xda_t=xda_t):
                return lambda g: g.tensor_tensor(
                    xda_t.rearrange("p (b k) -> p b k", k=32),
                    yb.rearrange("p (b k) -> p b k", k=32),
                    sc.broadcast_to([128, 32, 32]), ALU.mult)
            S.add("gpsimd", mk_dq(), inc=qnq)
            B["y0"].did_rd(qnq)
            B["y1"].did_rd(qnq)
            B["c0"].did_rd(qnq)
            B["c1"].did_rd(qnq)
            B["w0"].did_rd(qnq)
            B["w1"].did_rd(qnq)
            B[xdk].did_wr(qnq)
            ctx["qnq_at"][tb] = qnq.n

    def p2_ffn(st):
        ctx = p2ctx[st]
        p = ctx["p"]; x_sl = ctx["x_sl"]; xk = ctx["xk"]
        for tb in range(TPS):
            xdk = XDK[tb]
            xda_t = xd_ap(tb)
            sbank = stg_ps if tb % 2 == 0 else stg_ps2
            skey = "stg" if tb % 2 == 0 else "stg2"
            xtw = [(qnq, ctx["qnq_at"][tb])] + B[skey].wr_waits()

            def mk_xt(xda_t=xda_t, sbank=sbank):
                def f(te):
                    r = None
                    for c in range(8):
                        r = te.transpose(sbank[:, 128 * c:128 * (c + 1)],
                                         xda_t[:, 128 * c:128 * (c + 1)], ident[:])
                    return r
                return f
            S.add("tensor", mk_xt(), waits=xtw, inc=xdtr, drain_inc=True)
            B[xdk].did_rd(xdtr)
            B[skey].did_wr(xdtr)
            xk2 = "xdT0"
            xdT_sl = xdT[:, 0:TPS * D]
            ews3 = [(xdtr, xdtr.n)] + (B[xk2].wr_waits() if tb == 0 else [])
            dst3 = xdT_sl.rearrange("p (c i) -> p c i", c=8)[:, :, tb * 128:(tb + 1) * 128]
            src3 = sbank[:].rearrange("p (c i) -> p c i", c=8)

            def mk_xe(dst3=dst3, src3=src3):
                return lambda s: s.activation(dst3, src3, AF.Copy, bias=0.0)
            S.add("scalar", mk_xe(), waits=ews3, inc=xdev)
            B[skey].did_rd(xdev)
            if tb == 0:
                B[xk2].did_wr(xdev)
            for ocb in range(2):
                g = 2 * tb + ocb
                bank = mmA if g % 2 == 0 else mmB
                bkk = "bkA" if g % 2 == 0 else "bkB"
                gws = [(xdev, xdev.n), (wev, WEV_ALL)] + B[bkk].wr_waits()

                def mk_ff(tb=tb, ocb=ocb, bank=bank):
                    def f(te):
                        for c in range(8):
                            te.matmul(
                                bank[:],
                                xdT[:, c * 512 + tb * 128:c * 512 + (tb + 1) * 128],
                                wdT[:, c * 1024 + ocb * 512:c * 1024 + (ocb + 1) * 512],
                                start=(c == 0), stop=False)
                        return te.matmul(bank[:], onesr[:],
                                         biasf[:, ocb * 512:(ocb + 1) * 512],
                                         start=False, stop=True)
                    return f
                S.add("tensor", mk_ff(), waits=gws, inc=ffmm)
                B[bkk].did_wr(ffmm)
                if tb == TPS - 1 and ocb == 1:
                    B["xdT0"].did_rd(ffmm)

                def mk_r2(tb=tb, ocb=ocb, bank=bank, x_sl=x_sl):
                    sl = x_sl[:, tb * D + ocb * 512:tb * D + (ocb + 1) * 512]
                    return lambda v: v.tensor_tensor(sl, bank[:], sl, ALU.add)
                S.add("vector", mk_r2(), waits=[(ffmm, ffmm.n)], inc=rs2)
                B[bkk].did_rd(rs2)
        ows = [(rs2, rs2.n)]

        def mk_os(st=st, x_sl=x_sl):
            return lambda g: g.dma_start(
                out=out_d[st * ST:(st + 1) * ST, :].rearrange("(t p) d -> p t d", p=128),
                in_=x_sl.rearrange("p (t d) -> p t d", t=TPS))
        och = outp[st % 2]
        S.add("gpsimd", mk_os(), waits=ows, inc=och, amt=16)
        B[xk].did_rd(och)

    if 2 in phases:
        for st in range(NST):
            p2_front(st)
            if st >= 1:
                p2_ffn(st - 1)
            p2_attproj(st)
            p2_attproj_B(st)
        p2_ffn(NST - 1)

    S.add("gpsimd", lambda g: g.engine_nop(),
          waits=[(outp[0], outp[0].n), (outp[1], outp[1].n)] +
          ([(dbgc, dbgc.n)] if dbg else []))

    sems = ExitStack()
    for c in S.chans:
        c.sem = sems.enter_context(nc.semaphore(c.name))
    with nc.Block() as block:
        def emit(eng_name):
            drain_engs = ("vector", "scalar", "gpsimd")

            def body(e):
                for waits, f, incc, amt, dinc in S.ops[eng_name]:
                    for c, v in waits:
                        e.wait_ge(c.sem, v)
                    r = f(e)
                    if incc is not None:
                        if (eng_name in drain_engs and amt == 1) or dinc:
                            e.drain().then_inc(incc.sem, amt)
                        else:
                            r.then_inc(incc.sem, amt)
            return body
        block.sync(emit("sync"))
        block.tensor(emit("tensor"))
        block.vector(emit("vector"))
        block.scalar(emit("scalar"))
        block.gpsimd(emit("gpsimd"))
    sems.close()
    ctx.close()
    return nc


_NC_CACHE = {}


def kernel(L0_attn_x, L0_attn_W_qkv, L0_attn_W_proj, L0_attn_b_proj,
           L0_ffn_weight, L0_ffn_bias):
    x = np.ascontiguousarray(np.asarray(L0_attn_x, dtype=np.float32))
    wqkv = np.ascontiguousarray(np.asarray(L0_attn_W_qkv, dtype=np.float32))
    wp = np.ascontiguousarray(np.asarray(L0_attn_W_proj, dtype=np.float32))
    bp = np.ascontiguousarray(np.asarray(L0_attn_b_proj, dtype=np.float32))
    wf = np.ascontiguousarray(np.asarray(L0_ffn_weight, dtype=np.float32))
    bf = np.ascontiguousarray(np.asarray(L0_ffn_bias, dtype=np.float32))

    if "nc" not in _NC_CACHE:
        _NC_CACHE["nc"] = build_nc(dbg=False)
    nc = _NC_CACHE["nc"]

    in_maps = []
    for c in range(8):
        in_maps.append({
            "x": np.ascontiguousarray(x[c]),
            "wqkv": wqkv, "wproj": wp, "bproj": bp,
            "wffn": wf, "bffn": bf,
        })
    res = run_bass_kernel_spmd(nc, in_maps, core_ids=list(range(8)))
    out = np.stack([np.asarray(r["out"]) for r in res.results], axis=0)
    return out.astype(np.float32)


# version 36
# speedup vs baseline: 1.2157x; 1.2157x over previous
import numpy as np
from contextlib import ExitStack

import concourse.bass as bass
import concourse.mybir as mybir
from concourse.bass_utils import run_bass_kernel_spmd

F32 = mybir.dt.float32
F32R = mybir.dt.float32r
BF16 = mybir.dt.bfloat16
I32 = mybir.dt.int32
AF = mybir.ActivationFunctionType
ALU = mybir.AluOpType
AX = mybir.AxisListType

D = 1024
NTOK = 4096
NH = 16
DH = 64
ST = 512
NST = NTOK // ST
TPS = ST // 128
NB = D // 32

ENGS = ("sync", "tensor", "vector", "scalar", "gpsimd")


class Ch:
    __slots__ = ("name", "n", "sem")

    def __init__(self, name):
        self.name = name
        self.n = 0
        self.sem = None


class Buf:
    __slots__ = ("w", "readers")

    def __init__(self):
        self.w = None
        self.readers = []

    def rd_waits(self):
        return [self.w] if self.w else []

    def did_rd(self, ch):
        self.readers.append((ch, ch.n))

    def wr_waits(self):
        ws = list(self.readers)
        if self.w:
            ws.append(self.w)
        return ws

    def did_wr(self, ch):
        self.w = (ch, ch.n)
        self.readers = []


class Sched:
    def __init__(self):
        self.ops = {e: [] for e in ENGS}
        self.chans = []

    def ch(self, name):
        c = Ch(name)
        self.chans.append(c)
        return c

    def add(self, eng, emit, waits=(), inc=None, amt=1, drain_inc=False):
        m = {}
        for c, v in waits:
            if v > 0 and (c not in m or v > m[c]):
                m[c] = v
        self.ops[eng].append((sorted(m.items(), key=lambda kv: kv[0].name), emit, inc,
                              amt, drain_inc))
        if inc is not None:
            inc.n += amt
        return (inc, inc.n if inc else 0)


def build_nc(dbg=False, phases=(1, 2)):
    nc = bass.Bass("TRN2", target_bir_lowering=False)

    x_d = nc.dram_tensor("x", [NTOK, D], F32, kind="ExternalInput").ap()
    wqkv_d = nc.dram_tensor("wqkv", [3 * D, D], F32, kind="ExternalInput").ap()
    wp_d = nc.dram_tensor("wproj", [D, D], F32, kind="ExternalInput").ap()
    bp_d = nc.dram_tensor("bproj", [D], F32, kind="ExternalInput").ap()
    wf_d = nc.dram_tensor("wffn", [D, D], F32, kind="ExternalInput").ap()
    bfn_d = nc.dram_tensor("bffn", [D], F32, kind="ExternalInput").ap()
    out_d = nc.dram_tensor("out", [NTOK, D], F32, kind="ExternalOutput").ap()
    lnt_d = nc.dram_tensor("lnt_scratch", [NST, 128, TPS * D], BF16, kind="Internal").ap()

    dbg_o = {}
    if dbg:
        dbg_o["kvx_o"] = nc.dram_tensor("kvx_o", [128, 512], BF16, kind="ExternalOutput").ap()
        dbg_o["ksr_o"] = nc.dram_tensor("ksr_o", [1, 1024], BF16, kind="ExternalOutput").ap()
        dbg_o["kall_o"] = nc.dram_tensor("kall_o", [128, 128], BF16, kind="ExternalOutput").ap()
        dbg_o["lnt0_o"] = nc.dram_tensor("lnt0_o", [128, TPS * D], BF16, kind="ExternalOutput").ap()
        dbg_o["qfm0_o"] = nc.dram_tensor("qfm0_o", [128, 8 * 512], BF16, kind="ExternalOutput").ap()
        dbg_o["z0_o"] = nc.dram_tensor("z0_o", [16, 512], BF16, kind="ExternalOutput").ap()
        dbg_o["att0_o"] = nc.dram_tensor("att0_o", [128, 8 * 512], BF16, kind="ExternalOutput").ap()
        dbg_o["xmid0_o"] = nc.dram_tensor("xmid0_o", [128, D], F32, kind="ExternalOutput").ap()
        dbg_o["xsb0_o"] = nc.dram_tensor("xsb0_o", [128, TPS * D], F32, kind="ExternalOutput").ap()
        dbg_o["lntm0_o"] = nc.dram_tensor("lntm0_o", [128, D], BF16, kind="ExternalOutput").ap()
        dbg_o["st0_o"] = nc.dram_tensor("st0_o", [128, 8], F32, kind="ExternalOutput").ap()
        dbg_o["wdt_o"] = nc.dram_tensor("wdt_o", [128, 8 * 1024], BF16, kind="ExternalOutput").ap()
        dbg_o["xd0_o"] = nc.dram_tensor("xd0_o", [128, D], BF16, kind="ExternalOutput").ap()
        dbg_o["wst_o"] = nc.dram_tensor("wst_o", [128, D], F32, kind="ExternalOutput").ap()
        dbg_o["wyq_o"] = nc.dram_tensor("wyq_o", [128, D], F32, kind="ExternalOutput").ap()
        dbg_o["wcq_o"] = nc.dram_tensor("wcq_o", [128, D], F32, kind="ExternalOutput").ap()
        dbg_o["wscl_o"] = nc.dram_tensor("wscl_o", [128, 32], F32, kind="ExternalOutput").ap()
        dbg_o["wbf_o"] = nc.dram_tensor("wbf_o", [128, D], BF16, kind="ExternalOutput").ap()
        dbg_o["wst2_o"] = nc.dram_tensor("wst2_o", [128, D], F32, kind="ExternalOutput").ap()
        dbg_o["wbf33_o"] = nc.dram_tensor("wbf33_o", [128, D], BF16, kind="ExternalOutput").ap()
        dbg_o["stg33_o"] = nc.dram_tensor("stg33_o", [128, D], F32, kind="ExternalOutput").ap()
        dbg_o["amx_o"] = nc.dram_tensor("amx_o", [128, 32], F32, kind="ExternalOutput").ap()
        dbg_o["yq0_o"] = nc.dram_tensor("yq0_o", [128, D], F32, kind="ExternalOutput").ap()
        dbg_o["scl2_o"] = nc.dram_tensor("scl2_o", [128, 32], F32, kind="ExternalOutput").ap()
        dbg_o["inv2_o"] = nc.dram_tensor("inv2_o", [128, 32], F32, kind="ExternalOutput").ap()
        dbg_o["cq2_o"] = nc.dram_tensor("cq2_o", [128, D], F32, kind="ExternalOutput").ap()
        dbg_o["wq2_o"] = nc.dram_tensor("wq2_o", [128, D], F32, kind="ExternalOutput").ap()
        dbg_o["qq2_o"] = nc.dram_tensor("qq2_o", [128, D], F32, kind="ExternalOutput").ap()
        dbg_o["dq2_o"] = nc.dram_tensor("dq2_o", [128, D], BF16, kind="ExternalOutput").ap()

    ctx = ExitStack()
    sbt = lambda name, shape, dt: ctx.enter_context(nc.sbuf_tensor(name, shape, dt))
    pst = lambda name, shape, dt: ctx.enter_context(nc.psum_tensor(name, shape, dt))

    wkvT = sbt("wkvT", [128, 8 * 2048], BF16)
    wqT = sbt("wqT", [128, 8 * 1024], BF16)
    wpT = sbt("wpT", [128, 8 * 1024], BF16)
    wdT = sbt("wdT", [128, 8 * 1024], BF16)
    x_st = sbt("x_st", [128, 2 * TPS * D], F32)
    lnT = sbt("lnT", [128, 2 * TPS * D], BF16)
    ln_tm = sbt("ln_tm", [128, 2 * D], BF16)
    ktm = sbt("ktm", [128, 2 * D], BF16)
    vtm = sbt("vtm", [128, 2 * D], BF16)
    sqs = sbt("sqs", [128, 2 * 512], F32)
    qfm = sbt("qfm", [128, 2 * 8 * 512], BF16)
    zsb = sbt("zsb", [16, 512], BF16)
    zbc = sbt("zbc", [128, 2 * 512], F32)
    attsb = sbt("attsb", [128, 8 * 512], BF16)
    y_q = sbt("y_q", [128, 2 * 512], F32)
    c_q = sbt("c_q", [128, 2 * 512], F32)
    w_q = sbt("w_q", [128, 2 * 512], F32)
    xd = sbt("xd", [128, D], BF16)
    xdT = sbt("xdT", [128, TPS * D], BF16)
    wstage = sbt("wstage", [128, 2 * 2 * D], BF16)
    amx = sbt("amx", [128, 2 * 16], F32)
    scl = sbt("scl", [128, 4 * 16], F32)
    inv = sbt("inv", [128, 2 * 16], F32)
    K_all = sbt("K_all", [128, 128], BF16)
    sel_all = sbt("sel_all", [16, 8 * 128], BF16)
    ident = sbt("ident", [128, 128], BF16)
    onesr = sbt("onesr", [1, 128], BF16)
    onesc = sbt("onesc", [128, 1], BF16)
    ones64r = sbt("ones64r", [1, 64], BF16)
    eps6 = sbt("eps6", [128, 1], F32)
    eps5 = sbt("eps5", [128, 1], F32)
    mv_s = sbt("mv_s", [128, 8 * 2], F32)
    ve_s = sbt("ve_s", [128, 8], F32)
    sqv_s = sbt("sqv_s", [128, 8], F32)
    rstd_s = sbt("rstd_s", [128, 8], F32)
    bn6 = sbt("bn6", [128, 12], F32)

    stg_ps = pst("stg_ps", [128, 1024], BF16)
    stg_ps2 = pst("stg_ps2", [128, 1024], BF16)
    mmA = pst("mmA", [128, 512], F32)
    mmB = pst("mmB", [128, 512], F32)
    attA = pst("attA", [128, 512], F32)
    attB = pst("attB", [128, 512], F32)
    zq = pst("zq", [128, 512], F32)

    nc.const_aps.register(F32, 1e-6, eps6[:])
    nc.const_aps.register(F32, 1e-5, eps5[:])

    S = Sched()
    cst = S.ch("cst")
    iot = S.ch("iot")
    seldma = S.ch("seldma")
    wdma = S.ch("wdma")
    wdmb = S.ch("wdmb")
    bdma = S.ch("bdma")
    wfdma = S.ch("wfdma")
    wq_c = S.ch("wq")
    wtr = S.ch("wtr")
    wev = S.ch("wev")
    x_cp = [S.ch("x0"), S.ch("x1")]
    bnv = S.ch("bnv")
    sqc = S.ch("sqc")
    ln_c = S.ch("ln")
    lntr = S.ch("lntr")
    lnev = S.ch("lnev")
    lnstp = [S.ch("lnsta"), S.ch("lnstb")]
    kvmm = S.ch("kvmm")
    ksq = S.ch("ksq")
    kexp = S.ch("kexp")
    vcp = S.ch("vcp")
    kvein = S.ch("kvein")
    bnd = S.ch("bnd")
    kall = S.ch("kall")
    lnt2p = [S.ch("lnt2a"), S.ch("lnt2b")]
    qmm = S.ch("qmm")
    qsq = S.ch("qsq")
    qexp = S.ch("qexp")
    qdot = S.ch("qdot")
    zact = S.ch("zact")
    zbcmm = S.ch("zbcmm")
    zbcev = S.ch("zbcev")
    qz_c = S.ch("qz")
    attmm = S.ch("attmm")
    attev = S.ch("attev")
    pjmm = S.ch("pjmm")
    rs1 = S.ch("rs1")
    bnv2 = S.ch("bnv2")
    sq2 = S.ch("sq2")
    qna = S.ch("qna")
    qnw = S.ch("qnw")
    qnr = S.ch("qnr")
    qnq = S.ch("qnq")
    xdtr = S.ch("xdtr")
    xdev = S.ch("xdev")
    ffmm = S.ch("ffmm")
    rs2 = S.ch("rs2")
    outp = [S.ch("outsta"), S.ch("outstb")]
    dbgc = S.ch("dbgc")
    qdbg = S.ch("qdbg")

    B = {}
    for k in ["xst0", "xst1", "lnT0", "lnT1", "lntm0", "lntm1", "ktm0", "ktm1",
              "vtm0", "vtm1", "qfm0", "qfm1", "zsb", "zbc0", "zbc1", "attsb", "bkzb2", "stg2",
              "y0", "y1", "c0", "c1", "w0", "w1", "ln2h0", "ln2h1", "xd0", "xd1",
              "xdT0", "xdT1", "wst0", "wst1", "stg", "bkA", "bkB", "bkattA",
              "bkattB", "bkzq", "kvx", "ksr", "K_all", "sel", "wfst",
              "mv0", "mv1", "mv2", "mv3", "sqs0", "sqs1"]:
        B[k] = Buf()

    def dve_memset(ap, val):
        return lambda v: v.memset(ap, val)

    S.add("vector", dve_memset(eps6[:], 1e-6), inc=cst)
    S.add("vector", dve_memset(eps5[:], 1e-5), inc=cst)
    S.add("vector", dve_memset(K_all[:], 0.0), inc=cst)
    S.add("vector", dve_memset(sel_all[:].bitcast(mybir.dt.uint16), 0), inc=cst)
    S.add("vector", dve_memset(onesr[:].bitcast(mybir.dt.uint16), 0x3F80), inc=cst)
    S.add("vector", dve_memset(onesc[:].bitcast(mybir.dt.uint16), 0x3F80), inc=cst)
    S.add("vector", dve_memset(ones64r[:].bitcast(mybir.dt.uint16), 0x3F80), inc=cst)
    S.add("gpsimd", lambda g: g.iota(ident[:].bitcast(mybir.dt.int16), [[-1, 128]],
                                     base=0, channel_multiplier=1,
                                     allow_small_or_imprecise_dtypes=True), inc=iot)
    S.add("vector", lambda v: v.tensor_scalar(ident[:].bitcast(mybir.dt.int16),
                                              ident[:].bitcast(mybir.dt.int16),
                                              0, 0x3F80, ALU.is_equal, ALU.mult),
          waits=[(iot, 1)], inc=cst)
    N_CST = cst.n
    for c in range(8):
        for j in range(2):
            h = 2 * c + j

            def mk_sel(c=c, j=j, h=h):
                return lambda g: g.dma_start(
                    out=sel_all[h:h + 1, 128 * c + 64 * j:128 * c + 64 * j + 64],
                    in_=ones64r[0:1, :])
            S.add("gpsimd", mk_sel(), waits=[(cst, N_CST)], inc=seldma, amt=16)
    biasp = ln_tm[0:1, 0:D]
    biasf = ln_tm[0:1, D:2 * D]
    kvb = wstage

    def w_src(wi):
        if wi < 16:
            return wqkv_d[D + 128 * wi:D + 128 * (wi + 1), :]
        if wi < 24:
            return wqkv_d[128 * (wi - 16):128 * (wi - 15), :]
        if wi < 32:
            return wp_d[128 * (wi - 24):128 * (wi - 23), :]
        return wf_d[128 * (wi - 32):128 * (wi - 31), :]

    def w_dst(wi):
        if wi < 16:
            return wkvT, 2048, 128 * wi
        if wi < 24:
            return wqT, 1024, 128 * (wi - 16)
        if wi < 32:
            return wpT, 1024, 128 * (wi - 24)
        return wdT, 1024, 128 * (wi - 32)

    def emit_quant(eng_sched, src_ap, dst_ap, n_blk, am, sc, iv, yb, cb, wb,
                   waits0, inc_final, dbg_steps=False):
        F = 32 * n_blk
        ws = list(waits0)
        S.add(eng_sched, lambda v: v.tensor_reduce(
            am, src_ap.rearrange("p (b k) -> p b k", k=32), AX.X, ALU.max,
            apply_absolute_value=True), waits=ws,
            inc=qdbg if dbg_steps else None)
        S.add(eng_sched, lambda v: v.drain())
        if dbg_steps:
            S.add("sync", lambda s: s.dma_start(out=dbg_o["amx_o"], in_=am),
                  waits=[(qdbg, qdbg.n)], inc=dbgc, amt=16)
        S.add(eng_sched, lambda v: v.tensor_scalar(sc, am, 1.0 / 7.5, 1e-30, ALU.mult, ALU.max),
              waits=[(dbgc, dbgc.n)] if dbg_steps else [],
              inc=qdbg if dbg_steps else None)
        if dbg_steps:
            S.add("sync", lambda s: s.dma_start(out=dbg_o["scl2_o"], in_=sc),
                  waits=[(qdbg, qdbg.n)], inc=dbgc, amt=16)
        S.add(eng_sched, lambda v: v.drain())
        S.add(eng_sched, lambda v: v.reciprocal(iv, sc),
              waits=[(dbgc, dbgc.n)] if dbg_steps else [],
              inc=qdbg if dbg_steps else None)
        S.add(eng_sched, lambda v: v.drain())
        if dbg_steps:
            S.add("sync", lambda s: s.dma_start(out=dbg_o["inv2_o"], in_=iv),
                  waits=[(qdbg, qdbg.n)], inc=dbgc, amt=16)
            S.add(eng_sched, lambda v: v.drain(), waits=[(dbgc, dbgc.n)])
        S.add(eng_sched, lambda v: v.tensor_tensor(
            yb, src_ap.rearrange("p (b k) -> p b k", k=32),
            iv.broadcast_to([128, n_blk, 32]), ALU.mult),
            inc=qdbg if dbg_steps else None)
        if dbg_steps:
            S.add("sync", lambda s: s.dma_start(out=dbg_o["yq0_o"], in_=yb),
                  waits=[(qdbg, qdbg.n)], inc=dbgc, amt=16)
        S.add(eng_sched, lambda v: v.drain())
        S.add(eng_sched, lambda v: v.tensor_scalar(
            cb.bitcast(I32), yb.bitcast(I32), 0x7F800000, None, ALU.bitwise_and))
        S.add(eng_sched, lambda v: v.drain())
        S.add(eng_sched, lambda v: v.tensor_scalar(cb, cb, 1.0, 24.0, ALU.max, ALU.mult),
              inc=qdbg if dbg_steps else None)
        if dbg_steps:
            S.add("sync", lambda s: s.dma_start(out=dbg_o["cq2_o"], in_=cb),
                  waits=[(qdbg, qdbg.n)], inc=dbgc, amt=16)
        S.add(eng_sched, lambda v: v.drain())
        S.add(eng_sched, lambda v: v.tensor_tensor(wb, yb, cb, ALU.add),
              waits=[(dbgc, dbgc.n)] if dbg_steps else [])
        S.add(eng_sched, lambda v: v.drain())
        S.add(eng_sched, lambda v: v.scalar_tensor_tensor(wb, cb, 1.0 / 384.0, wb, ALU.mult, ALU.add))
        S.add(eng_sched, lambda v: v.drain())
        S.add(eng_sched, lambda v: v.tensor_scalar(
            wb.bitcast(I32), wb.bitcast(I32), -65536, None, ALU.bitwise_and),
            inc=qdbg if dbg_steps else None)
        if dbg_steps:
            S.add("sync", lambda s: s.dma_start(out=dbg_o["wq2_o"], in_=wb),
                  waits=[(qdbg, qdbg.n)], inc=dbgc, amt=16)
        S.add(eng_sched, lambda v: v.drain())
        S.add(eng_sched, lambda v: v.tensor_tensor(yb, wb, cb, ALU.subtract),
              waits=[(dbgc, dbgc.n)] if dbg_steps else [],
              inc=qdbg if dbg_steps else None)
        S.add(eng_sched, lambda v: v.drain())
        if dbg_steps:
            S.add("sync", lambda s: s.dma_start(out=dbg_o["qq2_o"], in_=yb),
                  waits=[(qdbg, qdbg.n)], inc=dbgc, amt=16)
        S.add(eng_sched, lambda v: v.tensor_tensor(
            dst_ap.rearrange("p (b k) -> p b k", k=32),
            yb.rearrange("p (b k) -> p b k", k=32),
            sc.broadcast_to([128, n_blk, 32]), ALU.mult),
            waits=[(dbgc, dbgc.n)] if dbg_steps else [], inc=inc_final)
        if dbg_steps:
            S.add("sync", lambda s: s.dma_start(out=dbg_o["dq2_o"], in_=dst_ap),
                  waits=[(inc_final, inc_final.n)], inc=dbgc, amt=16)

    def w_src_pair(wi):
        if wi < 16:
            return wqkv_d[D + 128 * wi:D + 128 * (wi + 2), :]
        if wi < 24:
            return wqkv_d[128 * (wi - 16):128 * (wi - 14), :]
        return wp_d[128 * (wi - 24):128 * (wi - 22), :]

    def emit_wtr_evac(wi, stage_ap, wst_key, ready):
        sbank = stg_ps if wi % 2 == 0 else stg_ps2
        skey = "stg" if wi % 2 == 0 else "stg2"
        tws = ready + [(cst, N_CST)] + B[skey].wr_waits()

        def mk_tr(stage_ap=stage_ap, sbank=sbank):
            def f(t):
                r = None
                for c in range(8):
                    r = t.transpose(sbank[:, 128 * c:128 * (c + 1)],
                                    stage_ap[:, 128 * c:128 * (c + 1)], ident[:])
                return r
            return f
        S.add("tensor", mk_tr(), waits=tws, inc=wtr, drain_inc=True)
        B[wst_key].did_rd(wtr)
        B[skey].did_wr(wtr)
        dstt, ofw, coff = w_dst(wi)
        dst_ap = dstt[:].rearrange("p (c o) -> p c o", c=8)[:, :, coff:coff + 128]
        src_ap2 = sbank[:].rearrange("p (c i) -> p c i", c=8)

        def mk_ev(dst_ap=dst_ap, src_ap2=src_ap2):
            return lambda s: s.activation(dst_ap, src_ap2, AF.Copy, bias=0.0)
        S.add("scalar", mk_ev(), waits=[(wtr, wtr.n)], inc=wev)
        B[skey].did_rd(wev)

    def emit_wpair(wi):
        pr = (wi // 2) % 2
        wst_key = "wst%d" % pr
        pair_ap = wstage[:, pr * 2 * D:(pr + 1) * 2 * D]
        ws = B[wst_key].wr_waits()
        if wi < 4:
            ws = ws + [(cst, N_CST)]

        def mk_wdma(wi=wi, pair_ap=pair_ap):
            return lambda g: g.dma_start(
                out=pair_ap.rearrange("p (c i) -> p c i", c=2),
                in_=w_src_pair(wi).rearrange("(c p) i -> p c i", p=128))
        wch = wdma if pr == 0 else wdmb
        S.add("gpsimd", mk_wdma(), waits=ws, inc=wch, amt=16)
        B[wst_key].did_wr(wch)
        ready = [(wch, wch.n)]
        for sub in range(2):
            emit_wtr_evac(wi + sub, pair_ap[:, sub * D:(sub + 1) * D], wst_key, ready)

    def emit_ffn_chunk(wi):
        pr = (wi // 2) % 2
        wst_key = "wst%d" % pr
        stage_ap = wstage[:, pr * 2 * D + (wi % 2) * D:pr * 2 * D + (wi % 2 + 1) * D]
        ws = (B["wfst"].wr_waits() + B["sqs0"].wr_waits() + B["sqs1"].wr_waits())

        def mk_wfdma(wi=wi):
            return lambda s: s.dma_start(out=sqs[:], in_=w_src(wi))
        S.add("sync", mk_wfdma(), waits=ws, inc=wfdma, amt=16)
        B["wfst"].did_wr(wfdma)
        B["sqs0"].did_wr(wfdma)
        B["sqs1"].did_wr(wfdma)
        qws = [(wfdma, wfdma.n), (cst, N_CST)] + B[wst_key].wr_waits()
        emit_quant("vector", sqs[:], stage_ap, 32, amx[:], scl[:, 0:32],
                   inv[:], y_q[:], c_q[:], w_q[:], qws, wq_c)
        B["wfst"].did_rd(wq_c)
        B["sqs0"].did_rd(wq_c)
        B["sqs1"].did_rd(wq_c)
        B[wst_key].did_wr(wq_c)
        emit_wtr_evac(wi, stage_ap, wst_key, [(wq_c, wq_c.n)])

    S.add("sync", lambda s: s.dma_start(
        out=x_st[:, 0:TPS * D].rearrange("p (t d) -> p t d", t=TPS),
        in_=x_d[0:ST, :].rearrange("(t p) d -> p t d", p=128)),
        waits=[], inc=x_cp[0], amt=16)
    B["xst0"].did_wr(x_cp[0])
    XC0 = (x_cp[0], x_cp[0].n)

    for k in range(16):
        emit_wpair(2 * k)
        if k % 2 == 1:
            emit_ffn_chunk(32 + k // 2)

    S.add("gpsimd", lambda g: g.dma_start(out=biasp, in_=bp_d.unsqueeze(0)),
          waits=[(lntr, NST * TPS)], inc=bdma, amt=16)
    S.add("gpsimd", lambda g: g.dma_start(out=biasf, in_=bfn_d.unsqueeze(0)),
          waits=[(lntr, NST * TPS)], inc=bdma, amt=16)
    BIAS_WDMA = bdma.n
    WEV_KV = 16
    WEV_Q = 24
    WEV_P = 32
    WEV_ALL = 40

    if dbg:
        S.add("sync", lambda s: s.dma_start(out=dbg_o["wdt_o"], in_=wdT[:]),
              waits=[(wev, WEV_ALL)], inc=dbgc, amt=16)
    for st in range(NST):
        p = st % 2
        xk = "xst%d" % p
        lk = "lnT%d" % p
        x_sl = x_st[:, p * TPS * D:(p + 1) * TPS * D]
        lnT_sl = lnT[:, p * TPS * D:(p + 1) * TPS * D]
        if st == 0:
            XC = XC0
        else:
            xw = B[xk].wr_waits()

            def mk_xl(st=st, x_sl=x_sl):
                return lambda s: s.dma_start(
                    out=x_sl.rearrange("p (t d) -> p t d", t=TPS),
                    in_=x_d[st * ST:(st + 1) * ST, :].rearrange("(t p) d -> p t d", p=128))
            xch = x_cp[st % 2]
            S.add("sync", mk_xl(), waits=xw, inc=xch, amt=16)
            B[xk].did_wr(xch)
            XC = (xch, xch.n)
        for t in range(TPS):
            xt = x_sl[:, t * D:(t + 1) * D]

            def mk_bn(xt=xt):
                return lambda v: v.bn_stats(bn6[:, 0:6], xt[:, 0:512])
            S.add("vector", mk_bn(), waits=[XC])

            def mk_bnb(xt=xt):
                return lambda v: v.bn_stats(bn6[:, 6:12], xt[:, 512:1024])
            S.add("vector", mk_bnb())
            S.add("vector", lambda v: v.drain())

            def mk_ag(t=t):
                return lambda v: v.bn_aggr(mv_s[:, 2 * t:2 * t + 2], bn6[:])
            S.add("vector", mk_ag(), inc=bnv)

            def mk_ln(t=t):
                return lambda s: s.activation(sqv_s[:, t:t + 1],
                                              mv_s[:, 2 * t + 1:2 * t + 2],
                                              AF.Ln, bias=1e-5)
            S.add("scalar", mk_ln(), waits=[(bnv, bnv.n)])
            S.add("scalar", lambda s: s.drain())

            def mk_sq(t=t):
                return lambda s: s.activation(rstd_s[:, t:t + 1], sqv_s[:, t:t + 1],
                                              AF.Exp, scale=-0.5)
            S.add("scalar", mk_sq(), inc=sqc)
        for t in range(TPS):
            tp = t % 2
            lntmk = "lntm%d" % tp
            xt = x_sl[:, t * D:(t + 1) * D]
            ln_ap = ln_tm[:, tp * D:(tp + 1) * D]

            nws = [(sqc, 4 * st + t + 1)] + B[lntmk].wr_waits()

            def mk_nm(t=t, xt=xt, ln_ap=ln_ap):
                return lambda v: v.tensor_scalar(
                    ln_ap, xt, mv_s[:, 2 * t:2 * t + 1], rstd_s[:, t:t + 1],
                    ALU.subtract, ALU.mult)
            S.add("vector", mk_nm(), waits=nws, inc=ln_c)
            B[lntmk].did_wr(ln_c)
            if t == TPS - 1:
                B[xk].did_rd(ln_c)
            if dbg and st == 0 and t == 0:
                S.add("sync", (lambda x_sl=x_sl: lambda s: s.dma_start(
                    out=dbg_o["xsb0_o"], in_=x_sl))(),
                    waits=[XC], inc=dbgc, amt=16)
                B[xk].did_rd(dbgc)
                S.add("sync", (lambda ln_ap=ln_ap: lambda s: s.dma_start(
                    out=dbg_o["lntm0_o"], in_=ln_tm[:, 0:D]))(),
                    waits=[(ln_c, ln_c.n)], inc=dbgc, amt=16)
                B[lntmk].did_rd(dbgc)
                S.add("sync", lambda s: s.dma_start(out=dbg_o["st0_o"], in_=mv_s[:, 0:8]),
                      waits=[(ln_c, ln_c.n)], inc=dbgc, amt=16)
            sbank = stg_ps if t % 2 == 0 else stg_ps2
            skey = "stg" if t % 2 == 0 else "stg2"
            tws = [(ln_c, ln_c.n), (wev, WEV_ALL)] + B[skey].wr_waits()

            def mk_lt(ln_ap=ln_ap, sbank=sbank):
                def f(te):
                    r = None
                    for c in range(8):
                        r = te.transpose(sbank[:, 128 * c:128 * (c + 1)],
                                         ln_ap[:, 128 * c:128 * (c + 1)], ident[:])
                    return r
                return f
            S.add("tensor", mk_lt(), waits=tws, inc=lntr, drain_inc=True)
            B[lntmk].did_rd(lntr)
            B[skey].did_wr(lntr)
            ews = [(lntr, lntr.n)] + B[lk].wr_waits() if t == 0 else [(lntr, lntr.n)]
            dst = lnT_sl.rearrange("p (c i) -> p c i", c=8)[:, :, t * 128:(t + 1) * 128]
            src = sbank[:].rearrange("p (c i) -> p c i", c=8)

            def mk_le(dst=dst, src=src):
                return lambda s: s.activation(dst, src, AF.Copy, bias=0.0)
            S.add("scalar", mk_le(), waits=ews, inc=lnev)
            B[skey].did_rd(lnev)
            if t == 0:
                B[lk].did_wr(lnev)
        sws = [(lnev, lnev.n)]

        def mk_ls(st=st, lnT_sl=lnT_sl):
            return lambda s: s.dma_start(out=lnt_d[st], in_=lnT_sl)
        sch = lnstp[st % 2]
        S.add("sync", mk_ls(), waits=sws, inc=sch, amt=16)
        B[lk].did_rd(sch)
        if dbg and st == 0:
            S.add("sync", lambda s: s.dma_start(out=dbg_o["lnt0_o"], in_=lnt_d[0]),
                  waits=[(lnstp[0], lnstp[0].n)], inc=dbgc, amt=16)
        for t in range(TPS):
            tp = t % 2
            ktm_ap = ktm[:, tp * D:(tp + 1) * D]
            vtm_ap = vtm[:, tp * D:(tp + 1) * D]
            for ob in (0, 2, 1, 3):
                bank = mmA if ob < 2 else mmB
                bkk = "bkA" if ob < 2 else "bkB"
                gws = [(lnev, 4 * st + t + 1), (wev, WEV_KV)] + B[bkk].wr_waits()

                def mk_g(t=t, ob=ob, bank=bank, lnT_sl=lnT_sl):
                    def f(te):
                        r = None
                        for c in range(8):
                            r = te.matmul(
                                bank[:],
                                lnT_sl[:, c * 512 + t * 128:c * 512 + (t + 1) * 128],
                                wkvT[:, c * 2048 + ob * 512:c * 2048 + (ob + 1) * 512],
                                start=(c == 0), stop=(c == 7))
                        return r
                    return f
                S.add("tensor", mk_g(), waits=gws, inc=kvmm)
                B[bkk].did_wr(kvmm)
                if t == TPS - 1 and ob == 3:
                    B[lk].did_rd(kvmm)
                if ob < 2:
                    sqp = (2 * t + ob) % 2
                    sws2 = [(kvmm, kvmm.n)] + B["sqs%d" % sqp].wr_waits()

                    def mk_s1(bank=bank, sqp=sqp):
                        return lambda s: s.activation(
                            sqs[:, sqp * 512:(sqp + 1) * 512], bank[:], AF.Square)
                    S.add("scalar", mk_s1(), waits=sws2, inc=ksq)
                    S.add("scalar", lambda s: s.drain())
                    B[bkk].did_rd(ksq)
                    B["sqs%d" % sqp].did_wr(ksq)
                    ews2 = B["ktm%d" % tp].wr_waits() if ob == 0 else []

                    def mk_s2(ktm_ap=ktm_ap, ob=ob, sqp=sqp):
                        return lambda s: s.activation(
                            ktm_ap[:, ob * 512:(ob + 1) * 512],
                            sqs[:, sqp * 512:(sqp + 1) * 512], AF.Exp, scale=-0.5)
                    S.add("scalar", mk_s2(), waits=ews2, inc=kexp)
                    B["sqs%d" % sqp].did_rd(kexp)
                    if ob == 0:
                        B["ktm%d" % tp].did_wr(kexp)
                else:
                    vws = [(kvmm, kvmm.n)] + (B["vtm%d" % tp].wr_waits() if ob == 2 else [])

                    def mk_vc(vtm_ap=vtm_ap, ob=ob, bank=bank):
                        return lambda v: v.tensor_copy(
                            vtm_ap[:, (ob - 2) * 512:(ob - 1) * 512], bank[:])
                    S.add("vector", mk_vc(), waits=vws, inc=vcp)
                    B[bkk].did_rd(vcp)
                    if ob == 2:
                        B["vtm%d" % tp].did_wr(vcp)
            kws = [(kexp, kexp.n), (vcp, vcp.n), (wev, WEV_KV)]
            first = (st == 0 and t == 0)
            last = (st == NST - 1 and t == TPS - 1)

            def mk_kv(ktm_ap=ktm_ap, vtm_ap=vtm_ap, first=first, last=last):
                def f(te):
                    for h in range(NH):
                        te.matmul(
                            attA[64 * (h % 2):64 * (h % 2) + 64,
                                 64 * (h // 2):64 * (h // 2) + 64],
                            ktm_ap[:, 64 * h:64 * (h + 1)],
                            vtm_ap[:, 64 * h:64 * (h + 1)],
                            start=(first and h < 2), stop=(last and h >= NH - 2),
                            skip_group_check=True)
                    r = te.matmul(zq[0:1, :], onesc[:], ktm_ap[:, 0:512],
                                  start=first, stop=last, skip_group_check=True)
                    r = te.matmul(attB[0:1, :], onesc[:], ktm_ap[:, 512:1024],
                                  start=first, stop=last, skip_group_check=True)
                    return r
                return f
            S.add("tensor", mk_kv(), waits=kws, inc=kvein)
            B["ktm%d" % tp].did_rd(kvein)
            B["vtm%d" % tp].did_rd(kvein)

    bws = [(kvein, kvein.n), (wtr, 40)]
    S.add("vector", dve_memset(kvb[:, 0:1024], 0.0), waits=bws)
    S.add("vector", lambda v: v.drain())
    S.add("vector", lambda v: v.tensor_copy(
        kvb[0:64, 0:1024].rearrange("p (c s i) -> p c s i", s=2, i=64)[:, :, 0, :],
        attA[0:64, :].rearrange("p (c i) -> p c i", i=64)))
    S.add("vector", lambda v: v.drain())
    S.add("vector", lambda v: v.tensor_copy(
        kvb[64:128, 0:1024].rearrange("p (c s i) -> p c s i", s=2, i=64)[:, :, 1, :],
        attA[64:128, :].rearrange("p (c i) -> p c i", i=64)))
    S.add("vector", lambda v: v.drain())
    S.add("vector", lambda v: v.tensor_copy(attsb[0:1, 0:512], zq[0:1, :]))
    S.add("vector", lambda v: v.tensor_copy(attsb[0:1, 512:1024], attB[0:1, :]), inc=bnd)
    BND = bnd.n
    for c in range(8):
        for j in range(2):
            h = 2 * c + j

            def mk_ka(c=c, j=j, h=h):
                return lambda g: g.dma_start(
                    out=K_all[64 * j:64 * j + 64, 16 * c + h:16 * c + h + 1],
                    in_=attsb[0:1, h * 64:(h + 1) * 64])
            S.add("gpsimd", mk_ka(), waits=[(bnd, BND)], inc=kall, amt=16)
    KALL = kall.n
    if dbg:
        S.add("sync", lambda s: s.dma_start(out=dbg_o["kvx_o"], in_=kvb[:, 0:512]),
              waits=[(bnd, BND)], inc=dbgc, amt=16)
        S.add("sync", lambda s: s.dma_start(out=dbg_o["ksr_o"], in_=attsb[0:1, 0:1024]),
              waits=[(bnd, BND)], inc=dbgc, amt=16)
        S.add("sync", lambda s: s.dma_start(out=dbg_o["kall_o"], in_=K_all[:]),
              waits=[(kall, KALL)], inc=dbgc, amt=16)

    zb2 = pst("zb2", [128, 512], F32)

    def xd_ap(tb):
        return [xd[:, 0:D], ktm[:, 0:D], ktm[:, D:2 * D], vtm[:, 0:D]][tb]
    XDK = ["xd0", "ktm0", "ktm1", "vtm0"]

    p2ctx = {}

    def p2_front(st):
        p = st % 2
        xk = "xst%d" % p
        lk = "lnT%d" % p
        x_sl = x_st[:, p * TPS * D:(p + 1) * TPS * D]
        lnT_sl = lnT[:, p * TPS * D:(p + 1) * TPS * D]
        qf_sl = qfm[:, p * 8 * 512:(p + 1) * 8 * 512]
        lws = B[lk].wr_waits()

        def mk_ll(st=st, lnT_sl=lnT_sl):
            return lambda s: s.dma_start(out=lnT_sl, in_=lnt_d[st])
        lch = lnt2p[st % 2]
        S.add("sync", mk_ll(),
              waits=lws + [(lnstp[0], lnstp[0].n), (lnstp[1], lnstp[1].n)],
              inc=lch, amt=16)
        B[lk].did_wr(lch)
        LNT2 = (lch, lch.n)
        xw = B[xk].wr_waits()

        def mk_xl2(st=st, x_sl=x_sl):
            return lambda s: s.dma_start(
                out=x_sl.rearrange("p (t d) -> p t d", t=TPS),
                in_=x_d[st * ST:(st + 1) * ST, :].rearrange("(t p) d -> p t d", p=128))
        xch = x_cp[st % 2]
        S.add("sync", mk_xl2(), waits=xw, inc=xch, amt=16)
        B[xk].did_wr(xch)
        XC = (xch, xch.n)
        p2ctx[st] = {"p": p, "xk": xk, "x_sl": x_sl, "qf_sl": qf_sl, "XC": XC,
                     "qnq_at": {}, "sq2_at": {}}
        for oc in range(8):
            bank = mmA if oc % 2 == 0 else mmB
            bkk = "bkA" if oc % 2 == 0 else "bkB"
            gws = [LNT2, (wev, WEV_Q)] + B[bkk].wr_waits()
            qfw = B["qfm%d" % p].wr_waits() if oc == 0 else []

            def mk_q(oc=oc, bank=bank, lnT_sl=lnT_sl):
                def f(te):
                    r = None
                    for c in range(8):
                        r = te.matmul(
                            bank[:],
                            wqT[:, c * 1024 + oc * 128:c * 1024 + (oc + 1) * 128],
                            lnT_sl[:, c * 512:(c + 1) * 512],
                            start=(c == 0), stop=(c == 7))
                    return r
                return f
            S.add("tensor", mk_q(), waits=gws, inc=qmm)
            B[bkk].did_wr(qmm)
            if oc == 7:
                B[lk].did_rd(qmm)
            sqp = oc % 2
            sws2 = [(qmm, qmm.n)] + B["sqs%d" % sqp].wr_waits()

            def mk_qs(bank=bank, sqp=sqp):
                return lambda s: s.activation(
                    sqs[:, sqp * 512:(sqp + 1) * 512], bank[:], AF.Square)
            S.add("scalar", mk_qs(), waits=sws2, inc=qsq)
            S.add("scalar", lambda s: s.drain())
            B[bkk].did_rd(qsq)
            B["sqs%d" % sqp].did_wr(qsq)

            def mk_qe(oc=oc, sqp=sqp, qf_sl=qf_sl):
                return lambda s: s.activation(
                    qf_sl[:, oc * 512:(oc + 1) * 512],
                    sqs[:, sqp * 512:(sqp + 1) * 512], AF.Exp, scale=-0.5)
            S.add("scalar", mk_qe(), waits=qfw, inc=qexp)
            B["sqs%d" % sqp].did_rd(qexp)
            if oc == 0:
                B["qfm%d" % p].did_wr(qexp)
        qdw = [(qexp, qexp.n), (kall, KALL)] + B["bkzq"].wr_waits()

        def mk_qd(qf_sl=qf_sl):
            def f(te):
                r = None
                for c in range(8):
                    r = te.matmul(zq[0:16, :], K_all[:, 16 * c:16 * (c + 1)],
                                  qf_sl[:, c * 512:(c + 1) * 512],
                                  start=(c == 0), stop=(c == 7))
                return r
            return f
        S.add("tensor", mk_qd(), waits=qdw, inc=qdot)
        B["bkzq"].did_wr(qdot)
        B["qfm%d" % p].did_rd(qdot)
        zws = [(qdot, qdot.n)] + B["zsb"].wr_waits() + B["sqs0"].wr_waits()
        S.add("scalar", lambda s: s.activation(sqs[0:16, 0:512], zq[0:16, :],
                                               AF.Ln, bias=1e-6), waits=zws)
        S.add("scalar", lambda s: s.drain())
        S.add("scalar", lambda s: s.activation(zsb[:], sqs[0:16, 0:512],
                                               AF.Exp, scale=-1.0), inc=zact)
        B["bkzq"].did_rd(zact)
        B["sqs0"].did_rd(zact)
        B["zsb"].did_wr(zact)

    def p2_attproj(st):
        ctx = p2ctx[st]
        p = ctx["p"]; x_sl = ctx["x_sl"]; qf_sl = ctx["qf_sl"]; XC = ctx["XC"]
        def emit_att(c):
            bank = attA if c % 2 == 0 else attB
            bkk = "bkattA" if c % 2 == 0 else "bkattB"
            aws = [(qz_c, qz_c.n), (bnd, BND)] + B[bkk].wr_waits()

            def mk_at(c=c, bank=bank, qf_sl=qf_sl):
                return lambda te: te.matmul(
                    bank[:],
                    kvb[:, 128 * c:128 * (c + 1)],
                    qf_sl[:, c * 512:(c + 1) * 512],
                    start=True, stop=True)
            S.add("tensor", mk_at(), waits=aws, inc=attmm)
            B[bkk].did_wr(attmm)
            if c == 7:
                B["qfm%d" % p].did_rd(attmm)
                B["zsb"].did_rd(attmm)
            ew = [(attmm, attmm.n)] + (B["attsb"].wr_waits() if c == 0 else [])
            if st == 0 and c <= 1:
                ew = ew + [(kall, KALL)]

            def mk_ae(c=c, bank=bank):
                return lambda s: s.activation(
                    attsb[:, c * 512:(c + 1) * 512],
                    bank[:], AF.Copy, bias=0.0)
            S.add("scalar", mk_ae(), waits=ew, inc=attev)
            B[bkk].did_rd(attev)
            if c == 0:
                B["attsb"].did_wr(attev)

        for c in range(8):
            zbank = zb2 if c % 2 == 0 else zq
            zbkk = "bkzb2" if c % 2 == 0 else "bkzq"
            zw = [(zact, zact.n)] + B[zbkk].wr_waits()
            if st == 0 and c <= 1:
                zw = zw + [(seldma, seldma.n)]

            def mk_zb(c=c, zbank=zbank):
                return lambda te: te.matmul(zbank[:], sel_all[:, 128 * c:128 * (c + 1)],
                                            zsb[:], start=True, stop=True)
            S.add("tensor", mk_zb(), waits=zw, inc=zbcmm)
            B[zbkk].did_wr(zbcmm)
            qzw = [(zbcmm, zbcmm.n), (qexp, qexp.n)]

            def mk_qz(c=c, zbank=zbank, qf_sl=qf_sl):
                return lambda v: v.tensor_tensor(
                    qf_sl[:, c * 512:(c + 1) * 512], qf_sl[:, c * 512:(c + 1) * 512],
                    zbank[:], ALU.mult)
            S.add("vector", mk_qz(), waits=qzw, inc=qz_c)
            B[zbkk].did_rd(qz_c)
            if c >= 1:
                emit_att(c - 1)
        emit_att(7)
        for tb in range(TPS):
            for ocb in range(2):
                g = 2 * tb + ocb
                bank = mmA if g % 2 == 0 else mmB
                bkk = "bkA" if g % 2 == 0 else "bkB"
                gws = [(attev, attev.n), (wev, WEV_P), (bdma, BIAS_WDMA)] + B[bkk].wr_waits()

                def mk_pj(tb=tb, ocb=ocb, bank=bank):
                    def f(te):
                        for c in range(8):
                            te.matmul(
                                bank[:],
                                attsb[:, c * 512 + tb * 128:c * 512 + (tb + 1) * 128],
                                wpT[:, c * 1024 + ocb * 512:c * 1024 + (ocb + 1) * 512],
                                start=(c == 0), stop=False)
                        return te.matmul(bank[:], onesr[:],
                                         biasp[:, ocb * 512:(ocb + 1) * 512],
                                         start=False, stop=True)
                    return f
                S.add("tensor", mk_pj(), waits=gws, inc=pjmm)
                B[bkk].did_wr(pjmm)
                if tb == TPS - 1 and ocb == 1:
                    B["attsb"].did_rd(pjmm)
                rws = [(pjmm, pjmm.n), XC]

                def mk_r1(tb=tb, ocb=ocb, bank=bank, x_sl=x_sl):
                    sl = x_sl[:, tb * D + ocb * 512:tb * D + (ocb + 1) * 512]
                    return lambda v: v.tensor_tensor(sl, bank[:], sl, ALU.add)
                S.add("vector", mk_r1(), waits=rws, inc=rs1)
                B[bkk].did_rd(rs1)
            tp = tb % 2
            xt = x_sl[:, tb * D:(tb + 1) * D]

            def mk_bn2(xt=xt):
                return lambda v: v.bn_stats(bn6[:, 0:6], xt[:, 0:512])
            S.add("vector", mk_bn2())

            def mk_bn2b(xt=xt):
                return lambda v: v.bn_stats(bn6[:, 6:12], xt[:, 512:1024])
            S.add("vector", mk_bn2b())
            S.add("vector", lambda v: v.drain())

            tq_ = 4 * (st % 2) + tb

            def mk_ag2(tq_=tq_):
                return lambda v: v.bn_aggr(mv_s[:, 2 * tq_:2 * tq_ + 2], bn6[:])
            S.add("vector", mk_ag2(), inc=bnv2)

            def mk_ln2(tq_=tq_):
                return lambda s: s.activation(sqv_s[:, tq_:tq_ + 1],
                                              mv_s[:, 2 * tq_ + 1:2 * tq_ + 2],
                                              AF.Ln, bias=1e-5)
            S.add("scalar", mk_ln2(), waits=[(bnv2, bnv2.n)])
            S.add("scalar", lambda s: s.drain())

            def mk_sq2(tq_=tq_):
                return lambda s: s.activation(rstd_s[:, tq_:tq_ + 1], sqv_s[:, tq_:tq_ + 1],
                                              AF.Exp, scale=-0.5)
            S.add("scalar", mk_sq2(), inc=sq2)
            ctx["sq2_at"][tb] = sq2.n
            if tb >= 1:
                p2_quant_tile(st, tb - 1)

    def p2_quant_tile(st, tb):
        ctx = p2ctx[st]
        p = ctx["p"]; x_sl = ctx["x_sl"]; XC = ctx["XC"]
        if True:
            xt = x_sl[:, tb * D:(tb + 1) * D]
            xdk = XDK[tb]
            xda_t = xd_ap(tb)
            l2 = zbc[:, 0:D]
            yb = y_q[:, 0:D]
            cb = c_q[:, 0:D]
            wb = w_q[:, 0:D]
            am = amx[:, 0:32]
            sc = scl[:, 32 * (tb % 2):32 * (tb % 2) + 32]
            iv = inv[:, 0:32]
            fws = B["ln2h0"].wr_waits()

            tq_ = 4 * (st % 2) + tb

            def mk_n2(xt=xt, l2=l2, tq_=tq_):
                return lambda v: v.tensor_scalar(
                    l2, xt, mv_s[:, 2 * tq_:2 * tq_ + 1], rstd_s[:, tq_:tq_ + 1],
                    ALU.subtract, ALU.mult)
            S.add("vector", mk_n2(), waits=fws + [(sq2, ctx["sq2_at"][tb])])
            S.add("vector", lambda v: v.drain())

            def mk_am(l2=l2, am=am):
                return lambda v: v.tensor_reduce(
                    am, l2.rearrange("p (b k) -> p b k", k=32), AX.X, ALU.max,
                    apply_absolute_value=True)
            S.add("vector", mk_am())
            S.add("vector", lambda v: v.drain())

            def mk_sc(am=am, sc=sc):
                return lambda v: v.tensor_scalar(sc, am, 1.0 / 7.5, 1e-30,
                                                 ALU.mult, ALU.max)
            S.add("vector", mk_sc())
            S.add("vector", lambda v: v.drain())

            def mk_iv(sc=sc, iv=iv):
                return lambda v: v.reciprocal(iv, sc)
            S.add("vector", mk_iv())
            S.add("vector", lambda v: v.drain())
            yws = B["y0"].wr_waits() + B["y1"].wr_waits()

            def mk_y(l2=l2, iv=iv, yb=yb):
                return lambda v: v.tensor_tensor(
                    yb, l2.rearrange("p (b k) -> p b k", k=32),
                    iv.broadcast_to([128, 32, 32]), ALU.mult)
            S.add("vector", mk_y(), waits=yws)
            S.add("vector", lambda v: v.drain())
            cws = B["c0"].wr_waits() + B["c1"].wr_waits()

            def mk_c1(yb=yb, cb=cb):
                return lambda v: v.tensor_scalar(
                    cb.bitcast(I32), yb.bitcast(I32), 0x7F800000, None,
                    ALU.bitwise_and)
            S.add("vector", mk_c1(), waits=cws)
            S.add("vector", lambda v: v.drain())

            def mk_c2(cb=cb):
                return lambda v: v.tensor_scalar(cb, cb, 1.0, 24.0, ALU.max, ALU.mult)
            S.add("vector", mk_c2())
            S.add("vector", lambda v: v.drain())

            def mk_yofs(yb=yb, cb=cb):
                return lambda v: v.scalar_tensor_tensor(yb, cb, 1.0 / 384.0, yb,
                                                        ALU.mult, ALU.add)
            S.add("vector", mk_yofs(), inc=qna)
            B["ln2h0"].did_wr(qna)
            B["y0"].did_wr(qna)
            B["y1"].did_wr(qna)
            B["c0"].did_wr(qna)
            B["c1"].did_wr(qna)
            pws = [(qna, qna.n)] + B["w0"].wr_waits() + B["w1"].wr_waits()

            def mk_w1(yb=yb, cb=cb, wb=wb):
                return lambda g: g.tensor_tensor(wb, yb, cb, ALU.add)
            S.add("gpsimd", lambda g: g.drain())
            S.add("gpsimd", mk_w1(), waits=pws, inc=qnw)
            B["w0"].did_wr(qnw)
            B["w1"].did_wr(qnw)
            S.add("vector", (lambda wb=wb: (lambda v: v.tensor_scalar(
                wb.bitcast(I32), wb.bitcast(I32), -65536, None,
                ALU.bitwise_and)))(), waits=[(qnw, qnw.n)], inc=qnr)
            S.add("vector", lambda v: v.drain())
            dws = [(qnr, qnr.n)] + B[xdk].wr_waits()

            def mk_q1(yb=yb, cb=cb, wb=wb):
                return lambda g: g.tensor_tensor(yb, wb, cb, ALU.subtract)
            S.add("gpsimd", mk_q1(), waits=dws)
            S.add("gpsimd", lambda g: g.drain())

            def mk_dq(yb=yb, sc=sc, xda_t=xda_t):
                return lambda g: g.tensor_tensor(
                    xda_t.rearrange("p (b k) -> p b k", k=32),
                    yb.rearrange("p (b k) -> p b k", k=32),
                    sc.broadcast_to([128, 32, 32]), ALU.mult)
            S.add("gpsimd", mk_dq(), inc=qnq)
            B["y0"].did_rd(qnq)
            B["y1"].did_rd(qnq)
            B["c0"].did_rd(qnq)
            B["c1"].did_rd(qnq)
            B["w0"].did_rd(qnq)
            B["w1"].did_rd(qnq)
            B[xdk].did_wr(qnq)
            ctx["qnq_at"][tb] = qnq.n

    def p2_ffn(st):
        ctx = p2ctx[st]
        p = ctx["p"]; x_sl = ctx["x_sl"]; xk = ctx["xk"]
        for tb in range(TPS):
            xdk = XDK[tb]
            xda_t = xd_ap(tb)
            sbank = stg_ps if tb % 2 == 0 else stg_ps2
            skey = "stg" if tb % 2 == 0 else "stg2"
            xtw = [(qnq, ctx["qnq_at"][tb])] + B[skey].wr_waits()

            def mk_xt(xda_t=xda_t, sbank=sbank):
                def f(te):
                    r = None
                    for c in range(8):
                        r = te.transpose(sbank[:, 128 * c:128 * (c + 1)],
                                         xda_t[:, 128 * c:128 * (c + 1)], ident[:])
                    return r
                return f
            S.add("tensor", mk_xt(), waits=xtw, inc=xdtr, drain_inc=True)
            B[xdk].did_rd(xdtr)
            B[skey].did_wr(xdtr)
            xk2 = "xdT0"
            xdT_sl = xdT[:, 0:TPS * D]
            ews3 = [(xdtr, xdtr.n)] + (B[xk2].wr_waits() if tb == 0 else [])
            dst3 = xdT_sl.rearrange("p (c i) -> p c i", c=8)[:, :, tb * 128:(tb + 1) * 128]
            src3 = sbank[:].rearrange("p (c i) -> p c i", c=8)

            def mk_xe(dst3=dst3, src3=src3):
                return lambda s: s.activation(dst3, src3, AF.Copy, bias=0.0)
            S.add("scalar", mk_xe(), waits=ews3, inc=xdev)
            B[skey].did_rd(xdev)
            if tb == 0:
                B[xk2].did_wr(xdev)
            for ocb in range(2):
                g = 2 * tb + ocb
                bank = mmA if g % 2 == 0 else mmB
                bkk = "bkA" if g % 2 == 0 else "bkB"
                gws = [(xdev, xdev.n), (wev, WEV_ALL)] + B[bkk].wr_waits()

                def mk_ff(tb=tb, ocb=ocb, bank=bank):
                    def f(te):
                        for c in range(8):
                            te.matmul(
                                bank[:],
                                xdT[:, c * 512 + tb * 128:c * 512 + (tb + 1) * 128],
                                wdT[:, c * 1024 + ocb * 512:c * 1024 + (ocb + 1) * 512],
                                start=(c == 0), stop=False)
                        return te.matmul(bank[:], onesr[:],
                                         biasf[:, ocb * 512:(ocb + 1) * 512],
                                         start=False, stop=True)
                    return f
                S.add("tensor", mk_ff(), waits=gws, inc=ffmm)
                B[bkk].did_wr(ffmm)
                if tb == TPS - 1 and ocb == 1:
                    B["xdT0"].did_rd(ffmm)

                def mk_r2(tb=tb, ocb=ocb, bank=bank, x_sl=x_sl):
                    sl = x_sl[:, tb * D + ocb * 512:tb * D + (ocb + 1) * 512]
                    return lambda v: v.tensor_tensor(sl, bank[:], sl, ALU.add)
                S.add("vector", mk_r2(), waits=[(ffmm, ffmm.n)], inc=rs2)
                B[bkk].did_rd(rs2)
        ows = [(rs2, rs2.n)]

        def mk_os(st=st, x_sl=x_sl):
            return lambda g: g.dma_start(
                out=out_d[st * ST:(st + 1) * ST, :].rearrange("(t p) d -> p t d", p=128),
                in_=x_sl.rearrange("p (t d) -> p t d", t=TPS))
        och = outp[st % 2]
        S.add("gpsimd", mk_os(), waits=ows, inc=och, amt=16)
        B[xk].did_rd(och)

    if 2 in phases:
        for st in range(NST):
            p2_front(st)
            if st >= 1:
                p2_ffn(st - 1)
            p2_attproj(st)
            p2_quant_tile(st, TPS - 1)
        p2_ffn(NST - 1)

    S.add("gpsimd", lambda g: g.engine_nop(),
          waits=[(outp[0], outp[0].n), (outp[1], outp[1].n)] +
          ([(dbgc, dbgc.n)] if dbg else []))

    sems = ExitStack()
    for c in S.chans:
        c.sem = sems.enter_context(nc.semaphore(c.name))
    with nc.Block() as block:
        def emit(eng_name):
            drain_engs = ("vector", "scalar", "gpsimd")

            def body(e):
                for waits, f, incc, amt, dinc in S.ops[eng_name]:
                    for c, v in waits:
                        e.wait_ge(c.sem, v)
                    r = f(e)
                    if incc is not None:
                        if (eng_name in drain_engs and amt == 1) or dinc:
                            e.drain().then_inc(incc.sem, amt)
                        else:
                            r.then_inc(incc.sem, amt)
            return body
        block.sync(emit("sync"))
        block.tensor(emit("tensor"))
        block.vector(emit("vector"))
        block.scalar(emit("scalar"))
        block.gpsimd(emit("gpsimd"))
    sems.close()
    ctx.close()
    return nc


_NC_CACHE = {}


def kernel(L0_attn_x, L0_attn_W_qkv, L0_attn_W_proj, L0_attn_b_proj,
           L0_ffn_weight, L0_ffn_bias):
    x = np.ascontiguousarray(np.asarray(L0_attn_x, dtype=np.float32))
    wqkv = np.ascontiguousarray(np.asarray(L0_attn_W_qkv, dtype=np.float32))
    wp = np.ascontiguousarray(np.asarray(L0_attn_W_proj, dtype=np.float32))
    bp = np.ascontiguousarray(np.asarray(L0_attn_b_proj, dtype=np.float32))
    wf = np.ascontiguousarray(np.asarray(L0_ffn_weight, dtype=np.float32))
    bf = np.ascontiguousarray(np.asarray(L0_ffn_bias, dtype=np.float32))

    if "nc" not in _NC_CACHE:
        _NC_CACHE["nc"] = build_nc(dbg=False)
    nc = _NC_CACHE["nc"]

    in_maps = []
    for c in range(8):
        in_maps.append({
            "x": np.ascontiguousarray(x[c]),
            "wqkv": wqkv, "wproj": wp, "bproj": bp,
            "wffn": wf, "bffn": bf,
        })
    res = run_bass_kernel_spmd(nc, in_maps, core_ids=list(range(8)))
    out = np.stack([np.asarray(r["out"]) for r in res.results], axis=0)
    return out.astype(np.float32)


# version 39
# speedup vs baseline: 1.2237x; 1.0066x over previous
import numpy as np
from contextlib import ExitStack

import concourse.bass as bass
import concourse.mybir as mybir
from concourse.bass_utils import run_bass_kernel_spmd

F32 = mybir.dt.float32
F32R = mybir.dt.float32r
BF16 = mybir.dt.bfloat16
I32 = mybir.dt.int32
AF = mybir.ActivationFunctionType
ALU = mybir.AluOpType
AX = mybir.AxisListType

D = 1024
NTOK = 4096
NH = 16
DH = 64
ST = 512
NST = NTOK // ST
TPS = ST // 128
NB = D // 32

ENGS = ("sync", "tensor", "vector", "scalar", "gpsimd")


class Ch:
    __slots__ = ("name", "n", "sem")

    def __init__(self, name):
        self.name = name
        self.n = 0
        self.sem = None


class Buf:
    __slots__ = ("w", "readers")

    def __init__(self):
        self.w = None
        self.readers = []

    def rd_waits(self):
        return [self.w] if self.w else []

    def did_rd(self, ch):
        self.readers.append((ch, ch.n))

    def wr_waits(self):
        ws = list(self.readers)
        if self.w:
            ws.append(self.w)
        return ws

    def did_wr(self, ch):
        self.w = (ch, ch.n)
        self.readers = []


class Sched:
    def __init__(self):
        self.ops = {e: [] for e in ENGS}
        self.chans = []

    def ch(self, name):
        c = Ch(name)
        self.chans.append(c)
        return c

    def add(self, eng, emit, waits=(), inc=None, amt=1, drain_inc=False):
        m = {}
        for c, v in waits:
            if v > 0 and (c not in m or v > m[c]):
                m[c] = v
        self.ops[eng].append((sorted(m.items(), key=lambda kv: kv[0].name), emit, inc,
                              amt, drain_inc))
        if inc is not None:
            inc.n += amt
        return (inc, inc.n if inc else 0)


def build_nc(dbg=False, phases=(1, 2)):
    nc = bass.Bass("TRN2", target_bir_lowering=False)

    x_d = nc.dram_tensor("x", [NTOK, D], F32, kind="ExternalInput").ap()
    wqkv_d = nc.dram_tensor("wqkv", [3 * D, D], F32, kind="ExternalInput").ap()
    wp_d = nc.dram_tensor("wproj", [D, D], F32, kind="ExternalInput").ap()
    bp_d = nc.dram_tensor("bproj", [D], F32, kind="ExternalInput").ap()
    wf_d = nc.dram_tensor("wffn", [D, D], F32, kind="ExternalInput").ap()
    bfn_d = nc.dram_tensor("bffn", [D], F32, kind="ExternalInput").ap()
    out_d = nc.dram_tensor("out", [NTOK, D], F32, kind="ExternalOutput").ap()
    lnt_d = nc.dram_tensor("lnt_scratch", [NST, 128, TPS * D], BF16, kind="Internal").ap()

    dbg_o = {}
    if dbg:
        dbg_o["kvx_o"] = nc.dram_tensor("kvx_o", [128, 512], BF16, kind="ExternalOutput").ap()
        dbg_o["ksr_o"] = nc.dram_tensor("ksr_o", [1, 1024], BF16, kind="ExternalOutput").ap()
        dbg_o["kall_o"] = nc.dram_tensor("kall_o", [128, 128], BF16, kind="ExternalOutput").ap()
        dbg_o["lnt0_o"] = nc.dram_tensor("lnt0_o", [128, TPS * D], BF16, kind="ExternalOutput").ap()
        dbg_o["qfm0_o"] = nc.dram_tensor("qfm0_o", [128, 8 * 512], BF16, kind="ExternalOutput").ap()
        dbg_o["z0_o"] = nc.dram_tensor("z0_o", [16, 512], BF16, kind="ExternalOutput").ap()
        dbg_o["att0_o"] = nc.dram_tensor("att0_o", [128, 8 * 512], BF16, kind="ExternalOutput").ap()
        dbg_o["xmid0_o"] = nc.dram_tensor("xmid0_o", [128, D], F32, kind="ExternalOutput").ap()
        dbg_o["xsb0_o"] = nc.dram_tensor("xsb0_o", [128, TPS * D], F32, kind="ExternalOutput").ap()
        dbg_o["lntm0_o"] = nc.dram_tensor("lntm0_o", [128, D], BF16, kind="ExternalOutput").ap()
        dbg_o["st0_o"] = nc.dram_tensor("st0_o", [128, 8], F32, kind="ExternalOutput").ap()
        dbg_o["wdt_o"] = nc.dram_tensor("wdt_o", [128, 8 * 1024], BF16, kind="ExternalOutput").ap()
        dbg_o["xd0_o"] = nc.dram_tensor("xd0_o", [128, D], BF16, kind="ExternalOutput").ap()
        dbg_o["wst_o"] = nc.dram_tensor("wst_o", [128, D], F32, kind="ExternalOutput").ap()
        dbg_o["wyq_o"] = nc.dram_tensor("wyq_o", [128, D], F32, kind="ExternalOutput").ap()
        dbg_o["wcq_o"] = nc.dram_tensor("wcq_o", [128, D], F32, kind="ExternalOutput").ap()
        dbg_o["wscl_o"] = nc.dram_tensor("wscl_o", [128, 32], F32, kind="ExternalOutput").ap()
        dbg_o["wbf_o"] = nc.dram_tensor("wbf_o", [128, D], BF16, kind="ExternalOutput").ap()
        dbg_o["wst2_o"] = nc.dram_tensor("wst2_o", [128, D], F32, kind="ExternalOutput").ap()
        dbg_o["wbf33_o"] = nc.dram_tensor("wbf33_o", [128, D], BF16, kind="ExternalOutput").ap()
        dbg_o["stg33_o"] = nc.dram_tensor("stg33_o", [128, D], F32, kind="ExternalOutput").ap()
        dbg_o["amx_o"] = nc.dram_tensor("amx_o", [128, 32], F32, kind="ExternalOutput").ap()
        dbg_o["yq0_o"] = nc.dram_tensor("yq0_o", [128, D], F32, kind="ExternalOutput").ap()
        dbg_o["scl2_o"] = nc.dram_tensor("scl2_o", [128, 32], F32, kind="ExternalOutput").ap()
        dbg_o["inv2_o"] = nc.dram_tensor("inv2_o", [128, 32], F32, kind="ExternalOutput").ap()
        dbg_o["cq2_o"] = nc.dram_tensor("cq2_o", [128, D], F32, kind="ExternalOutput").ap()
        dbg_o["wq2_o"] = nc.dram_tensor("wq2_o", [128, D], F32, kind="ExternalOutput").ap()
        dbg_o["qq2_o"] = nc.dram_tensor("qq2_o", [128, D], F32, kind="ExternalOutput").ap()
        dbg_o["dq2_o"] = nc.dram_tensor("dq2_o", [128, D], BF16, kind="ExternalOutput").ap()

    ctx = ExitStack()
    sbt = lambda name, shape, dt: ctx.enter_context(nc.sbuf_tensor(name, shape, dt))
    pst = lambda name, shape, dt: ctx.enter_context(nc.psum_tensor(name, shape, dt))

    wkvT = sbt("wkvT", [128, 8 * 2048], BF16)
    wqT = sbt("wqT", [128, 8 * 1024], BF16)
    wpT = sbt("wpT", [128, 8 * 1024], BF16)
    wdT = sbt("wdT", [128, 8 * 1024], BF16)
    x_st = sbt("x_st", [128, 2 * TPS * D], F32)
    lnT = sbt("lnT", [128, 2 * TPS * D], BF16)
    ln_tm = sbt("ln_tm", [128, 2 * D], BF16)
    ktm = sbt("ktm", [128, 2 * D], BF16)
    vtm = sbt("vtm", [128, 2 * D], BF16)
    sqs = sbt("sqs", [128, 2 * 512], F32)
    qfm = sbt("qfm", [128, 2 * 8 * 512], BF16)
    zsb = sbt("zsb", [16, 512], BF16)
    zbc = sbt("zbc", [128, 2 * 512], F32)
    attsb = sbt("attsb", [128, 8 * 512], BF16)
    y_q = sbt("y_q", [128, 2 * 512], F32)
    c_q = sbt("c_q", [128, 2 * 512], F32)
    w_q = sbt("w_q", [128, 2 * 512], F32)
    xd = sbt("xd", [128, D], BF16)
    xdT = sbt("xdT", [128, TPS * D], BF16)
    wstage = sbt("wstage", [128, 2 * 2 * D], BF16)
    amx = sbt("amx", [128, 2 * 16], F32)
    scl = sbt("scl", [128, 4 * 16], F32)
    inv = sbt("inv", [128, 2 * 16], F32)
    K_all = sbt("K_all", [128, 128], BF16)
    sel_all = sbt("sel_all", [16, 8 * 128], BF16)
    ident = sbt("ident", [128, 128], BF16)
    onesr = sbt("onesr", [1, 128], BF16)
    onesc = sbt("onesc", [128, 1], BF16)
    ones64r = sbt("ones64r", [1, 64], BF16)
    eps6 = sbt("eps6", [128, 1], F32)
    eps5 = sbt("eps5", [128, 1], F32)
    mv_s = sbt("mv_s", [128, 8 * 2], F32)
    ve_s = sbt("ve_s", [128, 8], F32)
    sqv_s = sbt("sqv_s", [128, 8], F32)
    rstd_s = sbt("rstd_s", [128, 8], F32)
    bn6 = sbt("bn6", [128, 12], F32)

    stg_ps = pst("stg_ps", [128, 1024], BF16)
    stg_ps2 = pst("stg_ps2", [128, 1024], BF16)
    mmA = pst("mmA", [128, 512], F32)
    mmB = pst("mmB", [128, 512], F32)
    attA = pst("attA", [128, 512], F32)
    attB = pst("attB", [128, 512], F32)
    zq = pst("zq", [128, 512], F32)

    nc.const_aps.register(F32, 1e-6, eps6[:])
    nc.const_aps.register(F32, 1e-5, eps5[:])

    S = Sched()
    cst = S.ch("cst")
    iot = S.ch("iot")
    seldma = S.ch("seldma")
    wdma = S.ch("wdma")
    wdmb = S.ch("wdmb")
    bdma = S.ch("bdma")
    wfdma = S.ch("wfdma")
    wq_c = S.ch("wq")
    wtr = S.ch("wtr")
    wev = S.ch("wev")
    x_cp = [S.ch("x0"), S.ch("x1")]
    bnv = S.ch("bnv")
    sqc = S.ch("sqc")
    ln_c = S.ch("ln")
    lntr = S.ch("lntr")
    lnev = S.ch("lnev")
    lnstp = [S.ch("lnsta"), S.ch("lnstb")]
    kvmm = S.ch("kvmm")
    ksq = S.ch("ksq")
    kexp = S.ch("kexp")
    vcp = S.ch("vcp")
    kvein = S.ch("kvein")
    bnd = S.ch("bnd")
    kall = S.ch("kall")
    lnt2p = [S.ch("lnt2a"), S.ch("lnt2b")]
    qmm = S.ch("qmm")
    qsq = S.ch("qsq")
    qexp = S.ch("qexp")
    qdot = S.ch("qdot")
    zact = S.ch("zact")
    zbcmm = S.ch("zbcmm")
    zbcev = S.ch("zbcev")
    qz_c = S.ch("qz")
    attmm = S.ch("attmm")
    attev = S.ch("attev")
    pjmm = S.ch("pjmm")
    rs1 = S.ch("rs1")
    bnv2 = S.ch("bnv2")
    sq2 = S.ch("sq2")
    qna = S.ch("qna")
    qnw = S.ch("qnw")
    qnr = S.ch("qnr")
    qnq = S.ch("qnq")
    xdtr = S.ch("xdtr")
    xdev = S.ch("xdev")
    ffmm = S.ch("ffmm")
    rs2 = S.ch("rs2")
    outp = [S.ch("outsta"), S.ch("outstb")]
    dbgc = S.ch("dbgc")
    qdbg = S.ch("qdbg")

    B = {}
    for k in ["xst0", "xst1", "lnT0", "lnT1", "lntm0", "lntm1", "ktm0", "ktm1",
              "vtm0", "vtm1", "qfm0", "qfm1", "zsb", "zbc0", "zbc1", "attsb", "bkzb2", "stg2",
              "y0", "y1", "c0", "c1", "w0", "w1", "ln2h0", "ln2h1", "xd0", "xd1",
              "xdT0", "xdT1", "wst0", "wst1", "stg", "bkA", "bkB", "bkattA",
              "bkattB", "bkzq", "kvx", "ksr", "K_all", "sel", "wfst",
              "mv0", "mv1", "mv2", "mv3", "sqs0", "sqs1"]:
        B[k] = Buf()

    def dve_memset(ap, val):
        return lambda v: v.memset(ap, val)

    S.add("vector", dve_memset(eps6[:], 1e-6), inc=cst)
    S.add("vector", dve_memset(eps5[:], 1e-5), inc=cst)
    S.add("vector", dve_memset(K_all[:], 0.0), inc=cst)
    S.add("vector", dve_memset(sel_all[:].bitcast(mybir.dt.uint16), 0), inc=cst)
    S.add("vector", dve_memset(onesr[:].bitcast(mybir.dt.uint16), 0x3F80), inc=cst)
    S.add("vector", dve_memset(onesc[:].bitcast(mybir.dt.uint16), 0x3F80), inc=cst)
    S.add("vector", dve_memset(ones64r[:].bitcast(mybir.dt.uint16), 0x3F80), inc=cst)
    S.add("gpsimd", lambda g: g.iota(ident[:].bitcast(mybir.dt.int16), [[-1, 128]],
                                     base=0, channel_multiplier=1,
                                     allow_small_or_imprecise_dtypes=True), inc=iot)
    S.add("vector", lambda v: v.tensor_scalar(ident[:].bitcast(mybir.dt.int16),
                                              ident[:].bitcast(mybir.dt.int16),
                                              0, 0x3F80, ALU.is_equal, ALU.mult),
          waits=[(iot, 1)], inc=cst)
    N_CST = cst.n
    for c in range(8):
        for j in range(2):
            h = 2 * c + j

            def mk_sel(c=c, j=j, h=h):
                return lambda g: g.dma_start(
                    out=sel_all[h:h + 1, 128 * c + 64 * j:128 * c + 64 * j + 64],
                    in_=ones64r[0:1, :])
            S.add("gpsimd", mk_sel(), waits=[(cst, N_CST)], inc=seldma, amt=16)
    biasp = ln_tm[0:1, 0:D]
    biasf = ln_tm[0:1, D:2 * D]
    kvb = wstage

    def w_src(wi):
        if wi < 16:
            return wqkv_d[D + 128 * wi:D + 128 * (wi + 1), :]
        if wi < 24:
            return wqkv_d[128 * (wi - 16):128 * (wi - 15), :]
        if wi < 32:
            return wp_d[128 * (wi - 24):128 * (wi - 23), :]
        return wf_d[128 * (wi - 32):128 * (wi - 31), :]

    def w_dst(wi):
        if wi < 16:
            return wkvT, 2048, 128 * wi
        if wi < 24:
            return wqT, 1024, 128 * (wi - 16)
        if wi < 32:
            return wpT, 1024, 128 * (wi - 24)
        return wdT, 1024, 128 * (wi - 32)

    def emit_quant(eng_sched, src_ap, dst_ap, n_blk, am, sc, iv, yb, cb, wb,
                   waits0, inc_final, dbg_steps=False):
        F = 32 * n_blk
        ws = list(waits0)
        S.add(eng_sched, lambda v: v.tensor_reduce(
            am, src_ap.rearrange("p (b k) -> p b k", k=32), AX.X, ALU.max,
            apply_absolute_value=True), waits=ws,
            inc=qdbg if dbg_steps else None)
        S.add(eng_sched, lambda v: v.drain())
        if dbg_steps:
            S.add("sync", lambda s: s.dma_start(out=dbg_o["amx_o"], in_=am),
                  waits=[(qdbg, qdbg.n)], inc=dbgc, amt=16)
        S.add(eng_sched, lambda v: v.tensor_scalar(sc, am, 1.0 / 7.5, 1e-30, ALU.mult, ALU.max),
              waits=[(dbgc, dbgc.n)] if dbg_steps else [],
              inc=qdbg if dbg_steps else None)
        if dbg_steps:
            S.add("sync", lambda s: s.dma_start(out=dbg_o["scl2_o"], in_=sc),
                  waits=[(qdbg, qdbg.n)], inc=dbgc, amt=16)
        S.add(eng_sched, lambda v: v.drain())
        S.add(eng_sched, lambda v: v.reciprocal(iv, sc),
              waits=[(dbgc, dbgc.n)] if dbg_steps else [],
              inc=qdbg if dbg_steps else None)
        S.add(eng_sched, lambda v: v.drain())
        if dbg_steps:
            S.add("sync", lambda s: s.dma_start(out=dbg_o["inv2_o"], in_=iv),
                  waits=[(qdbg, qdbg.n)], inc=dbgc, amt=16)
            S.add(eng_sched, lambda v: v.drain(), waits=[(dbgc, dbgc.n)])
        S.add(eng_sched, lambda v: v.tensor_tensor(
            yb, src_ap.rearrange("p (b k) -> p b k", k=32),
            iv.broadcast_to([128, n_blk, 32]), ALU.mult),
            inc=qdbg if dbg_steps else None)
        if dbg_steps:
            S.add("sync", lambda s: s.dma_start(out=dbg_o["yq0_o"], in_=yb),
                  waits=[(qdbg, qdbg.n)], inc=dbgc, amt=16)
        S.add(eng_sched, lambda v: v.drain())
        S.add(eng_sched, lambda v: v.tensor_scalar(
            cb.bitcast(I32), yb.bitcast(I32), 0x7F800000, None, ALU.bitwise_and))
        S.add(eng_sched, lambda v: v.drain())
        S.add(eng_sched, lambda v: v.tensor_scalar(cb, cb, 1.0, 24.0, ALU.max, ALU.mult),
              inc=qdbg if dbg_steps else None)
        if dbg_steps:
            S.add("sync", lambda s: s.dma_start(out=dbg_o["cq2_o"], in_=cb),
                  waits=[(qdbg, qdbg.n)], inc=dbgc, amt=16)
        S.add(eng_sched, lambda v: v.drain())
        S.add(eng_sched, lambda v: v.tensor_tensor(wb, yb, cb, ALU.add),
              waits=[(dbgc, dbgc.n)] if dbg_steps else [])
        S.add(eng_sched, lambda v: v.drain())
        S.add(eng_sched, lambda v: v.scalar_tensor_tensor(wb, cb, 1.0 / 384.0, wb, ALU.mult, ALU.add))
        S.add(eng_sched, lambda v: v.drain())
        S.add(eng_sched, lambda v: v.tensor_scalar(
            wb.bitcast(I32), wb.bitcast(I32), -65536, None, ALU.bitwise_and),
            inc=qdbg if dbg_steps else None)
        if dbg_steps:
            S.add("sync", lambda s: s.dma_start(out=dbg_o["wq2_o"], in_=wb),
                  waits=[(qdbg, qdbg.n)], inc=dbgc, amt=16)
        S.add(eng_sched, lambda v: v.drain())
        S.add(eng_sched, lambda v: v.tensor_tensor(yb, wb, cb, ALU.subtract),
              waits=[(dbgc, dbgc.n)] if dbg_steps else [],
              inc=qdbg if dbg_steps else None)
        S.add(eng_sched, lambda v: v.drain())
        if dbg_steps:
            S.add("sync", lambda s: s.dma_start(out=dbg_o["qq2_o"], in_=yb),
                  waits=[(qdbg, qdbg.n)], inc=dbgc, amt=16)
        S.add(eng_sched, lambda v: v.tensor_tensor(
            dst_ap.rearrange("p (b k) -> p b k", k=32),
            yb.rearrange("p (b k) -> p b k", k=32),
            sc.broadcast_to([128, n_blk, 32]), ALU.mult),
            waits=[(dbgc, dbgc.n)] if dbg_steps else [], inc=inc_final)
        if dbg_steps:
            S.add("sync", lambda s: s.dma_start(out=dbg_o["dq2_o"], in_=dst_ap),
                  waits=[(inc_final, inc_final.n)], inc=dbgc, amt=16)

    def w_src_pair(wi):
        if wi < 16:
            return wqkv_d[D + 128 * wi:D + 128 * (wi + 2), :]
        if wi < 24:
            return wqkv_d[128 * (wi - 16):128 * (wi - 14), :]
        return wp_d[128 * (wi - 24):128 * (wi - 22), :]

    def emit_wtr_evac(wi, stage_ap, wst_key, ready):
        sbank = stg_ps if wi % 2 == 0 else stg_ps2
        skey = "stg" if wi % 2 == 0 else "stg2"
        tws = ready + [(cst, N_CST)] + B[skey].wr_waits()

        def mk_tr(stage_ap=stage_ap, sbank=sbank):
            def f(t):
                r = None
                for c in range(8):
                    r = t.transpose(sbank[:, 128 * c:128 * (c + 1)],
                                    stage_ap[:, 128 * c:128 * (c + 1)], ident[:])
                return r
            return f
        S.add("tensor", mk_tr(), waits=tws, inc=wtr, drain_inc=True)
        B[wst_key].did_rd(wtr)
        B[skey].did_wr(wtr)
        dstt, ofw, coff = w_dst(wi)
        dst_ap = dstt[:].rearrange("p (c o) -> p c o", c=8)[:, :, coff:coff + 128]
        src_ap2 = sbank[:].rearrange("p (c i) -> p c i", c=8)

        def mk_ev(dst_ap=dst_ap, src_ap2=src_ap2):
            return lambda s: s.activation(dst_ap, src_ap2, AF.Copy, bias=0.0)
        S.add("scalar", mk_ev(), waits=[(wtr, wtr.n)], inc=wev)
        B[skey].did_rd(wev)

    def emit_wpair(wi):
        pr = (wi // 2) % 2
        wst_key = "wst%d" % pr
        pair_ap = wstage[:, pr * 2 * D:(pr + 1) * 2 * D]
        ws = B[wst_key].wr_waits()
        if wi < 4:
            ws = ws + [(cst, N_CST)]

        def mk_wdma(wi=wi, pair_ap=pair_ap):
            return lambda g: g.dma_start(
                out=pair_ap.rearrange("p (c i) -> p c i", c=2),
                in_=w_src_pair(wi).rearrange("(c p) i -> p c i", p=128))
        wch = wdma if pr == 0 else wdmb
        S.add("gpsimd", mk_wdma(), waits=ws, inc=wch, amt=16)
        B[wst_key].did_wr(wch)
        ready = [(wch, wch.n)]
        for sub in range(2):
            emit_wtr_evac(wi + sub, pair_ap[:, sub * D:(sub + 1) * D], wst_key, ready)

    def emit_ffn_chunk(wi):
        pr = (wi // 2) % 2
        wst_key = "wst%d" % pr
        stage_ap = wstage[:, pr * 2 * D + (wi % 2) * D:pr * 2 * D + (wi % 2 + 1) * D]
        ws = (B["wfst"].wr_waits() + B["sqs0"].wr_waits() + B["sqs1"].wr_waits())

        def mk_wfdma(wi=wi):
            return lambda s: s.dma_start(out=sqs[:], in_=w_src(wi))
        S.add("sync", mk_wfdma(), waits=ws, inc=wfdma, amt=16)
        B["wfst"].did_wr(wfdma)
        B["sqs0"].did_wr(wfdma)
        B["sqs1"].did_wr(wfdma)
        qws = [(wfdma, wfdma.n), (cst, N_CST)] + B[wst_key].wr_waits()
        emit_quant("vector", sqs[:], stage_ap, 32, amx[:], scl[:, 0:32],
                   inv[:], y_q[:], c_q[:], w_q[:], qws, wq_c)
        B["wfst"].did_rd(wq_c)
        B["sqs0"].did_rd(wq_c)
        B["sqs1"].did_rd(wq_c)
        B[wst_key].did_wr(wq_c)
        emit_wtr_evac(wi, stage_ap, wst_key, [(wq_c, wq_c.n)])

    S.add("sync", lambda s: s.dma_start(
        out=x_st[:, 0:TPS * D].rearrange("p (t d) -> p t d", t=TPS),
        in_=x_d[0:ST, :].rearrange("(t p) d -> p t d", p=128)),
        waits=[], inc=x_cp[0], amt=16)
    B["xst0"].did_wr(x_cp[0])
    XC0 = (x_cp[0], x_cp[0].n)

    for k in range(16):
        emit_wpair(2 * k)
        if k % 2 == 1:
            emit_ffn_chunk(32 + k // 2)

    S.add("gpsimd", lambda g: g.dma_start(out=biasp, in_=bp_d.unsqueeze(0)),
          waits=[(lntr, NST * TPS)], inc=bdma, amt=16)
    S.add("gpsimd", lambda g: g.dma_start(out=biasf, in_=bfn_d.unsqueeze(0)),
          waits=[(lntr, NST * TPS)], inc=bdma, amt=16)
    BIAS_WDMA = bdma.n
    WEV_KV = 16
    WEV_Q = 24
    WEV_P = 32
    WEV_ALL = 40

    if dbg:
        S.add("sync", lambda s: s.dma_start(out=dbg_o["wdt_o"], in_=wdT[:]),
              waits=[(wev, WEV_ALL)], inc=dbgc, amt=16)
    for st in range(NST):
        p = st % 2
        xk = "xst%d" % p
        lk = "lnT%d" % p
        x_sl = x_st[:, p * TPS * D:(p + 1) * TPS * D]
        lnT_sl = lnT[:, p * TPS * D:(p + 1) * TPS * D]
        if st == 0:
            XC = XC0
        else:
            xw = B[xk].wr_waits()

            def mk_xl(st=st, x_sl=x_sl):
                return lambda s: s.dma_start(
                    out=x_sl.rearrange("p (t d) -> p t d", t=TPS),
                    in_=x_d[st * ST:(st + 1) * ST, :].rearrange("(t p) d -> p t d", p=128))
            xch = x_cp[st % 2]
            S.add("sync", mk_xl(), waits=xw, inc=xch, amt=16)
            B[xk].did_wr(xch)
            XC = (xch, xch.n)
        def p1_stats(t):
            xt = x_sl[:, t * D:(t + 1) * D]

            def mk_bn(xt=xt):
                return lambda v: v.bn_stats(bn6[:, 0:6], xt[:, 0:512])
            S.add("vector", mk_bn(), waits=[XC])

            def mk_bnb(xt=xt):
                return lambda v: v.bn_stats(bn6[:, 6:12], xt[:, 512:1024])
            S.add("vector", mk_bnb())
            S.add("vector", lambda v: v.drain())

            def mk_ag(t=t):
                return lambda v: v.bn_aggr(mv_s[:, 2 * t:2 * t + 2], bn6[:])
            S.add("vector", mk_ag(), inc=bnv)

            def mk_ln(t=t):
                return lambda s: s.activation(sqv_s[:, t:t + 1],
                                              mv_s[:, 2 * t + 1:2 * t + 2],
                                              AF.Ln, bias=1e-5)
            S.add("scalar", mk_ln(), waits=[(bnv, bnv.n)])
            S.add("scalar", lambda s: s.drain())

            def mk_sq(t=t):
                return lambda s: s.activation(rstd_s[:, t:t + 1], sqv_s[:, t:t + 1],
                                              AF.Exp, scale=-0.5)
            S.add("scalar", mk_sq(), inc=sqc)
        def p1_tile(t):
            tp = t % 2
            lntmk = "lntm%d" % tp
            xt = x_sl[:, t * D:(t + 1) * D]
            ln_ap = ln_tm[:, tp * D:(tp + 1) * D]

            nws = [(sqc, 4 * st + t + 1)] + B[lntmk].wr_waits()

            def mk_nm(t=t, xt=xt, ln_ap=ln_ap):
                return lambda v: v.tensor_scalar(
                    ln_ap, xt, mv_s[:, 2 * t:2 * t + 1], rstd_s[:, t:t + 1],
                    ALU.subtract, ALU.mult)
            S.add("vector", mk_nm(), waits=nws, inc=ln_c)
            B[lntmk].did_wr(ln_c)
            if t == TPS - 1:
                B[xk].did_rd(ln_c)
            if dbg and st == 0 and t == 0:
                S.add("sync", (lambda x_sl=x_sl: lambda s: s.dma_start(
                    out=dbg_o["xsb0_o"], in_=x_sl))(),
                    waits=[XC], inc=dbgc, amt=16)
                B[xk].did_rd(dbgc)
                S.add("sync", (lambda ln_ap=ln_ap: lambda s: s.dma_start(
                    out=dbg_o["lntm0_o"], in_=ln_tm[:, 0:D]))(),
                    waits=[(ln_c, ln_c.n)], inc=dbgc, amt=16)
                B[lntmk].did_rd(dbgc)
                S.add("sync", lambda s: s.dma_start(out=dbg_o["st0_o"], in_=mv_s[:, 0:8]),
                      waits=[(ln_c, ln_c.n)], inc=dbgc, amt=16)
            sbank = stg_ps if t % 2 == 0 else stg_ps2
            skey = "stg" if t % 2 == 0 else "stg2"
            tws = [(ln_c, ln_c.n), (wev, WEV_ALL)] + B[skey].wr_waits()

            def mk_lt(ln_ap=ln_ap, sbank=sbank):
                def f(te):
                    r = None
                    for c in range(8):
                        r = te.transpose(sbank[:, 128 * c:128 * (c + 1)],
                                         ln_ap[:, 128 * c:128 * (c + 1)], ident[:])
                    return r
                return f
            S.add("tensor", mk_lt(), waits=tws, inc=lntr, drain_inc=True)
            B[lntmk].did_rd(lntr)
            B[skey].did_wr(lntr)
            ews = [(lntr, lntr.n)] + B[lk].wr_waits() if t == 0 else [(lntr, lntr.n)]
            dst = lnT_sl.rearrange("p (c i) -> p c i", c=8)[:, :, t * 128:(t + 1) * 128]
            src = sbank[:].rearrange("p (c i) -> p c i", c=8)

            def mk_le(dst=dst, src=src):
                return lambda s: s.activation(dst, src, AF.Copy, bias=0.0)
            S.add("scalar", mk_le(), waits=ews, inc=lnev)
            B[skey].did_rd(lnev)
            if t == 0:
                B[lk].did_wr(lnev)
        for t in range(TPS):
            p1_stats(t)
            if t >= 1:
                p1_tile(t - 1)
        p1_tile(TPS - 1)
        sws = [(lnev, lnev.n)]

        def mk_ls(st=st, lnT_sl=lnT_sl):
            return lambda s: s.dma_start(out=lnt_d[st], in_=lnT_sl)
        sch = lnstp[st % 2]
        S.add("sync", mk_ls(), waits=sws, inc=sch, amt=16)
        B[lk].did_rd(sch)
        if dbg and st == 0:
            S.add("sync", lambda s: s.dma_start(out=dbg_o["lnt0_o"], in_=lnt_d[0]),
                  waits=[(lnstp[0], lnstp[0].n)], inc=dbgc, amt=16)
        for t in range(TPS):
            tp = t % 2
            ktm_ap = ktm[:, tp * D:(tp + 1) * D]
            vtm_ap = vtm[:, tp * D:(tp + 1) * D]
            for ob in (0, 2, 1, 3):
                bank = mmA if ob < 2 else mmB
                bkk = "bkA" if ob < 2 else "bkB"
                gws = [(lnev, 4 * st + t + 1), (wev, WEV_KV)] + B[bkk].wr_waits()

                def mk_g(t=t, ob=ob, bank=bank, lnT_sl=lnT_sl):
                    def f(te):
                        r = None
                        for c in range(8):
                            r = te.matmul(
                                bank[:],
                                lnT_sl[:, c * 512 + t * 128:c * 512 + (t + 1) * 128],
                                wkvT[:, c * 2048 + ob * 512:c * 2048 + (ob + 1) * 512],
                                start=(c == 0), stop=(c == 7))
                        return r
                    return f
                S.add("tensor", mk_g(), waits=gws, inc=kvmm)
                B[bkk].did_wr(kvmm)
                if t == TPS - 1 and ob == 3:
                    B[lk].did_rd(kvmm)
                if ob < 2:
                    sqp = (2 * t + ob) % 2
                    sws2 = [(kvmm, kvmm.n)] + B["sqs%d" % sqp].wr_waits()

                    def mk_s1(bank=bank, sqp=sqp):
                        return lambda s: s.activation(
                            sqs[:, sqp * 512:(sqp + 1) * 512], bank[:], AF.Square)
                    S.add("scalar", mk_s1(), waits=sws2, inc=ksq)
                    S.add("scalar", lambda s: s.drain())
                    B[bkk].did_rd(ksq)
                    B["sqs%d" % sqp].did_wr(ksq)
                    ews2 = B["ktm%d" % tp].wr_waits() if ob == 0 else []

                    def mk_s2(ktm_ap=ktm_ap, ob=ob, sqp=sqp):
                        return lambda s: s.activation(
                            ktm_ap[:, ob * 512:(ob + 1) * 512],
                            sqs[:, sqp * 512:(sqp + 1) * 512], AF.Exp, scale=-0.5)
                    S.add("scalar", mk_s2(), waits=ews2, inc=kexp)
                    B["sqs%d" % sqp].did_rd(kexp)
                    if ob == 0:
                        B["ktm%d" % tp].did_wr(kexp)
                else:
                    vws = [(kvmm, kvmm.n)] + (B["vtm%d" % tp].wr_waits() if ob == 2 else [])

                    def mk_vc(vtm_ap=vtm_ap, ob=ob, bank=bank):
                        return lambda v: v.tensor_copy(
                            vtm_ap[:, (ob - 2) * 512:(ob - 1) * 512], bank[:])
                    S.add("vector", mk_vc(), waits=vws, inc=vcp)
                    B[bkk].did_rd(vcp)
                    if ob == 2:
                        B["vtm%d" % tp].did_wr(vcp)
            kws = [(kexp, kexp.n), (vcp, vcp.n), (wev, WEV_KV)]
            first = (st == 0 and t == 0)
            last = (st == NST - 1 and t == TPS - 1)

            def mk_kv(ktm_ap=ktm_ap, vtm_ap=vtm_ap, first=first, last=last):
                def f(te):
                    for h in range(NH):
                        te.matmul(
                            attA[64 * (h % 2):64 * (h % 2) + 64,
                                 64 * (h // 2):64 * (h // 2) + 64],
                            ktm_ap[:, 64 * h:64 * (h + 1)],
                            vtm_ap[:, 64 * h:64 * (h + 1)],
                            start=(first and h < 2), stop=(last and h >= NH - 2),
                            skip_group_check=True)
                    r = te.matmul(zq[0:1, :], onesc[:], ktm_ap[:, 0:512],
                                  start=first, stop=last, skip_group_check=True)
                    r = te.matmul(attB[0:1, :], onesc[:], ktm_ap[:, 512:1024],
                                  start=first, stop=last, skip_group_check=True)
                    return r
                return f
            S.add("tensor", mk_kv(), waits=kws, inc=kvein)
            B["ktm%d" % tp].did_rd(kvein)
            B["vtm%d" % tp].did_rd(kvein)

    bws = [(kvein, kvein.n), (wtr, 40)]
    S.add("vector", dve_memset(kvb[:, 0:1024], 0.0), waits=bws)
    S.add("vector", lambda v: v.drain())
    S.add("vector", lambda v: v.tensor_copy(
        kvb[0:64, 0:1024].rearrange("p (c s i) -> p c s i", s=2, i=64)[:, :, 0, :],
        attA[0:64, :].rearrange("p (c i) -> p c i", i=64)))
    S.add("vector", lambda v: v.drain())
    S.add("vector", lambda v: v.tensor_copy(
        kvb[64:128, 0:1024].rearrange("p (c s i) -> p c s i", s=2, i=64)[:, :, 1, :],
        attA[64:128, :].rearrange("p (c i) -> p c i", i=64)))
    S.add("vector", lambda v: v.drain())
    S.add("vector", lambda v: v.tensor_copy(attsb[0:1, 0:512], zq[0:1, :]))
    S.add("vector", lambda v: v.tensor_copy(attsb[0:1, 512:1024], attB[0:1, :]), inc=bnd)
    BND = bnd.n
    for c in range(8):
        for j in range(2):
            h = 2 * c + j

            def mk_ka(c=c, j=j, h=h):
                return lambda g: g.dma_start(
                    out=K_all[64 * j:64 * j + 64, 16 * c + h:16 * c + h + 1],
                    in_=attsb[0:1, h * 64:(h + 1) * 64])
            S.add("gpsimd", mk_ka(), waits=[(bnd, BND)], inc=kall, amt=16)
    KALL = kall.n
    if dbg:
        S.add("sync", lambda s: s.dma_start(out=dbg_o["kvx_o"], in_=kvb[:, 0:512]),
              waits=[(bnd, BND)], inc=dbgc, amt=16)
        S.add("sync", lambda s: s.dma_start(out=dbg_o["ksr_o"], in_=attsb[0:1, 0:1024]),
              waits=[(bnd, BND)], inc=dbgc, amt=16)
        S.add("sync", lambda s: s.dma_start(out=dbg_o["kall_o"], in_=K_all[:]),
              waits=[(kall, KALL)], inc=dbgc, amt=16)

    zb2 = pst("zb2", [128, 512], F32)

    def xd_ap(tb):
        return [xd[:, 0:D], ktm[:, 0:D], ktm[:, D:2 * D], vtm[:, 0:D]][tb]
    XDK = ["xd0", "ktm0", "ktm1", "vtm0"]

    p2ctx = {}

    def p2_front(st):
        p = st % 2
        xk = "xst%d" % p
        lk = "lnT%d" % p
        x_sl = x_st[:, p * TPS * D:(p + 1) * TPS * D]
        lnT_sl = lnT[:, p * TPS * D:(p + 1) * TPS * D]
        qf_sl = qfm[:, p * 8 * 512:(p + 1) * 8 * 512]
        lws = B[lk].wr_waits()

        def mk_ll(st=st, lnT_sl=lnT_sl):
            return lambda s: s.dma_start(out=lnT_sl, in_=lnt_d[st])
        lch = lnt2p[st % 2]
        S.add("sync", mk_ll(),
              waits=lws + [(lnstp[0], lnstp[0].n), (lnstp[1], lnstp[1].n)],
              inc=lch, amt=16)
        B[lk].did_wr(lch)
        LNT2 = (lch, lch.n)
        xw = B[xk].wr_waits()

        def mk_xl2(st=st, x_sl=x_sl):
            return lambda s: s.dma_start(
                out=x_sl.rearrange("p (t d) -> p t d", t=TPS),
                in_=x_d[st * ST:(st + 1) * ST, :].rearrange("(t p) d -> p t d", p=128))
        xch = x_cp[st % 2]
        S.add("sync", mk_xl2(), waits=xw, inc=xch, amt=16)
        B[xk].did_wr(xch)
        XC = (xch, xch.n)
        p2ctx[st] = {"p": p, "xk": xk, "x_sl": x_sl, "qf_sl": qf_sl, "XC": XC,
                     "qnq_at": {}, "sq2_at": {}}
        for oc in range(8):
            bank = mmA if oc % 2 == 0 else mmB
            bkk = "bkA" if oc % 2 == 0 else "bkB"
            gws = [LNT2, (wev, WEV_Q)] + B[bkk].wr_waits()
            qfw = B["qfm%d" % p].wr_waits() if oc == 0 else []

            def mk_q(oc=oc, bank=bank, lnT_sl=lnT_sl):
                def f(te):
                    r = None
                    for c in range(8):
                        r = te.matmul(
                            bank[:],
                            wqT[:, c * 1024 + oc * 128:c * 1024 + (oc + 1) * 128],
                            lnT_sl[:, c * 512:(c + 1) * 512],
                            start=(c == 0), stop=(c == 7))
                    return r
                return f
            S.add("tensor", mk_q(), waits=gws, inc=qmm)
            B[bkk].did_wr(qmm)
            if oc == 7:
                B[lk].did_rd(qmm)
            sqp = oc % 2
            sws2 = [(qmm, qmm.n)] + B["sqs%d" % sqp].wr_waits()

            def mk_qs(bank=bank, sqp=sqp):
                return lambda s: s.activation(
                    sqs[:, sqp * 512:(sqp + 1) * 512], bank[:], AF.Square)
            S.add("scalar", mk_qs(), waits=sws2, inc=qsq)
            S.add("scalar", lambda s: s.drain())
            B[bkk].did_rd(qsq)
            B["sqs%d" % sqp].did_wr(qsq)

            def mk_qe(oc=oc, sqp=sqp, qf_sl=qf_sl):
                return lambda s: s.activation(
                    qf_sl[:, oc * 512:(oc + 1) * 512],
                    sqs[:, sqp * 512:(sqp + 1) * 512], AF.Exp, scale=-0.5)
            S.add("scalar", mk_qe(), waits=qfw, inc=qexp)
            B["sqs%d" % sqp].did_rd(qexp)
            if oc == 0:
                B["qfm%d" % p].did_wr(qexp)
        qdw = [(qexp, qexp.n), (kall, KALL)] + B["bkzq"].wr_waits()

        def mk_qd(qf_sl=qf_sl):
            def f(te):
                r = None
                for c in range(8):
                    r = te.matmul(zq[0:16, :], K_all[:, 16 * c:16 * (c + 1)],
                                  qf_sl[:, c * 512:(c + 1) * 512],
                                  start=(c == 0), stop=(c == 7))
                return r
            return f
        S.add("tensor", mk_qd(), waits=qdw, inc=qdot)
        B["bkzq"].did_wr(qdot)
        B["qfm%d" % p].did_rd(qdot)
        zws = [(qdot, qdot.n)] + B["zsb"].wr_waits() + B["sqs0"].wr_waits()
        S.add("scalar", lambda s: s.activation(sqs[0:16, 0:512], zq[0:16, :],
                                               AF.Ln, bias=1e-6), waits=zws)
        S.add("scalar", lambda s: s.drain())
        S.add("scalar", lambda s: s.activation(zsb[:], sqs[0:16, 0:512],
                                               AF.Exp, scale=-1.0), inc=zact)
        B["bkzq"].did_rd(zact)
        B["sqs0"].did_rd(zact)
        B["zsb"].did_wr(zact)

    def p2_attproj(st):
        ctx = p2ctx[st]
        p = ctx["p"]; x_sl = ctx["x_sl"]; qf_sl = ctx["qf_sl"]; XC = ctx["XC"]
        def emit_att(c):
            bank = attA if c % 2 == 0 else attB
            bkk = "bkattA" if c % 2 == 0 else "bkattB"
            aws = [(qz_c, qz_c.n), (bnd, BND)] + B[bkk].wr_waits()

            def mk_at(c=c, bank=bank, qf_sl=qf_sl):
                return lambda te: te.matmul(
                    bank[:],
                    kvb[:, 128 * c:128 * (c + 1)],
                    qf_sl[:, c * 512:(c + 1) * 512],
                    start=True, stop=True)
            S.add("tensor", mk_at(), waits=aws, inc=attmm)
            B[bkk].did_wr(attmm)
            if c == 7:
                B["qfm%d" % p].did_rd(attmm)
                B["zsb"].did_rd(attmm)
            ew = [(attmm, attmm.n)] + (B["attsb"].wr_waits() if c == 0 else [])
            if st == 0 and c <= 1:
                ew = ew + [(kall, KALL)]

            def mk_ae(c=c, bank=bank):
                return lambda s: s.activation(
                    attsb[:, c * 512:(c + 1) * 512],
                    bank[:], AF.Copy, bias=0.0)
            S.add("scalar", mk_ae(), waits=ew, inc=attev)
            B[bkk].did_rd(attev)
            if c == 0:
                B["attsb"].did_wr(attev)

        for c in range(8):
            zbank = zb2 if c % 2 == 0 else zq
            zbkk = "bkzb2" if c % 2 == 0 else "bkzq"
            zw = [(zact, zact.n)] + B[zbkk].wr_waits()
            if st == 0 and c <= 1:
                zw = zw + [(seldma, seldma.n)]

            def mk_zb(c=c, zbank=zbank):
                return lambda te: te.matmul(zbank[:], sel_all[:, 128 * c:128 * (c + 1)],
                                            zsb[:], start=True, stop=True)
            S.add("tensor", mk_zb(), waits=zw, inc=zbcmm)
            B[zbkk].did_wr(zbcmm)
            qzw = [(zbcmm, zbcmm.n), (qexp, qexp.n)]

            def mk_qz(c=c, zbank=zbank, qf_sl=qf_sl):
                return lambda v: v.tensor_tensor(
                    qf_sl[:, c * 512:(c + 1) * 512], qf_sl[:, c * 512:(c + 1) * 512],
                    zbank[:], ALU.mult)
            S.add("vector", mk_qz(), waits=qzw, inc=qz_c)
            B[zbkk].did_rd(qz_c)
            if c >= 1:
                emit_att(c - 1)
        emit_att(7)
        for tb in range(TPS):
            for ocb in range(2):
                g = 2 * tb + ocb
                bank = mmA if g % 2 == 0 else mmB
                bkk = "bkA" if g % 2 == 0 else "bkB"
                gws = [(attev, attev.n), (wev, WEV_P), (bdma, BIAS_WDMA)] + B[bkk].wr_waits()

                def mk_pj(tb=tb, ocb=ocb, bank=bank):
                    def f(te):
                        for c in range(8):
                            te.matmul(
                                bank[:],
                                attsb[:, c * 512 + tb * 128:c * 512 + (tb + 1) * 128],
                                wpT[:, c * 1024 + ocb * 512:c * 1024 + (ocb + 1) * 512],
                                start=(c == 0), stop=False)
                        return te.matmul(bank[:], onesr[:],
                                         biasp[:, ocb * 512:(ocb + 1) * 512],
                                         start=False, stop=True)
                    return f
                S.add("tensor", mk_pj(), waits=gws, inc=pjmm)
                B[bkk].did_wr(pjmm)
                if tb == TPS - 1 and ocb == 1:
                    B["attsb"].did_rd(pjmm)
                rws = [(pjmm, pjmm.n), XC]

                def mk_r1(tb=tb, ocb=ocb, bank=bank, x_sl=x_sl):
                    sl = x_sl[:, tb * D + ocb * 512:tb * D + (ocb + 1) * 512]
                    return lambda v: v.tensor_tensor(sl, bank[:], sl, ALU.add)
                S.add("vector", mk_r1(), waits=rws, inc=rs1)
                B[bkk].did_rd(rs1)
            tp = tb % 2
            xt = x_sl[:, tb * D:(tb + 1) * D]

            def mk_bn2(xt=xt):
                return lambda v: v.bn_stats(bn6[:, 0:6], xt[:, 0:512])
            S.add("vector", mk_bn2())

            def mk_bn2b(xt=xt):
                return lambda v: v.bn_stats(bn6[:, 6:12], xt[:, 512:1024])
            S.add("vector", mk_bn2b())
            S.add("vector", lambda v: v.drain())

            tq_ = 4 * (st % 2) + tb

            def mk_ag2(tq_=tq_):
                return lambda v: v.bn_aggr(mv_s[:, 2 * tq_:2 * tq_ + 2], bn6[:])
            S.add("vector", mk_ag2(), inc=bnv2)

            def mk_ln2(tq_=tq_):
                return lambda s: s.activation(sqv_s[:, tq_:tq_ + 1],
                                              mv_s[:, 2 * tq_ + 1:2 * tq_ + 2],
                                              AF.Ln, bias=1e-5)
            S.add("scalar", mk_ln2(), waits=[(bnv2, bnv2.n)])
            S.add("scalar", lambda s: s.drain())

            def mk_sq2(tq_=tq_):
                return lambda s: s.activation(rstd_s[:, tq_:tq_ + 1], sqv_s[:, tq_:tq_ + 1],
                                              AF.Exp, scale=-0.5)
            S.add("scalar", mk_sq2(), inc=sq2)
            ctx["sq2_at"][tb] = sq2.n
            if tb >= 1:
                p2_quant_tile(st, tb - 1)

    def p2_quant_tile(st, tb):
        ctx = p2ctx[st]
        p = ctx["p"]; x_sl = ctx["x_sl"]; XC = ctx["XC"]
        if True:
            xt = x_sl[:, tb * D:(tb + 1) * D]
            xdk = XDK[tb]
            xda_t = xd_ap(tb)
            l2 = zbc[:, 0:D]
            yb = y_q[:, 0:D]
            cb = c_q[:, 0:D]
            wb = w_q[:, 0:D]
            am = amx[:, 0:32]
            sc = scl[:, 32 * (tb % 2):32 * (tb % 2) + 32]
            iv = inv[:, 0:32]
            fws = B["ln2h0"].wr_waits()

            tq_ = 4 * (st % 2) + tb

            def mk_n2(xt=xt, l2=l2, tq_=tq_):
                return lambda v: v.tensor_scalar(
                    l2, xt, mv_s[:, 2 * tq_:2 * tq_ + 1], rstd_s[:, tq_:tq_ + 1],
                    ALU.subtract, ALU.mult)
            S.add("vector", mk_n2(), waits=fws + [(sq2, ctx["sq2_at"][tb])])
            S.add("vector", lambda v: v.drain())

            def mk_am(l2=l2, am=am):
                return lambda v: v.tensor_reduce(
                    am, l2.rearrange("p (b k) -> p b k", k=32), AX.X, ALU.max,
                    apply_absolute_value=True)
            S.add("vector", mk_am())
            S.add("vector", lambda v: v.drain())

            def mk_sc(am=am, sc=sc):
                return lambda v: v.tensor_scalar(sc, am, 1.0 / 7.5, 1e-30,
                                                 ALU.mult, ALU.max)
            S.add("vector", mk_sc())
            S.add("vector", lambda v: v.drain())

            def mk_iv(sc=sc, iv=iv):
                return lambda v: v.reciprocal(iv, sc)
            S.add("vector", mk_iv())
            S.add("vector", lambda v: v.drain())
            yws = B["y0"].wr_waits() + B["y1"].wr_waits()

            def mk_y(l2=l2, iv=iv, yb=yb):
                return lambda v: v.tensor_tensor(
                    yb, l2.rearrange("p (b k) -> p b k", k=32),
                    iv.broadcast_to([128, 32, 32]), ALU.mult)
            S.add("vector", mk_y(), waits=yws)
            S.add("vector", lambda v: v.drain())
            cws = B["c0"].wr_waits() + B["c1"].wr_waits()

            def mk_c1(yb=yb, cb=cb):
                return lambda v: v.tensor_scalar(
                    cb.bitcast(I32), yb.bitcast(I32), 0x7F800000, None,
                    ALU.bitwise_and)
            S.add("vector", mk_c1(), waits=cws)
            S.add("vector", lambda v: v.drain())

            def mk_c2(cb=cb):
                return lambda v: v.tensor_scalar(cb, cb, 1.0, 24.0, ALU.max, ALU.mult)
            S.add("vector", mk_c2())
            S.add("vector", lambda v: v.drain())

            def mk_yofs(yb=yb, cb=cb):
                return lambda v: v.scalar_tensor_tensor(yb, cb, 1.0 / 384.0, yb,
                                                        ALU.mult, ALU.add)
            S.add("vector", mk_yofs(), inc=qna)
            B["ln2h0"].did_wr(qna)
            B["y0"].did_wr(qna)
            B["y1"].did_wr(qna)
            B["c0"].did_wr(qna)
            B["c1"].did_wr(qna)
            pws = [(qna, qna.n)] + B["w0"].wr_waits() + B["w1"].wr_waits()

            def mk_w1(yb=yb, cb=cb, wb=wb):
                return lambda g: g.tensor_tensor(wb, yb, cb, ALU.add)
            S.add("gpsimd", lambda g: g.drain())
            S.add("gpsimd", mk_w1(), waits=pws, inc=qnw)
            B["w0"].did_wr(qnw)
            B["w1"].did_wr(qnw)
            S.add("vector", (lambda wb=wb: (lambda v: v.tensor_scalar(
                wb.bitcast(I32), wb.bitcast(I32), -65536, None,
                ALU.bitwise_and)))(), waits=[(qnw, qnw.n)], inc=qnr)
            S.add("vector", lambda v: v.drain())
            dws = [(qnr, qnr.n)] + B[xdk].wr_waits()

            def mk_q1(yb=yb, cb=cb, wb=wb):
                return lambda g: g.tensor_tensor(yb, wb, cb, ALU.subtract)
            S.add("gpsimd", mk_q1(), waits=dws)
            S.add("gpsimd", lambda g: g.drain())

            def mk_dq(yb=yb, sc=sc, xda_t=xda_t):
                return lambda g: g.tensor_tensor(
                    xda_t.rearrange("p (b k) -> p b k", k=32),
                    yb.rearrange("p (b k) -> p b k", k=32),
                    sc.broadcast_to([128, 32, 32]), ALU.mult)
            S.add("gpsimd", mk_dq(), inc=qnq)
            B["y0"].did_rd(qnq)
            B["y1"].did_rd(qnq)
            B["c0"].did_rd(qnq)
            B["c1"].did_rd(qnq)
            B["w0"].did_rd(qnq)
            B["w1"].did_rd(qnq)
            B[xdk].did_wr(qnq)
            ctx["qnq_at"][tb] = qnq.n

    def p2_ffn(st):
        ctx = p2ctx[st]
        p = ctx["p"]; x_sl = ctx["x_sl"]; xk = ctx["xk"]
        for tb in range(TPS):
            xdk = XDK[tb]
            xda_t = xd_ap(tb)
            sbank = stg_ps if tb % 2 == 0 else stg_ps2
            skey = "stg" if tb % 2 == 0 else "stg2"
            xtw = [(qnq, ctx["qnq_at"][tb])] + B[skey].wr_waits()

            def mk_xt(xda_t=xda_t, sbank=sbank):
                def f(te):
                    r = None
                    for c in range(8):
                        r = te.transpose(sbank[:, 128 * c:128 * (c + 1)],
                                         xda_t[:, 128 * c:128 * (c + 1)], ident[:])
                    return r
                return f
            S.add("tensor", mk_xt(), waits=xtw, inc=xdtr, drain_inc=True)
            B[xdk].did_rd(xdtr)
            B[skey].did_wr(xdtr)
            xk2 = "xdT0"
            xdT_sl = xdT[:, 0:TPS * D]
            ews3 = [(xdtr, xdtr.n)] + (B[xk2].wr_waits() if tb == 0 else [])
            dst3 = xdT_sl.rearrange("p (c i) -> p c i", c=8)[:, :, tb * 128:(tb + 1) * 128]
            src3 = sbank[:].rearrange("p (c i) -> p c i", c=8)

            def mk_xe(dst3=dst3, src3=src3):
                return lambda s: s.activation(dst3, src3, AF.Copy, bias=0.0)
            S.add("scalar", mk_xe(), waits=ews3, inc=xdev)
            B[skey].did_rd(xdev)
            if tb == 0:
                B[xk2].did_wr(xdev)
            for ocb in range(2):
                g = 2 * tb + ocb
                bank = mmA if g % 2 == 0 else mmB
                bkk = "bkA" if g % 2 == 0 else "bkB"
                gws = [(xdev, xdev.n), (wev, WEV_ALL)] + B[bkk].wr_waits()

                def mk_ff(tb=tb, ocb=ocb, bank=bank):
                    def f(te):
                        for c in range(8):
                            te.matmul(
                                bank[:],
                                xdT[:, c * 512 + tb * 128:c * 512 + (tb + 1) * 128],
                                wdT[:, c * 1024 + ocb * 512:c * 1024 + (ocb + 1) * 512],
                                start=(c == 0), stop=False)
                        return te.matmul(bank[:], onesr[:],
                                         biasf[:, ocb * 512:(ocb + 1) * 512],
                                         start=False, stop=True)
                    return f
                S.add("tensor", mk_ff(), waits=gws, inc=ffmm)
                B[bkk].did_wr(ffmm)
                if tb == TPS - 1 and ocb == 1:
                    B["xdT0"].did_rd(ffmm)

                def mk_r2(tb=tb, ocb=ocb, bank=bank, x_sl=x_sl):
                    sl = x_sl[:, tb * D + ocb * 512:tb * D + (ocb + 1) * 512]
                    return lambda v: v.tensor_tensor(sl, bank[:], sl, ALU.add)
                S.add("vector", mk_r2(), waits=[(ffmm, ffmm.n)], inc=rs2)
                B[bkk].did_rd(rs2)
        ows = [(rs2, rs2.n)]

        def mk_os(st=st, x_sl=x_sl):
            return lambda g: g.dma_start(
                out=out_d[st * ST:(st + 1) * ST, :].rearrange("(t p) d -> p t d", p=128),
                in_=x_sl.rearrange("p (t d) -> p t d", t=TPS))
        och = outp[st % 2]
        S.add("gpsimd", mk_os(), waits=ows, inc=och, amt=16)
        B[xk].did_rd(och)

    if 2 in phases:
        for st in range(NST):
            p2_front(st)
            if st >= 1:
                p2_ffn(st - 1)
            p2_attproj(st)
            p2_quant_tile(st, TPS - 1)
        p2_ffn(NST - 1)

    S.add("gpsimd", lambda g: g.engine_nop(),
          waits=[(outp[0], outp[0].n), (outp[1], outp[1].n)] +
          ([(dbgc, dbgc.n)] if dbg else []))

    sems = ExitStack()
    for c in S.chans:
        c.sem = sems.enter_context(nc.semaphore(c.name))
    with nc.Block() as block:
        def emit(eng_name):
            drain_engs = ("vector", "scalar", "gpsimd")

            def body(e):
                for waits, f, incc, amt, dinc in S.ops[eng_name]:
                    for c, v in waits:
                        e.wait_ge(c.sem, v)
                    r = f(e)
                    if incc is not None:
                        if (eng_name in drain_engs and amt == 1) or dinc:
                            e.drain().then_inc(incc.sem, amt)
                        else:
                            r.then_inc(incc.sem, amt)
            return body
        block.sync(emit("sync"))
        block.tensor(emit("tensor"))
        block.vector(emit("vector"))
        block.scalar(emit("scalar"))
        block.gpsimd(emit("gpsimd"))
    sems.close()
    ctx.close()
    return nc


_NC_CACHE = {}


def kernel(L0_attn_x, L0_attn_W_qkv, L0_attn_W_proj, L0_attn_b_proj,
           L0_ffn_weight, L0_ffn_bias):
    x = np.ascontiguousarray(np.asarray(L0_attn_x, dtype=np.float32))
    wqkv = np.ascontiguousarray(np.asarray(L0_attn_W_qkv, dtype=np.float32))
    wp = np.ascontiguousarray(np.asarray(L0_attn_W_proj, dtype=np.float32))
    bp = np.ascontiguousarray(np.asarray(L0_attn_b_proj, dtype=np.float32))
    wf = np.ascontiguousarray(np.asarray(L0_ffn_weight, dtype=np.float32))
    bf = np.ascontiguousarray(np.asarray(L0_ffn_bias, dtype=np.float32))

    if "nc" not in _NC_CACHE:
        _NC_CACHE["nc"] = build_nc(dbg=False)
    nc = _NC_CACHE["nc"]

    in_maps = []
    for c in range(8):
        in_maps.append({
            "x": np.ascontiguousarray(x[c]),
            "wqkv": wqkv, "wproj": wp, "bproj": bp,
            "wffn": wf, "bffn": bf,
        })
    res = run_bass_kernel_spmd(nc, in_maps, core_ids=list(range(8)))
    out = np.stack([np.asarray(r["out"]) for r in res.results], axis=0)
    return out.astype(np.float32)


# version 41 (final)
# speedup vs baseline: 1.2366x; 1.0106x over previous
import numpy as np
from contextlib import ExitStack

import concourse.bass as bass
import concourse.mybir as mybir
from concourse.bass_utils import run_bass_kernel_spmd

F32 = mybir.dt.float32
F32R = mybir.dt.float32r
BF16 = mybir.dt.bfloat16
I32 = mybir.dt.int32
AF = mybir.ActivationFunctionType
ALU = mybir.AluOpType
AX = mybir.AxisListType

D = 1024
NTOK = 4096
NH = 16
DH = 64
ST = 512
NST = NTOK // ST
TPS = ST // 128
NB = D // 32

ENGS = ("sync", "tensor", "vector", "scalar", "gpsimd")


class Ch:
    __slots__ = ("name", "n", "sem")

    def __init__(self, name):
        self.name = name
        self.n = 0
        self.sem = None


class Buf:
    __slots__ = ("w", "readers")

    def __init__(self):
        self.w = None
        self.readers = []

    def rd_waits(self):
        return [self.w] if self.w else []

    def did_rd(self, ch):
        self.readers.append((ch, ch.n))

    def wr_waits(self):
        ws = list(self.readers)
        if self.w:
            ws.append(self.w)
        return ws

    def did_wr(self, ch):
        self.w = (ch, ch.n)
        self.readers = []


class Sched:
    def __init__(self):
        self.ops = {e: [] for e in ENGS}
        self.chans = []

    def ch(self, name):
        c = Ch(name)
        self.chans.append(c)
        return c

    def add(self, eng, emit, waits=(), inc=None, amt=1, drain_inc=False):
        m = {}
        for c, v in waits:
            if v > 0 and (c not in m or v > m[c]):
                m[c] = v
        self.ops[eng].append((sorted(m.items(), key=lambda kv: kv[0].name), emit, inc,
                              amt, drain_inc))
        if inc is not None:
            inc.n += amt
        return (inc, inc.n if inc else 0)


def build_nc(dbg=False, phases=(1, 2)):
    nc = bass.Bass("TRN2", target_bir_lowering=False)

    x_d = nc.dram_tensor("x", [NTOK, D], F32, kind="ExternalInput").ap()
    wqkv_d = nc.dram_tensor("wqkv", [3 * D, D], F32, kind="ExternalInput").ap()
    wp_d = nc.dram_tensor("wproj", [D, D], F32, kind="ExternalInput").ap()
    bp_d = nc.dram_tensor("bproj", [D], F32, kind="ExternalInput").ap()
    wf_d = nc.dram_tensor("wffn", [D, D], F32, kind="ExternalInput").ap()
    bfn_d = nc.dram_tensor("bffn", [D], F32, kind="ExternalInput").ap()
    out_d = nc.dram_tensor("out", [NTOK, D], F32, kind="ExternalOutput").ap()
    lnt_d = nc.dram_tensor("lnt_scratch", [NST, 128, TPS * D], BF16, kind="Internal").ap()

    dbg_o = {}
    if dbg:
        dbg_o["kvx_o"] = nc.dram_tensor("kvx_o", [128, 512], BF16, kind="ExternalOutput").ap()
        dbg_o["ksr_o"] = nc.dram_tensor("ksr_o", [1, 1024], BF16, kind="ExternalOutput").ap()
        dbg_o["kall_o"] = nc.dram_tensor("kall_o", [128, 128], BF16, kind="ExternalOutput").ap()
        dbg_o["lnt0_o"] = nc.dram_tensor("lnt0_o", [128, TPS * D], BF16, kind="ExternalOutput").ap()
        dbg_o["qfm0_o"] = nc.dram_tensor("qfm0_o", [128, 8 * 512], BF16, kind="ExternalOutput").ap()
        dbg_o["z0_o"] = nc.dram_tensor("z0_o", [16, 512], BF16, kind="ExternalOutput").ap()
        dbg_o["att0_o"] = nc.dram_tensor("att0_o", [128, 8 * 512], BF16, kind="ExternalOutput").ap()
        dbg_o["xmid0_o"] = nc.dram_tensor("xmid0_o", [128, D], F32, kind="ExternalOutput").ap()
        dbg_o["xsb0_o"] = nc.dram_tensor("xsb0_o", [128, TPS * D], F32, kind="ExternalOutput").ap()
        dbg_o["lntm0_o"] = nc.dram_tensor("lntm0_o", [128, D], BF16, kind="ExternalOutput").ap()
        dbg_o["st0_o"] = nc.dram_tensor("st0_o", [128, 8], F32, kind="ExternalOutput").ap()
        dbg_o["wdt_o"] = nc.dram_tensor("wdt_o", [128, 8 * 1024], BF16, kind="ExternalOutput").ap()
        dbg_o["xd0_o"] = nc.dram_tensor("xd0_o", [128, D], BF16, kind="ExternalOutput").ap()
        dbg_o["wst_o"] = nc.dram_tensor("wst_o", [128, D], F32, kind="ExternalOutput").ap()
        dbg_o["wyq_o"] = nc.dram_tensor("wyq_o", [128, D], F32, kind="ExternalOutput").ap()
        dbg_o["wcq_o"] = nc.dram_tensor("wcq_o", [128, D], F32, kind="ExternalOutput").ap()
        dbg_o["wscl_o"] = nc.dram_tensor("wscl_o", [128, 32], F32, kind="ExternalOutput").ap()
        dbg_o["wbf_o"] = nc.dram_tensor("wbf_o", [128, D], BF16, kind="ExternalOutput").ap()
        dbg_o["wst2_o"] = nc.dram_tensor("wst2_o", [128, D], F32, kind="ExternalOutput").ap()
        dbg_o["wbf33_o"] = nc.dram_tensor("wbf33_o", [128, D], BF16, kind="ExternalOutput").ap()
        dbg_o["stg33_o"] = nc.dram_tensor("stg33_o", [128, D], F32, kind="ExternalOutput").ap()
        dbg_o["amx_o"] = nc.dram_tensor("amx_o", [128, 32], F32, kind="ExternalOutput").ap()
        dbg_o["yq0_o"] = nc.dram_tensor("yq0_o", [128, D], F32, kind="ExternalOutput").ap()
        dbg_o["scl2_o"] = nc.dram_tensor("scl2_o", [128, 32], F32, kind="ExternalOutput").ap()
        dbg_o["inv2_o"] = nc.dram_tensor("inv2_o", [128, 32], F32, kind="ExternalOutput").ap()
        dbg_o["cq2_o"] = nc.dram_tensor("cq2_o", [128, D], F32, kind="ExternalOutput").ap()
        dbg_o["wq2_o"] = nc.dram_tensor("wq2_o", [128, D], F32, kind="ExternalOutput").ap()
        dbg_o["qq2_o"] = nc.dram_tensor("qq2_o", [128, D], F32, kind="ExternalOutput").ap()
        dbg_o["dq2_o"] = nc.dram_tensor("dq2_o", [128, D], BF16, kind="ExternalOutput").ap()

    ctx = ExitStack()
    sbt = lambda name, shape, dt: ctx.enter_context(nc.sbuf_tensor(name, shape, dt))
    pst = lambda name, shape, dt: ctx.enter_context(nc.psum_tensor(name, shape, dt))

    wkvT = sbt("wkvT", [128, 8 * 2048], BF16)
    wqT = sbt("wqT", [128, 8 * 1024], BF16)
    wpT = sbt("wpT", [128, 8 * 1024], BF16)
    wdT = sbt("wdT", [128, 8 * 1024], BF16)
    x_st = sbt("x_st", [128, 2 * TPS * D], F32)
    lnT = sbt("lnT", [128, 2 * TPS * D], BF16)
    ln_tm = sbt("ln_tm", [128, 2 * D], BF16)
    ktm = sbt("ktm", [128, 2 * D], BF16)
    vtm = sbt("vtm", [128, 2 * D], BF16)
    sqs = sbt("sqs", [128, 2 * 512], F32)
    qfm = sbt("qfm", [128, 2 * 8 * 512], BF16)
    zsb = sbt("zsb", [16, 512], BF16)
    zbc = sbt("zbc", [128, 2 * 512], F32)
    attsb = sbt("attsb", [128, 8 * 512], BF16)
    y_q = sbt("y_q", [128, 2 * 512], F32)
    c_q = sbt("c_q", [128, 2 * 512], F32)
    w_q = sbt("w_q", [128, 2 * 512], F32)
    xd = sbt("xd", [128, D], BF16)
    xdT = sbt("xdT", [128, TPS * D], BF16)
    wstage = sbt("wstage", [128, 2 * 2 * D], BF16)
    amx = sbt("amx", [128, 2 * 16], F32)
    scl = sbt("scl", [128, 4 * 16], F32)
    inv = sbt("inv", [128, 2 * 16], F32)
    K_all = sbt("K_all", [128, 128], BF16)
    sel_all = sbt("sel_all", [16, 8 * 128], BF16)
    ident = sbt("ident", [128, 128], BF16)
    onesr = sbt("onesr", [1, 128], BF16)
    onesc = sbt("onesc", [128, 1], BF16)
    ones64r = sbt("ones64r", [1, 64], BF16)
    eps6 = sbt("eps6", [128, 1], F32)
    eps5 = sbt("eps5", [128, 1], F32)
    mv_s = sbt("mv_s", [128, 8 * 2], F32)
    ve_s = sbt("ve_s", [128, 8], F32)
    sqv_s = sbt("sqv_s", [128, 8], F32)
    rstd_s = sbt("rstd_s", [128, 8], F32)
    bn6 = sbt("bn6", [128, 12], F32)

    stg_ps = pst("stg_ps", [128, 1024], BF16)
    stg_ps2 = pst("stg_ps2", [128, 1024], BF16)
    mmA = pst("mmA", [128, 512], F32)
    mmB = pst("mmB", [128, 512], F32)
    attA = pst("attA", [128, 512], F32)
    attB = pst("attB", [128, 512], F32)
    zq = pst("zq", [128, 512], F32)

    nc.const_aps.register(F32, 1e-6, eps6[:])
    nc.const_aps.register(F32, 1e-5, eps5[:])

    S = Sched()
    cst = S.ch("cst")
    iot = S.ch("iot")
    seldma = S.ch("seldma")
    wdma = S.ch("wdma")
    wdmb = S.ch("wdmb")
    bdma = S.ch("bdma")
    wfdma = S.ch("wfdma")
    wq_c = S.ch("wq")
    wtr = S.ch("wtr")
    wev = S.ch("wev")
    x_cp = [S.ch("x0"), S.ch("x1")]
    bnv = S.ch("bnv")
    sqc = S.ch("sqc")
    ln_c = S.ch("ln")
    lntr = S.ch("lntr")
    lnev = S.ch("lnev")
    lnstp = [S.ch("lnsta"), S.ch("lnstb")]
    kvmm = S.ch("kvmm")
    ksq = S.ch("ksq")
    kexp = S.ch("kexp")
    vcp = S.ch("vcp")
    kvein = S.ch("kvein")
    bnd = S.ch("bnd")
    kall = S.ch("kall")
    lnt2p = [S.ch("lnt2a"), S.ch("lnt2b")]
    qmm = S.ch("qmm")
    qsq = S.ch("qsq")
    qexp = S.ch("qexp")
    qdot = S.ch("qdot")
    zact = S.ch("zact")
    zbcmm = S.ch("zbcmm")
    zbcev = S.ch("zbcev")
    qz_c = S.ch("qz")
    attmm = S.ch("attmm")
    attev = S.ch("attev")
    pjmm = S.ch("pjmm")
    rs1 = S.ch("rs1")
    bnv2 = S.ch("bnv2")
    sq2 = S.ch("sq2")
    qna = S.ch("qna")
    qnw = S.ch("qnw")
    qnr = S.ch("qnr")
    qnq = S.ch("qnq")
    xdtr = S.ch("xdtr")
    xdev = S.ch("xdev")
    ffmm = S.ch("ffmm")
    rs2 = S.ch("rs2")
    outp = [S.ch("outsta"), S.ch("outstb")]
    dbgc = S.ch("dbgc")
    qdbg = S.ch("qdbg")

    B = {}
    for k in ["xst0", "xst1", "lnT0", "lnT1", "lntm0", "lntm1", "ktm0", "ktm1",
              "vtm0", "vtm1", "qfm0", "qfm1", "zsb", "zbc0", "zbc1", "attsb", "bkzb2", "stg2",
              "y0", "y1", "c0", "c1", "w0", "w1", "ln2h0", "ln2h1", "xd0", "xd1",
              "xdT0", "xdT1", "wst0", "wst1", "stg", "bkA", "bkB", "bkattA",
              "bkattB", "bkzq", "kvx", "ksr", "K_all", "sel", "wfst",
              "mv0", "mv1", "mv2", "mv3", "sqs0", "sqs1"]:
        B[k] = Buf()

    def dve_memset(ap, val):
        return lambda v: v.memset(ap, val)

    S.add("vector", dve_memset(eps6[:], 1e-6), inc=cst)
    S.add("vector", dve_memset(eps5[:], 1e-5), inc=cst)
    S.add("vector", dve_memset(K_all[:], 0.0), inc=cst)
    S.add("vector", dve_memset(sel_all[:].bitcast(mybir.dt.uint16), 0), inc=cst)
    S.add("vector", dve_memset(onesr[:].bitcast(mybir.dt.uint16), 0x3F80), inc=cst)
    S.add("vector", dve_memset(onesc[:].bitcast(mybir.dt.uint16), 0x3F80), inc=cst)
    S.add("vector", dve_memset(ones64r[:].bitcast(mybir.dt.uint16), 0x3F80), inc=cst)
    S.add("gpsimd", lambda g: g.iota(ident[:].bitcast(mybir.dt.int16), [[-1, 128]],
                                     base=0, channel_multiplier=1,
                                     allow_small_or_imprecise_dtypes=True), inc=iot)
    S.add("vector", lambda v: v.tensor_scalar(ident[:].bitcast(mybir.dt.int16),
                                              ident[:].bitcast(mybir.dt.int16),
                                              0, 0x3F80, ALU.is_equal, ALU.mult),
          waits=[(iot, 1)], inc=cst)
    N_CST = cst.n
    for c in range(8):
        for j in range(2):
            h = 2 * c + j

            def mk_sel(c=c, j=j, h=h):
                return lambda g: g.dma_start(
                    out=sel_all[h:h + 1, 128 * c + 64 * j:128 * c + 64 * j + 64],
                    in_=ones64r[0:1, :])
            S.add("gpsimd", mk_sel(), waits=[(cst, N_CST)], inc=seldma, amt=16)
    biasp = ln_tm[0:1, 0:D]
    biasf = ln_tm[0:1, D:2 * D]
    kvb = wstage

    def w_src(wi):
        if wi < 16:
            return wqkv_d[D + 128 * wi:D + 128 * (wi + 1), :]
        if wi < 24:
            return wqkv_d[128 * (wi - 16):128 * (wi - 15), :]
        if wi < 32:
            return wp_d[128 * (wi - 24):128 * (wi - 23), :]
        return wf_d[128 * (wi - 32):128 * (wi - 31), :]

    def w_dst(wi):
        if wi < 16:
            return wkvT, 2048, 128 * wi
        if wi < 24:
            return wqT, 1024, 128 * (wi - 16)
        if wi < 32:
            return wpT, 1024, 128 * (wi - 24)
        return wdT, 1024, 128 * (wi - 32)

    def emit_quant(eng_sched, src_ap, dst_ap, n_blk, am, sc, iv, yb, cb, wb,
                   waits0, inc_final, dbg_steps=False):
        F = 32 * n_blk
        ws = list(waits0)
        S.add(eng_sched, lambda v: v.tensor_reduce(
            am, src_ap.rearrange("p (b k) -> p b k", k=32), AX.X, ALU.max,
            apply_absolute_value=True), waits=ws,
            inc=qdbg if dbg_steps else None)
        S.add(eng_sched, lambda v: v.drain())
        if dbg_steps:
            S.add("sync", lambda s: s.dma_start(out=dbg_o["amx_o"], in_=am),
                  waits=[(qdbg, qdbg.n)], inc=dbgc, amt=16)
        S.add(eng_sched, lambda v: v.tensor_scalar(sc, am, 1.0 / 7.5, 1e-30, ALU.mult, ALU.max),
              waits=[(dbgc, dbgc.n)] if dbg_steps else [],
              inc=qdbg if dbg_steps else None)
        if dbg_steps:
            S.add("sync", lambda s: s.dma_start(out=dbg_o["scl2_o"], in_=sc),
                  waits=[(qdbg, qdbg.n)], inc=dbgc, amt=16)
        S.add(eng_sched, lambda v: v.drain())
        S.add(eng_sched, lambda v: v.reciprocal(iv, sc),
              waits=[(dbgc, dbgc.n)] if dbg_steps else [],
              inc=qdbg if dbg_steps else None)
        S.add(eng_sched, lambda v: v.drain())
        if dbg_steps:
            S.add("sync", lambda s: s.dma_start(out=dbg_o["inv2_o"], in_=iv),
                  waits=[(qdbg, qdbg.n)], inc=dbgc, amt=16)
            S.add(eng_sched, lambda v: v.drain(), waits=[(dbgc, dbgc.n)])
        S.add(eng_sched, lambda v: v.tensor_tensor(
            yb, src_ap.rearrange("p (b k) -> p b k", k=32),
            iv.broadcast_to([128, n_blk, 32]), ALU.mult),
            inc=qdbg if dbg_steps else None)
        if dbg_steps:
            S.add("sync", lambda s: s.dma_start(out=dbg_o["yq0_o"], in_=yb),
                  waits=[(qdbg, qdbg.n)], inc=dbgc, amt=16)
        S.add(eng_sched, lambda v: v.drain())
        S.add(eng_sched, lambda v: v.tensor_scalar(
            cb.bitcast(I32), yb.bitcast(I32), 0x7F800000, None, ALU.bitwise_and))
        S.add(eng_sched, lambda v: v.drain())
        S.add(eng_sched, lambda v: v.tensor_scalar(cb, cb, 1.0, 24.0, ALU.max, ALU.mult),
              inc=qdbg if dbg_steps else None)
        if dbg_steps:
            S.add("sync", lambda s: s.dma_start(out=dbg_o["cq2_o"], in_=cb),
                  waits=[(qdbg, qdbg.n)], inc=dbgc, amt=16)
        S.add(eng_sched, lambda v: v.drain())
        S.add(eng_sched, lambda v: v.tensor_tensor(wb, yb, cb, ALU.add),
              waits=[(dbgc, dbgc.n)] if dbg_steps else [])
        S.add(eng_sched, lambda v: v.drain())
        S.add(eng_sched, lambda v: v.scalar_tensor_tensor(wb, cb, 1.0 / 384.0, wb, ALU.mult, ALU.add))
        S.add(eng_sched, lambda v: v.drain())
        S.add(eng_sched, lambda v: v.tensor_scalar(
            wb.bitcast(I32), wb.bitcast(I32), -65536, None, ALU.bitwise_and),
            inc=qdbg if dbg_steps else None)
        if dbg_steps:
            S.add("sync", lambda s: s.dma_start(out=dbg_o["wq2_o"], in_=wb),
                  waits=[(qdbg, qdbg.n)], inc=dbgc, amt=16)
        S.add(eng_sched, lambda v: v.drain())
        S.add(eng_sched, lambda v: v.tensor_tensor(yb, wb, cb, ALU.subtract),
              waits=[(dbgc, dbgc.n)] if dbg_steps else [],
              inc=qdbg if dbg_steps else None)
        S.add(eng_sched, lambda v: v.drain())
        if dbg_steps:
            S.add("sync", lambda s: s.dma_start(out=dbg_o["qq2_o"], in_=yb),
                  waits=[(qdbg, qdbg.n)], inc=dbgc, amt=16)
        S.add(eng_sched, lambda v: v.tensor_tensor(
            dst_ap.rearrange("p (b k) -> p b k", k=32),
            yb.rearrange("p (b k) -> p b k", k=32),
            sc.broadcast_to([128, n_blk, 32]), ALU.mult),
            waits=[(dbgc, dbgc.n)] if dbg_steps else [], inc=inc_final)
        if dbg_steps:
            S.add("sync", lambda s: s.dma_start(out=dbg_o["dq2_o"], in_=dst_ap),
                  waits=[(inc_final, inc_final.n)], inc=dbgc, amt=16)

    def w_src_pair(wi):
        if wi < 16:
            return wqkv_d[D + 128 * wi:D + 128 * (wi + 2), :]
        if wi < 24:
            return wqkv_d[128 * (wi - 16):128 * (wi - 14), :]
        return wp_d[128 * (wi - 24):128 * (wi - 22), :]

    def emit_wtr_evac(wi, stage_ap, wst_key, ready):
        sbank = stg_ps if wi % 2 == 0 else stg_ps2
        skey = "stg" if wi % 2 == 0 else "stg2"
        tws = ready + [(cst, N_CST)] + B[skey].wr_waits()

        def mk_tr(stage_ap=stage_ap, sbank=sbank):
            def f(t):
                r = None
                for c in range(8):
                    r = t.transpose(sbank[:, 128 * c:128 * (c + 1)],
                                    stage_ap[:, 128 * c:128 * (c + 1)], ident[:])
                return r
            return f
        S.add("tensor", mk_tr(), waits=tws, inc=wtr, drain_inc=True)
        B[wst_key].did_rd(wtr)
        B[skey].did_wr(wtr)
        dstt, ofw, coff = w_dst(wi)
        dst_ap = dstt[:].rearrange("p (c o) -> p c o", c=8)[:, :, coff:coff + 128]
        src_ap2 = sbank[:].rearrange("p (c i) -> p c i", c=8)

        def mk_ev(dst_ap=dst_ap, src_ap2=src_ap2):
            return lambda s: s.activation(dst_ap, src_ap2, AF.Copy, bias=0.0)
        S.add("scalar", mk_ev(), waits=[(wtr, wtr.n)], inc=wev)
        B[skey].did_rd(wev)

    def emit_wpair(wi):
        pr = (wi // 2) % 2
        wst_key = "wst%d" % pr
        pair_ap = wstage[:, pr * 2 * D:(pr + 1) * 2 * D]
        ws = B[wst_key].wr_waits()
        if wi < 4:
            ws = ws + [(cst, N_CST)]

        def mk_wdma(wi=wi, pair_ap=pair_ap):
            return lambda g: g.dma_start(
                out=pair_ap.rearrange("p (c i) -> p c i", c=2),
                in_=w_src_pair(wi).rearrange("(c p) i -> p c i", p=128))
        wch = wdma if pr == 0 else wdmb
        S.add("gpsimd", mk_wdma(), waits=ws, inc=wch, amt=16)
        B[wst_key].did_wr(wch)
        ready = [(wch, wch.n)]
        for sub in range(2):
            emit_wtr_evac(wi + sub, pair_ap[:, sub * D:(sub + 1) * D], wst_key, ready)

    def emit_ffn_chunk(wi):
        pr = (wi // 2) % 2
        wst_key = "wst%d" % pr
        stage_ap = wstage[:, pr * 2 * D + (wi % 2) * D:pr * 2 * D + (wi % 2 + 1) * D]
        ws = (B["wfst"].wr_waits() + B["sqs0"].wr_waits() + B["sqs1"].wr_waits())

        def mk_wfdma(wi=wi):
            return lambda s: s.dma_start(out=sqs[:], in_=w_src(wi))
        S.add("gpsimd", mk_wfdma(), waits=ws, inc=wfdma, amt=16)
        B["wfst"].did_wr(wfdma)
        B["sqs0"].did_wr(wfdma)
        B["sqs1"].did_wr(wfdma)
        qws = [(wfdma, wfdma.n), (cst, N_CST)] + B[wst_key].wr_waits()
        emit_quant("vector", sqs[:], stage_ap, 32, amx[:], scl[:, 0:32],
                   inv[:], y_q[:], c_q[:], w_q[:], qws, wq_c)
        B["wfst"].did_rd(wq_c)
        B["sqs0"].did_rd(wq_c)
        B["sqs1"].did_rd(wq_c)
        B[wst_key].did_wr(wq_c)
        emit_wtr_evac(wi, stage_ap, wst_key, [(wq_c, wq_c.n)])

    S.add("sync", lambda s: s.dma_start(
        out=x_st[:, 0:TPS * D].rearrange("p (t d) -> p t d", t=TPS),
        in_=x_d[0:ST, :].rearrange("(t p) d -> p t d", p=128)),
        waits=[], inc=x_cp[0], amt=16)
    B["xst0"].did_wr(x_cp[0])
    XC0 = (x_cp[0], x_cp[0].n)

    for k in range(16):
        emit_wpair(2 * k)

    BIAS_WDMA = 32
    WEV_KV = 16
    WEV_Q = 24
    WEV_P = 32
    WEV_ALL = 40

    if dbg:
        S.add("sync", lambda s: s.dma_start(out=dbg_o["wdt_o"], in_=wdT[:]),
              waits=[(wev, WEV_ALL)], inc=dbgc, amt=16)
    for st in range(NST):
        p = st % 2
        xk = "xst%d" % p
        lk = "lnT%d" % p
        x_sl = x_st[:, p * TPS * D:(p + 1) * TPS * D]
        lnT_sl = lnT[:, p * TPS * D:(p + 1) * TPS * D]
        if st == 0:
            XC = XC0
        else:
            xw = B[xk].wr_waits()

            def mk_xl(st=st, x_sl=x_sl):
                return lambda s: s.dma_start(
                    out=x_sl.rearrange("p (t d) -> p t d", t=TPS),
                    in_=x_d[st * ST:(st + 1) * ST, :].rearrange("(t p) d -> p t d", p=128))
            xch = x_cp[st % 2]
            S.add("sync", mk_xl(), waits=xw, inc=xch, amt=16)
            B[xk].did_wr(xch)
            XC = (xch, xch.n)
        def p1_stats(t):
            xt = x_sl[:, t * D:(t + 1) * D]

            def mk_bn(xt=xt):
                return lambda v: v.bn_stats(bn6[:, 0:6], xt[:, 0:512])
            S.add("vector", mk_bn(), waits=[XC])

            def mk_bnb(xt=xt):
                return lambda v: v.bn_stats(bn6[:, 6:12], xt[:, 512:1024])
            S.add("vector", mk_bnb())
            S.add("vector", lambda v: v.drain())

            def mk_ag(t=t):
                return lambda v: v.bn_aggr(mv_s[:, 2 * t:2 * t + 2], bn6[:])
            S.add("vector", mk_ag(), inc=bnv)

            def mk_ln(t=t):
                return lambda s: s.activation(sqv_s[:, t:t + 1],
                                              mv_s[:, 2 * t + 1:2 * t + 2],
                                              AF.Ln, bias=1e-5)
            S.add("scalar", mk_ln(), waits=[(bnv, bnv.n)])
            S.add("scalar", lambda s: s.drain())

            def mk_sq(t=t):
                return lambda s: s.activation(rstd_s[:, t:t + 1], sqv_s[:, t:t + 1],
                                              AF.Exp, scale=-0.5)
            S.add("scalar", mk_sq(), inc=sqc)
        def p1_tile(t):
            tp = t % 2
            lntmk = "lntm%d" % tp
            xt = x_sl[:, t * D:(t + 1) * D]
            ln_ap = ln_tm[:, tp * D:(tp + 1) * D]

            nws = [(sqc, 4 * st + t + 1)] + B[lntmk].wr_waits()

            def mk_nm(t=t, xt=xt, ln_ap=ln_ap):
                return lambda v: v.tensor_scalar(
                    ln_ap, xt, mv_s[:, 2 * t:2 * t + 1], rstd_s[:, t:t + 1],
                    ALU.subtract, ALU.mult)
            S.add("vector", mk_nm(), waits=nws, inc=ln_c)
            B[lntmk].did_wr(ln_c)
            if t == TPS - 1:
                B[xk].did_rd(ln_c)
            if dbg and st == 0 and t == 0:
                S.add("sync", (lambda x_sl=x_sl: lambda s: s.dma_start(
                    out=dbg_o["xsb0_o"], in_=x_sl))(),
                    waits=[XC], inc=dbgc, amt=16)
                B[xk].did_rd(dbgc)
                S.add("sync", (lambda ln_ap=ln_ap: lambda s: s.dma_start(
                    out=dbg_o["lntm0_o"], in_=ln_tm[:, 0:D]))(),
                    waits=[(ln_c, ln_c.n)], inc=dbgc, amt=16)
                B[lntmk].did_rd(dbgc)
                S.add("sync", lambda s: s.dma_start(out=dbg_o["st0_o"], in_=mv_s[:, 0:8]),
                      waits=[(ln_c, ln_c.n)], inc=dbgc, amt=16)
            sbank = stg_ps if t % 2 == 0 else stg_ps2
            skey = "stg" if t % 2 == 0 else "stg2"
            tws = [(ln_c, ln_c.n)] + B[skey].wr_waits()

            def mk_lt(ln_ap=ln_ap, sbank=sbank):
                def f(te):
                    r = None
                    for c in range(8):
                        r = te.transpose(sbank[:, 128 * c:128 * (c + 1)],
                                         ln_ap[:, 128 * c:128 * (c + 1)], ident[:])
                    return r
                return f
            S.add("tensor", mk_lt(), waits=tws, inc=lntr, drain_inc=True)
            B[lntmk].did_rd(lntr)
            B[skey].did_wr(lntr)
            ews = [(lntr, lntr.n)] + B[lk].wr_waits() if t == 0 else [(lntr, lntr.n)]
            dst = lnT_sl.rearrange("p (c i) -> p c i", c=8)[:, :, t * 128:(t + 1) * 128]
            src = sbank[:].rearrange("p (c i) -> p c i", c=8)

            def mk_le(dst=dst, src=src):
                return lambda s: s.activation(dst, src, AF.Copy, bias=0.0)
            S.add("scalar", mk_le(), waits=ews, inc=lnev)
            B[skey].did_rd(lnev)
            if t == 0:
                B[lk].did_wr(lnev)
        for t in range(TPS):
            p1_stats(t)
            if t >= 1:
                p1_tile(t - 1)
        p1_tile(TPS - 1)
        emit_ffn_chunk(32 + st)
        sws = [(lnev, lnev.n)]

        def mk_ls(st=st, lnT_sl=lnT_sl):
            return lambda s: s.dma_start(out=lnt_d[st], in_=lnT_sl)
        sch = lnstp[st % 2]
        S.add("sync", mk_ls(), waits=sws, inc=sch, amt=16)
        B[lk].did_rd(sch)
        if dbg and st == 0:
            S.add("sync", lambda s: s.dma_start(out=dbg_o["lnt0_o"], in_=lnt_d[0]),
                  waits=[(lnstp[0], lnstp[0].n)], inc=dbgc, amt=16)
        for t in range(TPS):
            tp = t % 2
            ktm_ap = ktm[:, tp * D:(tp + 1) * D]
            vtm_ap = vtm[:, tp * D:(tp + 1) * D]
            for ob in (0, 2, 1, 3):
                bank = mmA if ob < 2 else mmB
                bkk = "bkA" if ob < 2 else "bkB"
                gws = [(lnev, 4 * st + t + 1), (wev, WEV_KV)] + B[bkk].wr_waits()

                def mk_g(t=t, ob=ob, bank=bank, lnT_sl=lnT_sl):
                    def f(te):
                        r = None
                        for c in range(8):
                            r = te.matmul(
                                bank[:],
                                lnT_sl[:, c * 512 + t * 128:c * 512 + (t + 1) * 128],
                                wkvT[:, c * 2048 + ob * 512:c * 2048 + (ob + 1) * 512],
                                start=(c == 0), stop=(c == 7))
                        return r
                    return f
                S.add("tensor", mk_g(), waits=gws, inc=kvmm)
                B[bkk].did_wr(kvmm)
                if t == TPS - 1 and ob == 3:
                    B[lk].did_rd(kvmm)
                if ob < 2:
                    sqp = (2 * t + ob) % 2
                    sws2 = [(kvmm, kvmm.n)] + B["sqs%d" % sqp].wr_waits()

                    def mk_s1(bank=bank, sqp=sqp):
                        return lambda s: s.activation(
                            sqs[:, sqp * 512:(sqp + 1) * 512], bank[:], AF.Square)
                    S.add("scalar", mk_s1(), waits=sws2, inc=ksq)
                    S.add("scalar", lambda s: s.drain())
                    B[bkk].did_rd(ksq)
                    B["sqs%d" % sqp].did_wr(ksq)
                    ews2 = B["ktm%d" % tp].wr_waits() if ob == 0 else []

                    def mk_s2(ktm_ap=ktm_ap, ob=ob, sqp=sqp):
                        return lambda s: s.activation(
                            ktm_ap[:, ob * 512:(ob + 1) * 512],
                            sqs[:, sqp * 512:(sqp + 1) * 512], AF.Exp, scale=-0.5)
                    S.add("scalar", mk_s2(), waits=ews2, inc=kexp)
                    B["sqs%d" % sqp].did_rd(kexp)
                    if ob == 0:
                        B["ktm%d" % tp].did_wr(kexp)
                else:
                    vws = [(kvmm, kvmm.n)] + (B["vtm%d" % tp].wr_waits() if ob == 2 else [])

                    def mk_vc(vtm_ap=vtm_ap, ob=ob, bank=bank):
                        return lambda v: v.tensor_copy(
                            vtm_ap[:, (ob - 2) * 512:(ob - 1) * 512], bank[:])
                    S.add("vector", mk_vc(), waits=vws, inc=vcp)
                    B[bkk].did_rd(vcp)
                    if ob == 2:
                        B["vtm%d" % tp].did_wr(vcp)
            kws = [(kexp, kexp.n), (vcp, vcp.n), (wev, WEV_KV)]
            first = (st == 0 and t == 0)
            last = (st == NST - 1 and t == TPS - 1)

            def mk_kv(ktm_ap=ktm_ap, vtm_ap=vtm_ap, first=first, last=last):
                def f(te):
                    for h in range(NH):
                        te.matmul(
                            attA[64 * (h % 2):64 * (h % 2) + 64,
                                 64 * (h // 2):64 * (h // 2) + 64],
                            ktm_ap[:, 64 * h:64 * (h + 1)],
                            vtm_ap[:, 64 * h:64 * (h + 1)],
                            start=(first and h < 2), stop=(last and h >= NH - 2),
                            skip_group_check=True)
                    r = te.matmul(zq[0:1, :], onesc[:], ktm_ap[:, 0:512],
                                  start=first, stop=last, skip_group_check=True)
                    r = te.matmul(attB[0:1, :], onesc[:], ktm_ap[:, 512:1024],
                                  start=first, stop=last, skip_group_check=True)
                    return r
                return f
            S.add("tensor", mk_kv(), waits=kws, inc=kvein)
            B["ktm%d" % tp].did_rd(kvein)
            B["vtm%d" % tp].did_rd(kvein)

    S.add("gpsimd", lambda g: g.dma_start(out=biasp, in_=bp_d.unsqueeze(0)),
          waits=[(lntr, NST * TPS)], inc=bdma, amt=16)
    S.add("gpsimd", lambda g: g.dma_start(out=biasf, in_=bfn_d.unsqueeze(0)),
          waits=[(lntr, NST * TPS)], inc=bdma, amt=16)
    assert bdma.n == BIAS_WDMA

    bws = [(kvein, kvein.n), (wtr, 40)]
    S.add("vector", dve_memset(kvb[:, 0:1024], 0.0), waits=bws)
    S.add("vector", lambda v: v.drain())
    S.add("vector", lambda v: v.tensor_copy(
        kvb[0:64, 0:1024].rearrange("p (c s i) -> p c s i", s=2, i=64)[:, :, 0, :],
        attA[0:64, :].rearrange("p (c i) -> p c i", i=64)))
    S.add("vector", lambda v: v.drain())
    S.add("vector", lambda v: v.tensor_copy(
        kvb[64:128, 0:1024].rearrange("p (c s i) -> p c s i", s=2, i=64)[:, :, 1, :],
        attA[64:128, :].rearrange("p (c i) -> p c i", i=64)))
    S.add("vector", lambda v: v.drain())
    S.add("vector", lambda v: v.tensor_copy(attsb[0:1, 0:512], zq[0:1, :]))
    S.add("vector", lambda v: v.tensor_copy(attsb[0:1, 512:1024], attB[0:1, :]), inc=bnd)
    BND = bnd.n
    for c in range(8):
        for j in range(2):
            h = 2 * c + j

            def mk_ka(c=c, j=j, h=h):
                return lambda g: g.dma_start(
                    out=K_all[64 * j:64 * j + 64, 16 * c + h:16 * c + h + 1],
                    in_=attsb[0:1, h * 64:(h + 1) * 64])
            S.add("gpsimd", mk_ka(), waits=[(bnd, BND)], inc=kall, amt=16)
    KALL = kall.n
    if dbg:
        S.add("sync", lambda s: s.dma_start(out=dbg_o["kvx_o"], in_=kvb[:, 0:512]),
              waits=[(bnd, BND)], inc=dbgc, amt=16)
        S.add("sync", lambda s: s.dma_start(out=dbg_o["ksr_o"], in_=attsb[0:1, 0:1024]),
              waits=[(bnd, BND)], inc=dbgc, amt=16)
        S.add("sync", lambda s: s.dma_start(out=dbg_o["kall_o"], in_=K_all[:]),
              waits=[(kall, KALL)], inc=dbgc, amt=16)

    zb2 = pst("zb2", [128, 512], F32)

    def xd_ap(tb):
        return [xd[:, 0:D], ktm[:, 0:D], ktm[:, D:2 * D], vtm[:, 0:D]][tb]
    XDK = ["xd0", "ktm0", "ktm1", "vtm0"]

    p2ctx = {}

    def p2_front(st):
        p = st % 2
        xk = "xst%d" % p
        lk = "lnT%d" % p
        x_sl = x_st[:, p * TPS * D:(p + 1) * TPS * D]
        lnT_sl = lnT[:, p * TPS * D:(p + 1) * TPS * D]
        qf_sl = qfm[:, p * 8 * 512:(p + 1) * 8 * 512]
        lws = B[lk].wr_waits()

        def mk_ll(st=st, lnT_sl=lnT_sl):
            return lambda s: s.dma_start(out=lnT_sl, in_=lnt_d[st])
        lch = lnt2p[st % 2]
        S.add("sync", mk_ll(),
              waits=lws + [(lnstp[0], lnstp[0].n), (lnstp[1], lnstp[1].n)],
              inc=lch, amt=16)
        B[lk].did_wr(lch)
        LNT2 = (lch, lch.n)
        xw = B[xk].wr_waits()

        def mk_xl2(st=st, x_sl=x_sl):
            return lambda s: s.dma_start(
                out=x_sl.rearrange("p (t d) -> p t d", t=TPS),
                in_=x_d[st * ST:(st + 1) * ST, :].rearrange("(t p) d -> p t d", p=128))
        xch = x_cp[st % 2]
        S.add("sync", mk_xl2(), waits=xw, inc=xch, amt=16)
        B[xk].did_wr(xch)
        XC = (xch, xch.n)
        p2ctx[st] = {"p": p, "xk": xk, "x_sl": x_sl, "qf_sl": qf_sl, "XC": XC,
                     "qnq_at": {}, "sq2_at": {}}
        for oc in range(8):
            bank = mmA if oc % 2 == 0 else mmB
            bkk = "bkA" if oc % 2 == 0 else "bkB"
            gws = [LNT2, (wev, WEV_Q)] + B[bkk].wr_waits()
            qfw = B["qfm%d" % p].wr_waits() if oc == 0 else []

            def mk_q(oc=oc, bank=bank, lnT_sl=lnT_sl):
                def f(te):
                    r = None
                    for c in range(8):
                        r = te.matmul(
                            bank[:],
                            wqT[:, c * 1024 + oc * 128:c * 1024 + (oc + 1) * 128],
                            lnT_sl[:, c * 512:(c + 1) * 512],
                            start=(c == 0), stop=(c == 7))
                    return r
                return f
            S.add("tensor", mk_q(), waits=gws, inc=qmm)
            B[bkk].did_wr(qmm)
            if oc == 7:
                B[lk].did_rd(qmm)
            sqp = oc % 2
            sws2 = [(qmm, qmm.n)] + B["sqs%d" % sqp].wr_waits()

            def mk_qs(bank=bank, sqp=sqp):
                return lambda s: s.activation(
                    sqs[:, sqp * 512:(sqp + 1) * 512], bank[:], AF.Square)
            S.add("scalar", mk_qs(), waits=sws2, inc=qsq)
            S.add("scalar", lambda s: s.drain())
            B[bkk].did_rd(qsq)
            B["sqs%d" % sqp].did_wr(qsq)

            def mk_qe(oc=oc, sqp=sqp, qf_sl=qf_sl):
                return lambda s: s.activation(
                    qf_sl[:, oc * 512:(oc + 1) * 512],
                    sqs[:, sqp * 512:(sqp + 1) * 512], AF.Exp, scale=-0.5)
            S.add("scalar", mk_qe(), waits=qfw, inc=qexp)
            B["sqs%d" % sqp].did_rd(qexp)
            if oc == 0:
                B["qfm%d" % p].did_wr(qexp)
        qdw = [(qexp, qexp.n), (kall, KALL)] + B["bkzq"].wr_waits()

        def mk_qd(qf_sl=qf_sl):
            def f(te):
                r = None
                for c in range(8):
                    r = te.matmul(zq[0:16, :], K_all[:, 16 * c:16 * (c + 1)],
                                  qf_sl[:, c * 512:(c + 1) * 512],
                                  start=(c == 0), stop=(c == 7))
                return r
            return f
        S.add("tensor", mk_qd(), waits=qdw, inc=qdot)
        B["bkzq"].did_wr(qdot)
        B["qfm%d" % p].did_rd(qdot)
        zws = [(qdot, qdot.n)] + B["zsb"].wr_waits() + B["sqs0"].wr_waits()
        S.add("scalar", lambda s: s.activation(sqs[0:16, 0:512], zq[0:16, :],
                                               AF.Ln, bias=1e-6), waits=zws)
        S.add("scalar", lambda s: s.drain())
        S.add("scalar", lambda s: s.activation(zsb[:], sqs[0:16, 0:512],
                                               AF.Exp, scale=-1.0), inc=zact)
        B["bkzq"].did_rd(zact)
        B["sqs0"].did_rd(zact)
        B["zsb"].did_wr(zact)

    def p2_attproj(st):
        ctx = p2ctx[st]
        p = ctx["p"]; x_sl = ctx["x_sl"]; qf_sl = ctx["qf_sl"]; XC = ctx["XC"]
        def emit_att(c):
            bank = attA if c % 2 == 0 else attB
            bkk = "bkattA" if c % 2 == 0 else "bkattB"
            aws = [(qz_c, qz_c.n), (bnd, BND)] + B[bkk].wr_waits()

            def mk_at(c=c, bank=bank, qf_sl=qf_sl):
                return lambda te: te.matmul(
                    bank[:],
                    kvb[:, 128 * c:128 * (c + 1)],
                    qf_sl[:, c * 512:(c + 1) * 512],
                    start=True, stop=True)
            S.add("tensor", mk_at(), waits=aws, inc=attmm)
            B[bkk].did_wr(attmm)
            if c == 7:
                B["qfm%d" % p].did_rd(attmm)
                B["zsb"].did_rd(attmm)
            ew = [(attmm, attmm.n)] + (B["attsb"].wr_waits() if c == 0 else [])
            if st == 0 and c <= 1:
                ew = ew + [(kall, KALL)]

            def mk_ae(c=c, bank=bank):
                return lambda s: s.activation(
                    attsb[:, c * 512:(c + 1) * 512],
                    bank[:], AF.Copy, bias=0.0)
            S.add("scalar", mk_ae(), waits=ew, inc=attev)
            B[bkk].did_rd(attev)
            if c == 0:
                B["attsb"].did_wr(attev)

        for c in range(8):
            zbank = zb2 if c % 2 == 0 else zq
            zbkk = "bkzb2" if c % 2 == 0 else "bkzq"
            zw = [(zact, zact.n)] + B[zbkk].wr_waits()
            if st == 0 and c <= 1:
                zw = zw + [(seldma, seldma.n)]

            def mk_zb(c=c, zbank=zbank):
                return lambda te: te.matmul(zbank[:], sel_all[:, 128 * c:128 * (c + 1)],
                                            zsb[:], start=True, stop=True)
            S.add("tensor", mk_zb(), waits=zw, inc=zbcmm)
            B[zbkk].did_wr(zbcmm)
            qzw = [(zbcmm, zbcmm.n), (qexp, qexp.n)]

            def mk_qz(c=c, zbank=zbank, qf_sl=qf_sl):
                return lambda v: v.tensor_tensor(
                    qf_sl[:, c * 512:(c + 1) * 512], qf_sl[:, c * 512:(c + 1) * 512],
                    zbank[:], ALU.mult)
            S.add("vector", mk_qz(), waits=qzw, inc=qz_c)
            B[zbkk].did_rd(qz_c)
            if c >= 1:
                emit_att(c - 1)
        emit_att(7)
        for tb in range(TPS):
            for ocb in range(2):
                g = 2 * tb + ocb
                bank = mmA if g % 2 == 0 else mmB
                bkk = "bkA" if g % 2 == 0 else "bkB"
                gws = [(attev, attev.n), (wev, WEV_P), (bdma, BIAS_WDMA)] + B[bkk].wr_waits()

                def mk_pj(tb=tb, ocb=ocb, bank=bank):
                    def f(te):
                        for c in range(8):
                            te.matmul(
                                bank[:],
                                attsb[:, c * 512 + tb * 128:c * 512 + (tb + 1) * 128],
                                wpT[:, c * 1024 + ocb * 512:c * 1024 + (ocb + 1) * 512],
                                start=(c == 0), stop=False)
                        return te.matmul(bank[:], onesr[:],
                                         biasp[:, ocb * 512:(ocb + 1) * 512],
                                         start=False, stop=True)
                    return f
                S.add("tensor", mk_pj(), waits=gws, inc=pjmm)
                B[bkk].did_wr(pjmm)
                if tb == TPS - 1 and ocb == 1:
                    B["attsb"].did_rd(pjmm)
                rws = [(pjmm, pjmm.n), XC]

                def mk_r1(tb=tb, ocb=ocb, bank=bank, x_sl=x_sl):
                    sl = x_sl[:, tb * D + ocb * 512:tb * D + (ocb + 1) * 512]
                    return lambda v: v.tensor_tensor(sl, bank[:], sl, ALU.add)
                S.add("vector", mk_r1(), waits=rws, inc=rs1)
                B[bkk].did_rd(rs1)
            tp = tb % 2
            xt = x_sl[:, tb * D:(tb + 1) * D]

            def mk_bn2(xt=xt):
                return lambda v: v.bn_stats(bn6[:, 0:6], xt[:, 0:512])
            S.add("vector", mk_bn2())

            def mk_bn2b(xt=xt):
                return lambda v: v.bn_stats(bn6[:, 6:12], xt[:, 512:1024])
            S.add("vector", mk_bn2b())
            S.add("vector", lambda v: v.drain())

            tq_ = 4 * (st % 2) + tb

            def mk_ag2(tq_=tq_):
                return lambda v: v.bn_aggr(mv_s[:, 2 * tq_:2 * tq_ + 2], bn6[:])
            S.add("vector", mk_ag2(), inc=bnv2)

            def mk_ln2(tq_=tq_):
                return lambda s: s.activation(sqv_s[:, tq_:tq_ + 1],
                                              mv_s[:, 2 * tq_ + 1:2 * tq_ + 2],
                                              AF.Ln, bias=1e-5)
            S.add("scalar", mk_ln2(), waits=[(bnv2, bnv2.n)])
            S.add("scalar", lambda s: s.drain())

            def mk_sq2(tq_=tq_):
                return lambda s: s.activation(rstd_s[:, tq_:tq_ + 1], sqv_s[:, tq_:tq_ + 1],
                                              AF.Exp, scale=-0.5)
            S.add("scalar", mk_sq2(), inc=sq2)
            ctx["sq2_at"][tb] = sq2.n
            if tb >= 1:
                p2_quant_tile(st, tb - 1)

    def p2_quant_tile(st, tb):
        ctx = p2ctx[st]
        p = ctx["p"]; x_sl = ctx["x_sl"]; XC = ctx["XC"]
        if True:
            xt = x_sl[:, tb * D:(tb + 1) * D]
            xdk = XDK[tb]
            xda_t = xd_ap(tb)
            l2 = zbc[:, 0:D]
            yb = y_q[:, 0:D]
            cb = c_q[:, 0:D]
            wb = w_q[:, 0:D]
            am = amx[:, 0:32]
            sc = scl[:, 32 * (tb % 2):32 * (tb % 2) + 32]
            iv = inv[:, 0:32]
            fws = B["ln2h0"].wr_waits()

            tq_ = 4 * (st % 2) + tb

            def mk_n2(xt=xt, l2=l2, tq_=tq_):
                return lambda v: v.tensor_scalar(
                    l2, xt, mv_s[:, 2 * tq_:2 * tq_ + 1], rstd_s[:, tq_:tq_ + 1],
                    ALU.subtract, ALU.mult)
            S.add("vector", mk_n2(), waits=fws + [(sq2, ctx["sq2_at"][tb])])
            S.add("vector", lambda v: v.drain())

            def mk_am(l2=l2, am=am):
                return lambda v: v.tensor_reduce(
                    am, l2.rearrange("p (b k) -> p b k", k=32), AX.X, ALU.max,
                    apply_absolute_value=True)
            S.add("vector", mk_am())
            S.add("vector", lambda v: v.drain())

            def mk_sc(am=am, sc=sc):
                return lambda v: v.tensor_scalar(sc, am, 1.0 / 7.5, 1e-30,
                                                 ALU.mult, ALU.max)
            S.add("vector", mk_sc())
            S.add("vector", lambda v: v.drain())

            def mk_iv(sc=sc, iv=iv):
                return lambda v: v.reciprocal(iv, sc)
            S.add("vector", mk_iv())
            S.add("vector", lambda v: v.drain())
            yws = B["y0"].wr_waits() + B["y1"].wr_waits()

            def mk_y(l2=l2, iv=iv, yb=yb):
                return lambda v: v.tensor_tensor(
                    yb, l2.rearrange("p (b k) -> p b k", k=32),
                    iv.broadcast_to([128, 32, 32]), ALU.mult)
            S.add("vector", mk_y(), waits=yws)
            S.add("vector", lambda v: v.drain())
            cws = B["c0"].wr_waits() + B["c1"].wr_waits()

            def mk_c1(yb=yb, cb=cb):
                return lambda v: v.tensor_scalar(
                    cb.bitcast(I32), yb.bitcast(I32), 0x7F800000, None,
                    ALU.bitwise_and)
            S.add("vector", mk_c1(), waits=cws)
            S.add("vector", lambda v: v.drain())

            def mk_c2(cb=cb):
                return lambda v: v.tensor_scalar(cb, cb, 1.0, 24.0, ALU.max, ALU.mult)
            S.add("vector", mk_c2())
            S.add("vector", lambda v: v.drain())

            def mk_yofs(yb=yb, cb=cb):
                return lambda v: v.scalar_tensor_tensor(yb, cb, 1.0 / 384.0, yb,
                                                        ALU.mult, ALU.add)
            S.add("vector", mk_yofs(), inc=qna)
            B["ln2h0"].did_wr(qna)
            B["y0"].did_wr(qna)
            B["y1"].did_wr(qna)
            B["c0"].did_wr(qna)
            B["c1"].did_wr(qna)
            pws = [(qna, qna.n)] + B["w0"].wr_waits() + B["w1"].wr_waits()

            def mk_w1(yb=yb, cb=cb, wb=wb):
                return lambda g: g.tensor_tensor(wb, yb, cb, ALU.add)
            S.add("gpsimd", lambda g: g.drain())
            S.add("gpsimd", mk_w1(), waits=pws, inc=qnw)
            B["w0"].did_wr(qnw)
            B["w1"].did_wr(qnw)
            S.add("vector", (lambda wb=wb: (lambda v: v.tensor_scalar(
                wb.bitcast(I32), wb.bitcast(I32), -65536, None,
                ALU.bitwise_and)))(), waits=[(qnw, qnw.n)], inc=qnr)
            S.add("vector", lambda v: v.drain())
            dws = [(qnr, qnr.n)] + B[xdk].wr_waits()

            def mk_q1(yb=yb, cb=cb, wb=wb):
                return lambda g: g.tensor_tensor(yb, wb, cb, ALU.subtract)
            S.add("gpsimd", mk_q1(), waits=dws)
            S.add("gpsimd", lambda g: g.drain())

            def mk_dq(yb=yb, sc=sc, xda_t=xda_t):
                return lambda g: g.tensor_tensor(
                    xda_t.rearrange("p (b k) -> p b k", k=32),
                    yb.rearrange("p (b k) -> p b k", k=32),
                    sc.broadcast_to([128, 32, 32]), ALU.mult)
            S.add("gpsimd", mk_dq(), inc=qnq)
            B["y0"].did_rd(qnq)
            B["y1"].did_rd(qnq)
            B["c0"].did_rd(qnq)
            B["c1"].did_rd(qnq)
            B["w0"].did_rd(qnq)
            B["w1"].did_rd(qnq)
            B[xdk].did_wr(qnq)
            ctx["qnq_at"][tb] = qnq.n

    def p2_ffn(st):
        ctx = p2ctx[st]
        p = ctx["p"]; x_sl = ctx["x_sl"]; xk = ctx["xk"]
        for tb in range(TPS):
            xdk = XDK[tb]
            xda_t = xd_ap(tb)
            sbank = stg_ps if tb % 2 == 0 else stg_ps2
            skey = "stg" if tb % 2 == 0 else "stg2"
            xtw = [(qnq, ctx["qnq_at"][tb])] + B[skey].wr_waits()

            def mk_xt(xda_t=xda_t, sbank=sbank):
                def f(te):
                    r = None
                    for c in range(8):
                        r = te.transpose(sbank[:, 128 * c:128 * (c + 1)],
                                         xda_t[:, 128 * c:128 * (c + 1)], ident[:])
                    return r
                return f
            S.add("tensor", mk_xt(), waits=xtw, inc=xdtr, drain_inc=True)
            B[xdk].did_rd(xdtr)
            B[skey].did_wr(xdtr)
            xk2 = "xdT0"
            xdT_sl = xdT[:, 0:TPS * D]
            ews3 = [(xdtr, xdtr.n)] + (B[xk2].wr_waits() if tb == 0 else [])
            dst3 = xdT_sl.rearrange("p (c i) -> p c i", c=8)[:, :, tb * 128:(tb + 1) * 128]
            src3 = sbank[:].rearrange("p (c i) -> p c i", c=8)

            def mk_xe(dst3=dst3, src3=src3):
                return lambda s: s.activation(dst3, src3, AF.Copy, bias=0.0)
            S.add("scalar", mk_xe(), waits=ews3, inc=xdev)
            B[skey].did_rd(xdev)
            if tb == 0:
                B[xk2].did_wr(xdev)
            for ocb in range(2):
                g = 2 * tb + ocb
                bank = mmA if g % 2 == 0 else mmB
                bkk = "bkA" if g % 2 == 0 else "bkB"
                gws = [(xdev, xdev.n), (wev, WEV_ALL)] + B[bkk].wr_waits()

                def mk_ff(tb=tb, ocb=ocb, bank=bank):
                    def f(te):
                        for c in range(8):
                            te.matmul(
                                bank[:],
                                xdT[:, c * 512 + tb * 128:c * 512 + (tb + 1) * 128],
                                wdT[:, c * 1024 + ocb * 512:c * 1024 + (ocb + 1) * 512],
                                start=(c == 0), stop=False)
                        return te.matmul(bank[:], onesr[:],
                                         biasf[:, ocb * 512:(ocb + 1) * 512],
                                         start=False, stop=True)
                    return f
                S.add("tensor", mk_ff(), waits=gws, inc=ffmm)
                B[bkk].did_wr(ffmm)
                if tb == TPS - 1 and ocb == 1:
                    B["xdT0"].did_rd(ffmm)

                def mk_r2(tb=tb, ocb=ocb, bank=bank, x_sl=x_sl):
                    sl = x_sl[:, tb * D + ocb * 512:tb * D + (ocb + 1) * 512]
                    return lambda v: v.tensor_tensor(sl, bank[:], sl, ALU.add)
                S.add("vector", mk_r2(), waits=[(ffmm, ffmm.n)], inc=rs2)
                B[bkk].did_rd(rs2)
        ows = [(rs2, rs2.n)]

        def mk_os(st=st, x_sl=x_sl):
            return lambda g: g.dma_start(
                out=out_d[st * ST:(st + 1) * ST, :].rearrange("(t p) d -> p t d", p=128),
                in_=x_sl.rearrange("p (t d) -> p t d", t=TPS))
        och = outp[st % 2]
        S.add("gpsimd", mk_os(), waits=ows, inc=och, amt=16)
        B[xk].did_rd(och)

    if 2 in phases:
        for st in range(NST):
            p2_front(st)
            if st >= 1:
                p2_ffn(st - 1)
            p2_attproj(st)
            p2_quant_tile(st, TPS - 1)
        p2_ffn(NST - 1)

    S.add("gpsimd", lambda g: g.engine_nop(),
          waits=[(outp[0], outp[0].n), (outp[1], outp[1].n)] +
          ([(dbgc, dbgc.n)] if dbg else []))

    sems = ExitStack()
    for c in S.chans:
        c.sem = sems.enter_context(nc.semaphore(c.name))
    with nc.Block() as block:
        def emit(eng_name):
            drain_engs = ("vector", "scalar", "gpsimd")

            def body(e):
                for waits, f, incc, amt, dinc in S.ops[eng_name]:
                    for c, v in waits:
                        e.wait_ge(c.sem, v)
                    r = f(e)
                    if incc is not None:
                        if (eng_name in drain_engs and amt == 1) or dinc:
                            e.drain().then_inc(incc.sem, amt)
                        else:
                            r.then_inc(incc.sem, amt)
            return body
        block.sync(emit("sync"))
        block.tensor(emit("tensor"))
        block.vector(emit("vector"))
        block.scalar(emit("scalar"))
        block.gpsimd(emit("gpsimd"))
    sems.close()
    ctx.close()
    return nc


_NC_CACHE = {}


def kernel(L0_attn_x, L0_attn_W_qkv, L0_attn_W_proj, L0_attn_b_proj,
           L0_ffn_weight, L0_ffn_bias):
    x = np.ascontiguousarray(np.asarray(L0_attn_x, dtype=np.float32))
    wqkv = np.ascontiguousarray(np.asarray(L0_attn_W_qkv, dtype=np.float32))
    wp = np.ascontiguousarray(np.asarray(L0_attn_W_proj, dtype=np.float32))
    bp = np.ascontiguousarray(np.asarray(L0_attn_b_proj, dtype=np.float32))
    wf = np.ascontiguousarray(np.asarray(L0_ffn_weight, dtype=np.float32))
    bf = np.ascontiguousarray(np.asarray(L0_ffn_bias, dtype=np.float32))

    if "nc" not in _NC_CACHE:
        _NC_CACHE["nc"] = build_nc(dbg=False)
    nc = _NC_CACHE["nc"]

    in_maps = []
    for c in range(8):
        in_maps.append({
            "x": np.ascontiguousarray(x[c]),
            "wqkv": wqkv, "wproj": wp, "bproj": bp,
            "wffn": wf, "bffn": bf,
        })
    res = run_bass_kernel_spmd(nc, in_maps, core_ids=list(range(8)))
    out = np.stack([np.asarray(r["out"]) for r in res.results], axis=0)
    return out.astype(np.float32)
